# version 01
# baseline (speedup 1.0000x reference)
import os
import numpy as np
import concourse.bass as bass
import concourse.mybir as mybir
from concourse.bass_utils import run_bass_kernel_spmd
from concourse.bass import IndirectOffsetOnAxis

F32 = mybir.dt.float32
BF16 = mybir.dt.bfloat16
I32 = mybir.dt.int32
U8 = mybir.dt.uint8
OP = mybir.AluOpType
ACT = mybir.ActivationFunctionType
AX = mybir.AxisListType.X

B, T, D, H, HD = 16, 2048, 1024, 16, 64
TOPK = 614
E, FF = 8, 4096
CCAP = 640
NCORES = 8
BPC = 2
TPC = BPC * T
NT = T // 128
NTC = TPC // 128
DC = D // 128
CAP0, CAP1 = 384, 128
R0STR, R1STR = CAP0 + 1, CAP1 + 1
K1BASE = E * R0STR
REGIN_ROWS = E * R0STR + E * R1STR
ZROW = REGIN_ROWS
N_NEWTON = 3
N_FP = 6
N_PROBE = 4
G = 2
Z0 = 0.52533
INV_DENS = 1.0 / (2048 * 0.34765)
TOPKF = float(TOPK)


class Sem:
    def __init__(self, handle):
        self.h = handle
        self.n = 0

    def inc(self, by=1):
        return (self, by)


def _ap_tensor_names(v):
    names = set()
    if v is None or isinstance(v, (int, float, str, bool)):
        return names
    t = getattr(v, "tensor", None)
    if t is not None and hasattr(t, "name"):
        names.add(t.name)
    return names


class EngRec:
    def __init__(self, rec, name):
        self.rec = rec
        self.name = name
        self.last_outs = set()
        self.sep = None

    def wait(self, sem, val=None):
        v = sem.n if val is None else val
        self.rec.ops[self.name].append(("wait", sem.h, v))

    def op(self, method, *args, inc=None, **kwargs):
        if self.name == "vector" and self.last_outs:
            ins = set()
            for v in args[1:]:
                ins |= _ap_tensor_names(v)
            for k, v in kwargs.items():
                if k not in ("out", "accum_out"):
                    ins |= _ap_tensor_names(v)
            if method != "memset" and (ins & self.last_outs) and self.sep is not None:
                self.rec.ops[self.name].append(
                    ("op", "tensor_copy", (), {"out": self.sep[0], "in_": self.sep[1]}, []))
        spec = []
        if inc is not None:
            for s, by in (inc if isinstance(inc, list) else [inc]):
                s.n += by
                spec.append((s.h, by))
        self.rec.ops[self.name].append(("op", method, args, kwargs, spec))
        outs = set()
        if args:
            outs |= _ap_tensor_names(args[0])
        outs |= _ap_tensor_names(kwargs.get("out"))
        outs |= _ap_tensor_names(kwargs.get("accum_out"))
        self.last_outs = outs


class Rec:
    def __init__(self):
        self.ops = {k: [] for k in ("sync", "tensor", "vector", "scalar", "gpsimd")}
        for k in self.ops:
            setattr(self, k, EngRec(self, k))

    def replay(self, name, eng):
        for item in self.ops[name]:
            if item[0] == "wait":
                _, h, v = item
                eng.wait_ge(h, v)
            else:
                _, method, args, kwargs, incs = item
                inst = getattr(eng, method)(*args, **kwargs)
                for h, by in incs:
                    inst = inst.then_inc(h, by)


def build_nc():
    nc = bass.Bass("TRN2", target_bir_lowering=False)

    x_in = nc.dram_tensor("x_in", [TPC, D], F32, kind="ExternalInput").ap()
    wqkvT = nc.dram_tensor("wqkvT", [D, 3 * D], F32, kind="ExternalInput").ap()
    bqkv = nc.dram_tensor("bqkv", [3 * D, 1], F32, kind="ExternalInput").ap()
    woutT = nc.dram_tensor("woutT", [D, D], F32, kind="ExternalInput").ap()
    bout_bc = nc.dram_tensor("bout_bc", [128, D], F32, kind="ExternalInput").ap()
    wrouter = nc.dram_tensor("wrouter", [D, E], F32, kind="ExternalInput").ap()
    wi_in = nc.dram_tensor("wi_in", [E * D, FF], F32, kind="ExternalInput").ap()
    wo_in = nc.dram_tensor("wo_in", [E * FF, D], F32, kind="ExternalInput").ap()
    ident_in = nc.dram_tensor("ident_in", [128, 128], F32, kind="ExternalInput").ap()
    iota8bc_in = nc.dram_tensor("iota8bc_in", [128, E], F32, kind="ExternalInput").ap()
    iota8col_in = nc.dram_tensor("iota8col_in", [E, 1], F32, kind="ExternalInput").ap()
    ones8_in = nc.dram_tensor("ones8_in", [E, 1], F32, kind="ExternalInput").ap()
    rowmask_in = nc.dram_tensor("rowmask_in", [E, 1], F32, kind="ExternalInput").ap()
    maskprev_in = nc.dram_tensor("maskprev_in", [E, 1], F32, kind="ExternalInput").ap()
    maskgrp_in = nc.dram_tensor("maskgrp_in", [E, 1], F32, kind="ExternalInput").ap()

    y_out = nc.dram_tensor("y_out", [TPC, D], F32, kind="ExternalOutput").ap()
    _dbg_env = os.environ.get("MK_DEBUG", "")
    DBG = _dbg_env in ("1", "2")
    DBG2 = _dbg_env == "2"
    if DBG:
        dbg_xmid = nc.dram_tensor("dbg_xmid", [TPC, D], F32, kind="ExternalOutput").ap()
        dbg_xln2 = nc.dram_tensor("dbg_xln2", [TPC, D], F32, kind="ExternalOutput").ap()
        dbg_cols = nc.dram_tensor("dbg_cols", [128, 10 * NTC], F32, kind="ExternalOutput").ap()
        dbg_all = nc.dram_tensor("dbg_all", [8, 32], F32, kind="ExternalOutput").ap()
        dbg_reg = nc.dram_tensor("dbg_reg", [REGIN_ROWS, D], F32, kind="ExternalOutput").ap()
        dbg_rout = nc.dram_tensor("dbg_rout", [REGIN_ROWS + 1, D], F32, kind="ExternalOutput").ap()
        dbg_ht = nc.dram_tensor("dbg_ht", [128, 16384], F32, kind="ExternalOutput").ap()
        dbg_sc = nc.dram_tensor("dbg_sc", [128, 2 * 2048], F32, kind="ExternalOutput").ap()
        dbg_w = nc.dram_tensor("dbg_w", [128, 2 * 2048], F32, kind="ExternalOutput").ap()
        dbg_st = nc.dram_tensor("dbg_st", [128, 8 * G], F32, kind="ExternalOutput").ap()
        dbg_kq = nc.dram_tensor("dbg_kq", [64, 3 * 2048], F32, kind="ExternalOutput").ap()
        dbg_xfm = nc.dram_tensor("dbg_xfm", [128, 16384], F32, kind="ExternalOutput").ap()
        dbg_ln = nc.dram_tensor("dbg_ln", [128, 1032], F32, kind="ExternalOutput").ap()

    xmid_h = nc.dram_tensor("xmid_h", [TPC, D], F32, kind="Internal").ap()
    xln2_h = nc.dram_tensor("xln2_h", [TPC, D], F32, kind="Internal").ap()
    regin_h = nc.dram_tensor("regin_h", [REGIN_ROWS, D], F32, kind="Internal").ap()
    regout_h = nc.dram_tensor("regout_h", [REGIN_ROWS + 1, D], F32, kind="Internal").ap()
    ccin_h = nc.dram_tensor("ccin_h", [E, 32], F32, kind="Internal")
    ccout_h = nc.dram_tensor("ccout_h", [E, 32], F32, kind="Internal")

    from contextlib import ExitStack
    ctx = ExitStack()
    sb = lambda name, shape, dt=F32: ctx.enter_context(nc.sbuf_tensor(name, shape, dt))

    arenaA = sb("arenaA", [128, 16384])
    arenaB = sb("arenaB", [128, 16384])
    d0g = sb("d0g", [128, G * 2048])
    junk = sb("junk", [128, 2048])
    msk = sb("msk", [128, 2048], BF16)
    u8big = sb("u8big", [128, 2048], U8)
    kfm_t = sb("kfm_t", [64, 2048])
    qfm_t = sb("qfm_t", [64, 2048])
    vfm = sb("vfm", [64, 2048])
    vtm = sb("vtm", [128, 16 * 65])
    wqkvh = sb("wqkvh", [128, DC * 192])
    bbc = sb("bbc", [128, 1024])
    ident = sb("ident", [128, 128])
    bq_h = sb("bq_h", [64, 4])
    tLO = sb("tLO", [128, G]); tHI = sb("tHI", [128, G])
    tCLO = sb("tCLO", [128, G]); tCHI = sb("tCHI", [128, G])
    tT = sb("tT", [128, G]); tC = sb("tC", [128, G])
    tTHIT = sb("tTHIT", [128, G]); tFND = sb("tFND", [128, G])
    tS1 = sb("tS1", [128, G]); tS2 = sb("tS2", [128, G]); tS3 = sb("tS3", [128, G])
    tSIG = sb("tSIG", [128, G]); tMU = sb("tMU", [128, G])
    tNEG = sb("tNEG", [128, G]); tP = sb("tP", [128, G])
    u8g = sb("u8g", [128, G], U8)
    lnA = sb("lnA", [128, 1]); lnB = sb("lnB", [128, 1]); lnC = sb("lnC", [128, 1])
    neghuge = sb("neghuge", [128, 1])
    zrowc = sb("zrowc", [128, 1])
    recz = sb("recz", [128, 1])
    c104n = sb("c104n", [128, 1])
    sepA = sb("sepA", [128, 2]); sepB = sb("sepB", [128, 2])
    logits = sb("logits", [128, NTC * E])
    e1c = sb("e1c", [128, NTC]); e2c = sb("e2c", [128, NTC])
    g1c = sb("g1c", [128, NTC]); g2c = sb("g2c", [128, NTC])
    gs0 = sb("gs0", [128, NTC]); gs1 = sb("gs1", [128, NTC])
    r0c = sb("r0c", [128, NTC]); r1c = sb("r1c", [128, NTC])
    of0c = sb("of0c", [128, NTC]); of1c = sb("of1c", [128, NTC])
    sidx0 = sb("sidx0", [128, NTC], I32); sidx1 = sb("sidx1", [128, NTC], I32)
    itmp0 = sb("itmp0", [128, 1], I32); itmp1 = sb("itmp1", [128, 1], I32)
    aflag = sb("aflag", [128, NTC])
    gidx0 = sb("gidx0", [128, NTC], I32); gidx1 = sb("gidx1", [128, NTC], I32)
    rt1 = sb("rt1", [128, 8]); rt2 = sb("rt2", [128, 8])
    rm1 = sb("rm1", [128, 1]); rm2 = sb("rm2", [128, 1]); rz8 = sb("rz8", [128, 1])
    re2 = sb("re2", [128, 1])
    u8r = sb("u8r", [128, NTC], U8)
    cnt16 = sb("cnt16", [1, 32])
    allin = sb("allin", [8, 32]); allout = sb("allout", [8, 32])
    offp = sb("offp", [1, 32]); offg = sb("offg", [1, 32])
    cA0 = sb("cA0", [8, 1]); cB0 = sb("cB0", [8, 1])
    cA1 = sb("cA1", [8, 1]); cB1 = sb("cB1", [8, 1])
    offA0 = sb("offA0", [8, 1]); offB0 = sb("offB0", [8, 1])
    offA1 = sb("offA1", [8, 1]); offB1 = sb("offB1", [8, 1])
    tot0c = sb("tot0c", [8, 1]); tmp8 = sb("tmp8", [8, 1]); tmp8b = sb("tmp8b", [8, 1])
    off0col = sb("off0col", [8, 1]); off1col = sb("off1col", [8, 1])
    iota8bc = sb("iota8bc", [128, E])
    iota8col = sb("iota8col", [E, 1])
    iotaP = sb("iotaP", [128, 1])
    ones8 = sb("ones8", [E, 1])
    rowmask = sb("rowmask", [E, 1])
    maskprev = sb("maskprev", [E, 1])
    maskgrp = sb("maskgrp", [E, 1])

    psA = ctx.enter_context(nc.psum_tensor("psA", [128, 2048], F32))
    psB = ctx.enter_context(nc.psum_tensor("psB", [128, 2048], F32))

    s_ld = Sem(ctx.enter_context(nc.semaphore("s_ld")))
    s_gd = Sem(ctx.enter_context(nc.semaphore("s_gd")))
    s_te = Sem(ctx.enter_context(nc.semaphore("s_te")))
    s_ve = Sem(ctx.enter_context(nc.semaphore("s_ve")))
    s_se = Sem(ctx.enter_context(nc.semaphore("s_se")))
    s_cc = Sem(ctx.enter_context(nc.semaphore("s_cc")))

    r = Rec()
    SY, TE, VE, SE, GP = r.sync, r.tensor, r.vector, r.scalar, r.gpsimd
    VE.sep = (sepA[:, 0:1], sepB[:, 0:1])


    def xlnfm(dc_, lo, n):
        return arenaA[:, dc_ * 2048 + lo: dc_ * 2048 + lo + n]

    def attnout(tt, lo, n):
        return arenaB[:, tt * 1024 + lo: tt * 1024 + lo + n]

    ind0 = arenaA[0:8, 0:4096]; cum0 = arenaA[0:8, 4096:8192]
    ind1 = arenaA[0:8, 8192:12288]; cum1 = arenaA[0:8, 12288:16384]
    e1b = arenaA[96:104, 0:4096]; e2b = arenaA[96:104, 4096:8192]
    rowsA = arenaA[32:33, 0:4096]; rowsB = arenaA[32:33, 4096:8192]
    rowsC = arenaA[32:33, 8192:12288]; rowsD = arenaA[32:33, 12288:16384]
    zfill = arenaA[64:96, 0:1024]

    def hT0(fc, lo, n):
        return arenaA[:, fc * 384 + lo: fc * 384 + lo + n]

    HT1B = 32 * 384

    def hT1(fc, lo, n):
        return arenaA[:, HT1B + fc * 128 + lo: HT1B + fc * 128 + lo + n]

    def wfbuf(pp):
        return arenaB[:, pp * 1024: (pp + 1) * 1024]

    def wobuf(pp):
        return arenaB[:, 2048 + pp * 1024: 2048 + (pp + 1) * 1024]

    def gtile(i):
        return arenaB[:, 4096 + i * 1024: 4096 + (i + 1) * 1024]

    def outsb(i):
        return arenaB[:, 8192 + i * 1024: 8192 + (i + 1) * 1024]

    xfm2 = arenaB[:, 12288:13312]
    xtile_w = junk[:, 1024:2048]

    def gfm0(dc_, lo, n):
        return d0g[:, dc_ * 384 + lo: dc_ * 384 + lo + n]

    GFM1B = 8 * 384

    def gfm1(dc_, lo, n):
        return d0g[:, GFM1B + dc_ * 128 + lo: GFM1B + dc_ * 128 + lo + n]

    def xtile_ln(par):
        return d0g[:, par * 1024:(par + 1) * 1024]

    def d0(gi):
        return d0g[:, gi * 2048:(gi + 1) * 2048]

    SY.op("dma_start", out=ident[:], in_=ident_in, inc=[s_ld.inc(16)])
    SY.op("dma_start", out=bbc[:], in_=bout_bc, inc=[s_ld.inc(16)])
    SY.op("dma_start", out=iota8bc[:], in_=iota8bc_in, inc=[s_ld.inc(16)])
    SY.op("dma_start", out=iota8col[:], in_=iota8col_in, inc=[s_ld.inc(16)])
    SY.op("dma_start", out=iotaP[96:104, :], in_=iota8col_in, inc=[s_ld.inc(16)])
    SY.op("dma_start", out=ones8[:], in_=ones8_in, inc=[s_ld.inc(16)])
    SY.op("dma_start", out=rowmask[:], in_=rowmask_in, inc=[s_ld.inc(16)])
    SY.op("dma_start", out=maskprev[:], in_=maskprev_in, inc=[s_ld.inc(16)])
    SY.op("dma_start", out=maskgrp[:], in_=maskgrp_in, inc=[s_ld.inc(16)])
    VE.wait(s_ld)
    VE.op("memset", neghuge[:], -1e30)
    VE.op("memset", zrowc[:], float(ZROW))
    VE.op("memset", c104n[:], -104.0)
    VE.op("memset", sepB[:], 0.0)
    VE.op("memset", sepA[:], 0.0)
    VE.op("memset", vtm[:].rearrange("p (c w) -> p c w", w=65)[:, :, 64:65], 1.0,
          inc=[s_ve.inc()])

    def ln_tile(xt_view, out_view):
        VE.op("reduce_sum", lnA[:], xt_view, axis=AX)
        VE.op("tensor_scalar", out=lnA[:], in0=lnA[:], scalar1=1.0 / D, scalar2=None,
              op0=OP.mult)
        VE.op("tensor_scalar", out=lnB[:], in0=lnA[:], scalar1=-1.0, scalar2=None,
              op0=OP.mult, inc=[s_ve.inc()])
        SE.wait(s_ve)
        SE.op("activation", out=junk[:, 0:1024], in_=xt_view, func=ACT.Square,
              bias=lnB[:], scale=1.0, accum_out=lnC[:], inc=[s_se.inc()])
        VE.wait(s_se)
        VE.op("tensor_scalar", out=lnC[:], in0=lnC[:], scalar1=1.0 / D, scalar2=1e-5,
              op0=OP.mult, op1=OP.add, inc=[s_ve.inc()])
        SE.wait(s_ve)
        SE.op("activation", out=lnB[:], in_=lnC[:], func=ACT.Ln, bias=0.0, scale=1.0)
        SE.op("activation", out=lnC[:], in_=lnB[:], func=ACT.Exp, bias=0.0, scale=-0.5,
              inc=[s_se.inc()])
        VE.wait(s_se)
        VE.op("tensor_scalar", out=out_view, in0=xt_view, scalar1=lnA[:],
              scalar2=lnC[:], op0=OP.subtract, op1=OP.mult, inc=[s_ve.inc()])

    for b in range(BPC):
        tok0 = b * T
        for tt in range(NT):
            xt = xtile_ln(tt % 2)
            SY.wait(s_ve)
            SY.op("dma_start", out=xt, in_=x_in[tok0 + tt * 128: tok0 + (tt + 1) * 128, :],
                  inc=[s_ld.inc(16)])
            VE.wait(s_ld)
            ln_tile(xt, junk[:, 0:1024])
            TE.wait(s_ve)
            TE.wait(s_se)
            for dc_ in range(DC):
                TE.op("transpose", psB[:, dc_ * 128:(dc_ + 1) * 128],
                      junk[:, dc_ * 128:(dc_ + 1) * 128], ident[:],
                      inc=([s_te.inc()] if dc_ == DC - 1 else None))
            VE.wait(s_te)
            VE.op("tensor_copy",
                  out=arenaA[:].rearrange("p (c w) -> p c w", w=2048)[:, :, tt * 128:(tt + 1) * 128],
                  in_=psB[:, 0:1024].rearrange("p (c w) -> p c w", w=128),
                  inc=[s_ve.inc()])
            if DBG2 and b == 0 and tt == 0:
                GP.wait(s_ve)
                GP.op("dma_start", out=dbg_ln[:, 0:1024], in_=junk[:, 0:1024], inc=[s_gd.inc(16)])
                GP.op("dma_start", out=dbg_ln[:, 1024:1025], in_=lnA[:], allow_slow_non_contiguous=True, inc=[s_gd.inc(16)])
                GP.op("dma_start", out=dbg_ln[:, 1025:1026], in_=lnC[:], allow_slow_non_contiguous=True, inc=[s_gd.inc(16)])
                VE.wait(s_gd)
            TE.wait(s_ve)

        for h in range(H):
            SY.wait(s_se)
            for j, base in enumerate((0, D, 2 * D)):
                SY.op("dma_start",
                      out=wqkvh[:].rearrange("p (c w) -> p c w", w=192)[:, :, j * 64:(j + 1) * 64],
                      in_=wqkvT[:, base + h * 64: base + (h + 1) * 64]
                      .rearrange("(c p) f -> p c f", p=128),
                      inc=[s_ld.inc(16)])
                SY.op("dma_start", out=bq_h[:, j:j + 1],
                      in_=bqkv[base + h * 64: base + (h + 1) * 64, :],
                      inc=[s_ld.inc(16)])
            TE.wait(s_ld)
            SE.wait(s_ld)
            SE.op("activation", out=bq_h[0:64, 3:4], in_=bq_h[0:64, 0:1],
                  func=ACT.Identity, bias=0.0, scale=1.0 / 32, inc=[s_se.inc()])
            for j, dest, bias_col, scl in ((0, qfm_t[0:64, :], 3, 1.0 / 32), (1, kfm_t[0:64, :], 1, 1.0),
                                           (2, vfm[0:64, :], 2, 1.0)):
                TE.wait(s_se)
                TE.wait(s_ve)
                for dc_ in range(DC):
                    for cc_ in range(4):
                        TE.op("matmul", psB[0:64, cc_ * 512:(cc_ + 1) * 512],
                              wqkvh[:, dc_ * 192 + j * 64: dc_ * 192 + (j + 1) * 64],
                              xlnfm(dc_, cc_ * 512, 512),
                              start=(dc_ == 0), stop=(dc_ == DC - 1),
                              inc=([s_te.inc()] if (dc_ == DC - 1 and cc_ == 3) else None))
                SE.wait(s_te)
                SE.op("activation", out=dest, in_=psB[0:64, :], func=ACT.Identity,
                      bias=bq_h[0:64, bias_col:bias_col + 1], scale=scl,
                      inc=[s_se.inc()])
            TE.wait(s_se)
            for jc in range(16):
                TE.op("transpose", psB[:, jc * 64:(jc + 1) * 64],
                      vfm[0:64, jc * 128:(jc + 1) * 128], ident[0:64, 0:64],
                      inc=([s_te.inc()] if jc == 15 else None))
            VE.wait(s_te)
            VE.op("tensor_copy",
                  out=vtm[:].rearrange("p (c w) -> p c w", w=65)[:, :, 0:64],
                  in_=psB[:, 0:1024].rearrange("p (c w) -> p c w", w=64),
                  inc=[s_ve.inc()])
            TE.wait(s_ve)

            for g in range(NT // G):
                for gi in range(G):
                    tt = g * G + gi
                    TE.wait(s_se)
                    TE.wait(s_ve)
                    for cc_ in range(4):
                        TE.op("matmul", psA[:, cc_ * 512:(cc_ + 1) * 512],
                              qfm_t[0:64, tt * 128:(tt + 1) * 128],
                              kfm_t[0:64, cc_ * 512:(cc_ + 1) * 512],
                              start=True, stop=True,
                              inc=([s_te.inc()] if cc_ == 3 else None))
                    SE.wait(s_te)
                    SE.wait(s_ve)
                    SE.op("activation", out=d0(gi), in_=psA[:, :], func=ACT.Identity,
                          bias=0.0, scale=1.0, inc=[s_se.inc()])
                    SE.op("activation", out=junk[:], in_=d0(gi), func=ACT.Square,
                          bias=0.0, scale=1.0, accum_out=tS3[:, gi:gi + 1],
                          inc=[s_se.inc()])
                    VE.wait(s_se)
                    VE.op("reduce_sum", tS1[:, gi:gi + 1], d0(gi), axis=AX,
                          inc=[s_ve.inc()])
                if DBG2 and b == 0 and h == 0 and g == 0:
                    GP.wait(s_ve)
                    GP.op("dma_start", out=dbg_sc, in_=d0g[:, :], inc=[s_gd.inc(16)])
                    GP.op("dma_start", out=dbg_kq[0:64, 0:2048], in_=kfm_t[:, :], inc=[s_gd.inc(16)])
                    GP.op("dma_start", out=dbg_kq[0:64, 2048:4096], in_=qfm_t[:, :], inc=[s_gd.inc(16)])
                    GP.op("dma_start", out=dbg_kq[0:64, 4096:6144], in_=vfm[:, :], inc=[s_gd.inc(16)])
                    GP.op("dma_start", out=dbg_xfm, in_=arenaA[:, :], inc=[s_gd.inc(16)])
                    VE.wait(s_gd)
                VE.op("tensor_scalar", out=tMU[:], in0=tS1[:], scalar1=1.0 / 2048,
                      scalar2=None, op0=OP.mult)
                VE.op("tensor_scalar", out=tS2[:], in0=tS3[:], scalar1=1.0 / 2048,
                      scalar2=None, op0=OP.mult)
                VE.op("tensor_tensor", out=tS1[:], in0=tMU[:], in1=tMU[:], op=OP.mult)
                VE.op("tensor_tensor", out=tS2[:], in0=tS2[:], in1=tS1[:], op=OP.subtract)
                VE.op("tensor_scalar", out=tS2[:], in0=tS2[:], scalar1=1e-12, scalar2=None,
                      op0=OP.max, inc=[s_ve.inc()])
                SE.wait(s_ve)
                SE.op("activation", out=tS1[:], in_=tS2[:], func=ACT.Ln, bias=0.0, scale=1.0)
                SE.op("activation", out=tSIG[:], in_=tS1[:], func=ACT.Exp, bias=0.0,
                      scale=0.5, inc=[s_se.inc()])
                VE.wait(s_se)
                VE.op("scalar_tensor_tensor", out=tLO[:], in0=tSIG[:], scalar=-2.0,
                      in1=tMU[:], op0=OP.mult, op1=OP.add)
                VE.op("scalar_tensor_tensor", out=tHI[:], in0=tSIG[:], scalar=3.0,
                      in1=tMU[:], op0=OP.mult, op1=OP.add)
                VE.op("scalar_tensor_tensor", out=tT[:], in0=tSIG[:], scalar=Z0,
                      in1=tMU[:], op0=OP.mult, op1=OP.add)
                VE.op("memset", tCLO[:], 2048.0)
                VE.op("memset", tCHI[:], 0.0)
                VE.op("memset", tTHIT[:], 0.0)
                VE.op("memset", tFND[:], 0.0)

                def upd_eval():
                    VE.op("tensor_scalar", out=u8g[:], in0=tC[:], scalar1=TOPKF,
                          scalar2=None, op0=OP.is_ge)
                    VE.op("select", out=tLO[:], mask=u8g[:], on_true=tT[:], on_false=tLO[:])
                    VE.op("select", out=tCLO[:], mask=u8g[:], on_true=tC[:], on_false=tCLO[:])
                    VE.op("select", out=tHI[:], mask=u8g[:], on_true=tHI[:], on_false=tT[:])
                    VE.op("select", out=tCHI[:], mask=u8g[:], on_true=tCHI[:], on_false=tC[:])
                    VE.op("tensor_scalar", out=u8g[:], in0=tC[:], scalar1=TOPKF,
                          scalar2=None, op0=OP.is_equal)
                    VE.op("select", out=tTHIT[:], mask=u8g[:], on_true=tT[:], on_false=tTHIT[:])
                    VE.op("tensor_scalar", out=tS1[:], in0=tC[:], scalar1=TOPKF,
                          scalar2=None, op0=OP.is_equal)
                    VE.op("tensor_tensor", out=tFND[:], in0=tFND[:], in1=tS1[:], op=OP.max)

                def clip_t():
                    VE.op("tensor_tensor", out=tS1[:], in0=tHI[:], in1=tLO[:], op=OP.subtract)
                    VE.op("scalar_tensor_tensor", out=tS2[:], in0=tS1[:], scalar=0.002,
                          in1=tLO[:], op0=OP.mult, op1=OP.add)
                    VE.op("scalar_tensor_tensor", out=tS3[:], in0=tS1[:], scalar=-0.002,
                          in1=tHI[:], op0=OP.mult, op1=OP.add)
                    VE.op("tensor_tensor", out=tT[:], in0=tT[:], in1=tS2[:], op=OP.max)
                    VE.op("tensor_tensor", out=tT[:], in0=tT[:], in1=tS3[:], op=OP.min)
                    VE.op("tensor_scalar", out=u8g[:], in0=tFND[:], scalar1=0.5,
                          scalar2=None, op0=OP.is_ge)
                    VE.op("select", out=tT[:], mask=u8g[:], on_true=tLO[:], on_false=tT[:])

                def count_eval(use_scalar):
                    if use_scalar:
                        VE.op("tensor_scalar", out=tNEG[:], in0=tT[:], scalar1=-1.0,
                              scalar2=None, op0=OP.mult, inc=[s_ve.inc()])
                        SE.wait(s_ve)
                        for gi in range(G):
                            SE.op("activation", out=msk[:], in_=d0(gi), func=ACT.Sign,
                                  bias=tNEG[:, gi:gi + 1], scale=1.0,
                                  accum_out=tC[:, gi:gi + 1],
                                  inc=([s_se.inc()] if gi == G - 1 else None))
                        VE.wait(s_se)
                        VE.op("tensor_scalar", out=tC[:], in0=tC[:], scalar1=2048.0,
                              scalar2=0.5, op0=OP.add, op1=OP.mult)
                    else:
                        for gi in range(G):
                            VE.op("tensor_scalar", out=junk[:], in0=d0(gi),
                                  scalar1=tT[:, gi:gi + 1], scalar2=None,
                                  op0=OP.is_ge, op1=OP.add, accum_out=tC[:, gi:gi + 1])

                n_ev = 0
                count_eval(False); n_ev += 1
                upd_eval()
                for _ in range(N_NEWTON - 1):
                    VE.op("tensor_scalar", out=tS1[:], in0=tC[:], scalar1=TOPKF,
                          scalar2=None, op0=OP.subtract)
                    VE.op("tensor_tensor", out=tS1[:], in0=tS1[:], in1=tSIG[:], op=OP.mult)
                    VE.op("scalar_tensor_tensor", out=tT[:], in0=tS1[:], scalar=INV_DENS,
                          in1=tT[:], op0=OP.mult, op1=OP.add)
                    clip_t()
                    count_eval(n_ev % 2 == 1); n_ev += 1
                    upd_eval()
                for _ in range(N_FP):
                    VE.op("tensor_scalar", out=tS1[:], in0=tCLO[:], scalar1=TOPKF,
                          scalar2=None, op0=OP.subtract)
                    VE.op("tensor_tensor", out=tS2[:], in0=tCLO[:], in1=tCHI[:],
                          op=OP.subtract)
                    VE.op("tensor_scalar", out=tS2[:], in0=tS2[:], scalar1=1.0,
                          scalar2=None, op0=OP.max)
                    VE.op("reciprocal", tS2[:], tS2[:])
                    VE.op("tensor_tensor", out=tS1[:], in0=tS1[:], in1=tS2[:], op=OP.mult)
                    VE.op("tensor_tensor", out=tS2[:], in0=tHI[:], in1=tLO[:], op=OP.subtract)
                    VE.op("tensor_tensor", out=tS1[:], in0=tS1[:], in1=tS2[:], op=OP.mult)
                    VE.op("tensor_tensor", out=tT[:], in0=tLO[:], in1=tS1[:], op=OP.add)
                    clip_t()
                    count_eval(n_ev % 2 == 1); n_ev += 1
                    upd_eval()
                for _ in range(N_PROBE):
                    for gi in range(G):
                        VE.op("tensor_scalar", out=u8big[:], in0=d0(gi),
                              scalar1=tHI[:, gi:gi + 1], scalar2=None, op0=OP.is_lt)
                        VE.op("select", out=junk[:], mask=u8big[:], on_true=d0(gi),
                              on_false=neghuge[:].to_broadcast([128, 2048]))
                        VE.op("reduce_max", tP[:, gi:gi + 1], junk[:], axis=AX)
                    VE.op("tensor_copy", out=tT[:], in_=tP[:])
                    clip_t()
                    count_eval(False); n_ev += 1
                    upd_eval()
                VE.op("tensor_scalar", out=u8g[:], in0=tFND[:], scalar1=0.5,
                      scalar2=None, op0=OP.is_ge)
                VE.op("select", out=tT[:], mask=u8g[:], on_true=tTHIT[:], on_false=tLO[:])
                VE.op("tensor_scalar", out=tS2[:], in0=tT[:], scalar1=104.0,
                      scalar2=None, op0=OP.subtract, inc=[s_ve.inc()])
                if DBG2 and b == 0 and h == 0 and g == 0:
                    GP.wait(s_ve)
                    for kk, tile_ in enumerate((tT, tC, tCLO, tCHI, tFND, tLO, tHI, tSIG)):
                        GP.op("dma_start", out=dbg_st[:, kk * G:(kk + 1) * G], in_=tile_[:],
                              inc=[s_gd.inc(16)])
                    VE.wait(s_gd)

                for gi in range(G):
                    tt = g * G + gi
                    VE.op("tensor_scalar", out=msk[:], in0=d0(gi), scalar1=tT[:, gi:gi + 1],
                          scalar2=None, op0=OP.is_ge)
                    VE.op("scalar_tensor_tensor", out=junk[:], in0=d0(gi),
                          scalar=tS2[:, gi:gi + 1], in1=msk[:],
                          op0=OP.subtract, op1=OP.mult, inc=[s_ve.inc()])
                    TE.wait(s_ve)
                    TE.wait(s_se)
                    for jc in range(16):
                        TE.op("transpose", psB[:, jc * 128:(jc + 1) * 128],
                              junk[:, jc * 128:(jc + 1) * 128], ident[:],
                              inc=([s_te.inc()] if jc == 15 else None))
                    SE.wait(s_te)
                    SE.op("activation", out=d0(gi), in_=psB[:, :], func=ACT.Exp,
                          bias=c104n[:], scale=1.0, inc=[s_se.inc()])
                    TE.wait(s_se)
                    TE.wait(s_ve)
                    for jc in range(16):
                        TE.op("matmul", psA[:, 0:65], d0(gi)[:, jc * 128:(jc + 1) * 128],
                              vtm[:, jc * 65:(jc + 1) * 65],
                              start=(jc == 0), stop=(jc == 15),
                              inc=([s_te.inc()] if jc == 15 else None))
                    if DBG2 and b == 0 and h == 0 and g == 0:
                        GP.wait(s_se)
                        GP.op("dma_start", out=dbg_w[:, gi * 2048:(gi + 1) * 2048],
                              in_=d0(gi), inc=[s_gd.inc(16)])
                        TE.wait(s_gd)
                    VE.wait(s_te)
                    VE.op("reciprocal", recz[:], psA[:, 64:65])
                    VE.op("tensor_scalar", out=attnout(tt, h * 64, 64), in0=psA[:, 0:64],
                          scalar1=recz[:], scalar2=None, op0=OP.mult, inc=[s_ve.inc()])

        SY.wait(s_ve)
        SY.wait(s_se)
        SY.op("dma_start", out=arenaA[:, 0:8192].rearrange("p (c w) -> p c w", w=1024),
              in_=woutT.rearrange("(c p) o -> p c o", p=128), inc=[s_ld.inc(16)])
        for tt in range(NT):
            SY.wait(s_ve)
            SY.op("dma_start", out=xtile_w,
                  in_=x_in[tok0 + tt * 128: tok0 + (tt + 1) * 128, :], inc=[s_ld.inc(16)])
            TE.wait(s_ve)
            for dc_ in range(DC):
                TE.op("transpose", psB[:, dc_ * 128:(dc_ + 1) * 128],
                      attnout(tt, dc_ * 128, 128), ident[:],
                      inc=([s_te.inc()] if dc_ == DC - 1 else None))
            VE.wait(s_te)
            VE.op("tensor_copy", out=junk[:, 0:1024], in_=psB[:, 0:1024], inc=[s_ve.inc()])
            TE.wait(s_ve)
            TE.wait(s_ld)
            for dc_ in range(DC):
                for cc_ in range(2):
                    TE.op("matmul", psA[:, 1024 + cc_ * 512: 1024 + (cc_ + 1) * 512],
                          junk[:, dc_ * 128:(dc_ + 1) * 128],
                          arenaA[:, dc_ * 1024 + cc_ * 512: dc_ * 1024 + (cc_ + 1) * 512],
                          start=(dc_ == 0), stop=(dc_ == DC - 1),
                          inc=([s_te.inc()] if (dc_ == DC - 1 and cc_ == 1) else None))
            VE.wait(s_te)
            VE.wait(s_ld)
            VE.op("tensor_tensor", out=xtile_w, in0=xtile_w, in1=bbc[:], op=OP.add)
            VE.op("scalar_tensor_tensor", out=xtile_w, in0=psA[:, 1024:2048], scalar=0.0,
                  in1=xtile_w, op0=OP.add, op1=OP.add, inc=[s_ve.inc()])
            GP.wait(s_ve)
            GP.op("dma_start", out=xmid_h[tok0 + tt * 128: tok0 + (tt + 1) * 128, :],
                  in_=xtile_w, inc=[s_gd.inc(16)])
            TE.wait(s_ve)
            SY.wait(s_gd)

    VE.wait(s_gd)
    VE.op("memset", zfill[:, :], 0.0, inc=[s_ve.inc()])
    GP.wait(s_ve)
    nfill = (REGIN_ROWS + 31) // 32
    for i in range(nfill):
        lo = i * 32
        n = min(32, REGIN_ROWS - lo)
        GP.op("dma_start", out=regin_h[lo:lo + n, :], in_=zfill[0:n, :], inc=[s_gd.inc(16)])
    GP.op("dma_start", out=regout_h[ZROW:ZROW + 1, :], in_=zfill[0:1, :], inc=[s_gd.inc(16)])

    SY.op("dma_start", out=wqkvh[:].rearrange("p (c w) -> p c w", w=192)[:, :, 184:192],
          in_=wrouter.rearrange("(c p) e -> p c e", p=128), inc=[s_ld.inc(16)])
    for gt in range(NTC):
        xt = xtile_ln(gt % 2)
        SY.wait(s_ve)
        SY.wait(s_gd)
        SY.op("dma_start", out=xt, in_=xmid_h[gt * 128:(gt + 1) * 128, :], inc=[s_ld.inc(16)])
        VE.wait(s_ld)
        ln_tile(xt, junk[:, 0:1024])
        GP.wait(s_ve)
        GP.op("dma_start", out=xln2_h[gt * 128:(gt + 1) * 128, :], in_=junk[:, 0:1024],
              inc=[s_gd.inc(16)])
        TE.wait(s_ve)
        TE.wait(s_se)
        for dc_ in range(DC):
            TE.op("transpose", psB[:, dc_ * 128:(dc_ + 1) * 128],
                  junk[:, dc_ * 128:(dc_ + 1) * 128], ident[:],
                  inc=([s_te.inc()] if dc_ == DC - 1 else None))
        VE.wait(s_te)
        VE.wait(s_gd)
        VE.op("tensor_copy", out=xfm2[:], in_=psB[:, 0:1024], inc=[s_ve.inc()])
        TE.wait(s_ve)
        TE.wait(s_ld)
        for dc_ in range(DC):
            TE.op("matmul", psA[:, 0:8], xfm2[:, dc_ * 128:(dc_ + 1) * 128],
                  wqkvh[:, dc_ * 192 + 184: dc_ * 192 + 192],
                  start=(dc_ == 0), stop=(dc_ == DC - 1),
                  inc=([s_te.inc()] if dc_ == DC - 1 else None))
        VE.wait(s_te)
        VE.op("tensor_copy", out=logits[:, gt * 8:(gt + 1) * 8], in_=psA[:, 0:8],
              inc=[s_ve.inc()])
        TE.wait(s_ve)

    for gt in range(NTC):
        lg = logits[:, gt * 8:(gt + 1) * 8]
        VE.op("reduce_max", rm1[:], lg, axis=AX)
        VE.op("tensor_scalar", out=rt1[:], in0=lg, scalar1=rm1[:], scalar2=None,
              op0=OP.is_equal)
        VE.op("scalar_tensor_tensor", out=rt2[:], in0=rt1[:], scalar=-1e30, in1=lg,
              op0=OP.mult, op1=OP.add)
        VE.op("reduce_max", rm2[:], rt2[:], axis=AX)
        VE.op("tensor_tensor", out=rt1[:], in0=rt1[:], in1=iota8bc[:], op=OP.mult)
        VE.op("reduce_sum", e1c[:, gt:gt + 1], rt1[:], axis=AX)
        VE.op("tensor_scalar", out=rt1[:], in0=rt2[:], scalar1=rm2[:], scalar2=None,
              op0=OP.is_equal)
        VE.op("tensor_tensor", out=rt1[:], in0=rt1[:], in1=iota8bc[:], op=OP.mult)
        VE.op("reduce_sum", e2c[:, gt:gt + 1], rt1[:], axis=AX)
        VE.op("tensor_scalar", out=rm1[:], in0=rm1[:], scalar1=-1.0, scalar2=None,
              op0=OP.mult, inc=[s_ve.inc()])
        SE.wait(s_ve)
        SE.op("activation", out=rt2[:], in_=lg, func=ACT.Exp, bias=rm1[:], scale=1.0,
              accum_out=rz8[:])
        SE.op("activation", out=re2[:], in_=rm2[:], func=ACT.Exp, bias=rm1[:], scale=1.0,
              inc=[s_se.inc()])
        VE.wait(s_se)
        VE.op("reciprocal", g1c[:, gt:gt + 1], rz8[:])
        VE.op("tensor_tensor", out=g2c[:, gt:gt + 1], in0=re2[:], in1=g1c[:, gt:gt + 1],
              op=OP.mult, inc=[s_ve.inc()])

    for (col, row) in ((e1c, rowsA), (e2c, rowsB)):
        for half in range(2):
            TE.wait(s_ve)
            for i in range(16):
                gt = half * 16 + i
                TE.op("transpose", psB[0:1, i * 128:(i + 1) * 128], col[:, gt:gt + 1],
                      ident[:], inc=([s_te.inc()] if i == 15 else None))
            VE.wait(s_te)
            VE.op("tensor_copy", out=row[:, half * 2048:(half + 1) * 2048],
                  in_=psB[0:1, 0:2048], inc=[s_ve.inc()])
            TE.wait(s_ve)
    GP.wait(s_ve)
    for p in range(8):
        GP.op("dma_start", out=e1b[p:p + 1, :], in_=rowsA[:, :], inc=[s_gd.inc(16)])
        GP.op("dma_start", out=e2b[p:p + 1, :], in_=rowsB[:, :], inc=[s_gd.inc(16)])
    VE.wait(s_gd)
    VE.op("tensor_scalar", out=ind0[:, :], in0=e1b[:, :], scalar1=iotaP[96:104, :],
          scalar2=None, op0=OP.is_equal)
    VE.op("tensor_scalar", out=ind1[:, :], in0=e2b[:, :], scalar1=iotaP[96:104, :],
          scalar2=None, op0=OP.is_equal)
    VE.op("tensor_tensor_scan", out=cum0[:, :], data0=ind0[:, :], data1=ind0[:, :],
          initial=0.0, op0=OP.add, op1=OP.max)
    VE.op("tensor_tensor_scan", out=cum1[:, :], data0=ind1[:, :], data1=ind1[:, :],
          initial=0.0, op0=OP.add, op1=OP.max)
    VE.op("tensor_copy", out=cA0[:], in_=cum0[:, T - 1: T])
    VE.op("tensor_copy", out=cA1[:], in_=cum1[:, T - 1: T])
    VE.op("tensor_copy", out=cB0[:], in_=cum0[:, TPC - 1: TPC])
    VE.op("tensor_copy", out=cB1[:], in_=cum1[:, TPC - 1: TPC])
    VE.op("tensor_copy", out=sepA[:, 0:1], in_=sepB[:, 0:1])
    VE.op("tensor_tensor", out=cB0[:], in0=cB0[:], in1=cA0[:], op=OP.subtract)
    VE.op("tensor_tensor", out=cB1[:], in0=cB1[:], in1=cA1[:], op=OP.subtract,
          inc=[s_ve.inc()])
    TE.wait(s_ve)
    TE.op("transpose", psB[0:1, 0:8], cA0[:], ident[0:8, 0:8])
    TE.op("transpose", psB[0:1, 8:16], cB0[:], ident[0:8, 0:8])
    TE.op("transpose", psB[0:1, 16:24], cA1[:], ident[0:8, 0:8])
    TE.op("transpose", psB[0:1, 24:32], cB1[:], ident[0:8, 0:8],
          inc=[s_te.inc()])
    VE.wait(s_te)
    VE.op("tensor_copy", out=cnt16[:, :], in_=psB[0:1, 0:32], inc=[s_ve.inc()])
    TE.wait(s_ve)
    GP.wait(s_ve)
    for p in range(8):
        GP.op("dma_start", out=allin[p:p + 1, :], in_=cnt16[:, :], inc=[s_gd.inc(16)])
    VE.wait(s_gd)
    VE.op("tensor_scalar", out=allin[:], in0=allin[:], scalar1=rowmask[:], scalar2=None,
          op0=OP.mult, inc=[s_ve.inc()])
    GP.wait(s_ve)
    GP.op("dma_start", out=ccin_h.ap(), in_=allin[:], inc=[s_gd.inc(16)])
    GP.wait(s_gd)
    GP.op("collective_compute", "AllReduce", OP.add, replica_groups=[list(range(8))],
          ins=[ccin_h.ap().opt()], outs=[ccout_h.ap().opt()], inc=[s_cc.inc()])
    GP.wait(s_cc)
    GP.op("dma_start", out=allout[:], in_=ccout_h.ap(), inc=[s_gd.inc(16)])

    VE.op("tensor_tensor", out=cum0[:, :], in0=cum0[:, :], in1=ind0[:, :], op=OP.mult)
    VE.op("tensor_tensor", out=cum1[:, :], in0=cum1[:, :], in1=ind1[:, :], op=OP.mult,
          inc=[s_ve.inc()])
    TE.wait(s_ve)

    def row_reduce_and_transpose(src8, rowbuf, colbuf):
        for half in range(2):
            for cc_ in range(4):
                TE.op("matmul", psB[0:1, cc_ * 512:(cc_ + 1) * 512], ones8[:],
                      src8[:, half * 2048 + cc_ * 512: half * 2048 + (cc_ + 1) * 512],
                      start=True, stop=True,
                      inc=([s_te.inc()] if cc_ == 3 else None))
            VE.wait(s_te)
            VE.op("tensor_copy", out=rowbuf[:, half * 2048:(half + 1) * 2048],
                  in_=psB[0:1, 0:2048], inc=[s_ve.inc()])
            TE.wait(s_ve)
        for half in range(2):
            for i in range(16):
                TE.op("transpose", psB[:, i:i + 1],
                      rowbuf[:, (half * 16 + i) * 128:(half * 16 + i + 1) * 128],
                      ident[32:33, 32:33], inc=([s_te.inc()] if i == 15 else None))
            VE.wait(s_te)
            VE.op("tensor_copy", out=colbuf[:, half * 16:(half + 1) * 16], in_=psB[:, 0:16],
                  inc=[s_ve.inc()])
            TE.wait(s_ve)

    row_reduce_and_transpose(cum0, rowsC, r0c)
    row_reduce_and_transpose(cum1, rowsD, r1c)

    VE.wait(s_gd)
    VE.op("engine_nop", inc=[s_ve.inc()])
    TE.wait(s_ve)
    TE.op("matmul", psB[0:1, 0:32], maskprev[:], allout[:], start=True, stop=True)
    TE.op("matmul", psB[0:1, 32:64], maskgrp[:], allout[:], start=True, stop=True,
          inc=[s_te.inc()])
    VE.wait(s_te)
    VE.op("tensor_copy", out=offp[:, :], in_=psB[0:1, 0:32])
    VE.op("tensor_copy", out=offg[:, :], in_=psB[0:1, 32:64], inc=[s_ve.inc()])
    TE.wait(s_ve)
    pass
    for i4, rowv in enumerate((offp, offg)):
        for j4 in range(4):
            TE.op("transpose", psB[0:8, i4 * 4 + j4: i4 * 4 + j4 + 1],
                  rowv[0:1, j4 * 8:(j4 + 1) * 8], ident[0:1, 0:1],
                  inc=([s_te.inc()] if (i4 == 1 and j4 == 3) else None))
    VE.wait(s_te)
    VE.op("tensor_copy", out=offA0[:], in_=psB[0:8, 0:1])
    VE.op("tensor_copy", out=offB0[:], in_=psB[0:8, 1:2])
    VE.op("tensor_copy", out=offA1[:], in_=psB[0:8, 2:3])
    VE.op("tensor_copy", out=offB1[:], in_=psB[0:8, 3:4])
    VE.op("tensor_copy", out=tot0c[:], in_=psB[0:8, 4:5])
    VE.op("tensor_copy", out=tmp8[:], in_=psB[0:8, 6:7])
    VE.op("tensor_tensor", out=offB0[:], in0=offB0[:], in1=tot0c[:], op=OP.add)
    VE.op("tensor_tensor", out=offB0[:], in0=offB0[:], in1=cA0[:], op=OP.subtract)
    VE.op("tensor_copy", out=tmp8b[:], in_=psB[0:8, 5:6])
    VE.op("tensor_tensor", out=tot0c[:], in0=tot0c[:], in1=tmp8b[:], op=OP.add)
    VE.op("tensor_tensor", out=offA1[:], in0=offA1[:], in1=tot0c[:], op=OP.add)
    VE.op("tensor_tensor", out=offB1[:], in0=offB1[:], in1=tot0c[:], op=OP.add)
    VE.op("tensor_tensor", out=offB1[:], in0=offB1[:], in1=tmp8[:], op=OP.add)
    VE.op("tensor_tensor", out=offB1[:], in0=offB1[:], in1=cA1[:], op=OP.subtract,
          inc=[s_ve.inc()])
    for (ind, ocA, ocB, rowbuf, colbuf) in ((ind0, offA0, offB0, rowsA, of0c),
                                            (ind1, offA1, offB1, rowsB, of1c)):
        VE.op("tensor_scalar", out=ind[:, 0:T], in0=ind[:, 0:T], scalar1=ocA[:],
              scalar2=None, op0=OP.mult)
        VE.op("tensor_scalar", out=ind[:, T:TPC], in0=ind[:, T:TPC], scalar1=ocB[:],
              scalar2=None, op0=OP.mult, inc=[s_ve.inc()])
        TE.wait(s_ve)
        row_reduce_and_transpose(ind, rowbuf, colbuf)

    for (rcol, ocol, ecol, gcol, gscol, scol, gicol, cap, rstr, base) in (
            (r0c, of0c, e1c, g1c, gs0, sidx0, gidx0, CAP0, R0STR, 0),
            (r1c, of1c, e2c, g2c, gs1, sidx1, gidx1, CAP1, R1STR, K1BASE)):
        VE.op("tensor_tensor", out=ocol[:], in0=ocol[:], in1=rcol[:], op=OP.add)
        VE.op("tensor_scalar", out=ocol[:], in0=ocol[:], scalar1=1.0, scalar2=None,
              op0=OP.subtract)
        VE.op("tensor_scalar", out=gscol[:], in0=ocol[:], scalar1=float(CCAP),
              scalar2=None, op0=OP.is_lt)
        VE.op("tensor_scalar", out=ocol[:], in0=rcol[:], scalar1=float(cap) + 0.5,
              scalar2=None, op0=OP.is_lt)
        VE.op("tensor_tensor", out=gscol[:], in0=gscol[:], in1=ocol[:], op=OP.mult)
        VE.op("tensor_tensor", out=gscol[:], in0=gscol[:], in1=gcol[:], op=OP.mult)
        VE.op("tensor_scalar", out=ocol[:], in0=rcol[:], scalar1=1.0, scalar2=float(cap),
              op0=OP.subtract, op1=OP.min)
        VE.op("scalar_tensor_tensor", out=ocol[:], in0=ecol[:], scalar=float(rstr),
              in1=ocol[:], op0=OP.mult, op1=OP.add)
        VE.op("tensor_scalar", out=ocol[:], in0=ocol[:], scalar1=float(base),
              scalar2=None, op0=OP.add)
        VE.op("tensor_copy", out=scol[:], in_=ocol[:])
        VE.op("tensor_scalar", out=aflag[:], in0=gscol[:], scalar1=0.0, scalar2=None,
              op0=OP.is_gt)
        VE.op("tensor_tensor", out=ocol[:], in0=ocol[:], in1=aflag[:], op=OP.mult)
        VE.op("tensor_scalar", out=aflag[:], in0=aflag[:], scalar1=-float(ZROW),
              scalar2=float(ZROW), op0=OP.mult, op1=OP.add)
        VE.op("tensor_tensor", out=ocol[:], in0=ocol[:], in1=aflag[:], op=OP.add)
        VE.op("tensor_copy", out=gicol[:], in_=ocol[:], inc=[s_ve.inc()])

    GP.wait(s_ve)
    GP.wait(s_gd)
    for gt in range(NTC):
        VE.wait(s_gd)
        VE.op("tensor_copy", out=itmp0[:], in_=sidx0[:, gt:gt + 1])
        VE.op("tensor_copy", out=itmp1[:], in_=sidx1[:, gt:gt + 1], inc=[s_ve.inc()])
        GP.wait(s_ve)
        GP.op("dma_start", out=gtile(gt % 2)[:, :], in_=xln2_h[gt * 128:(gt + 1) * 128, :],
              inc=[s_gd.inc(16)])
        GP.wait(s_gd)
        GP.op("indirect_dma_start", out=regin_h,
              out_offset=IndirectOffsetOnAxis(itmp0[:], 0),
              in_=gtile(gt % 2)[:, :], in_offset=None, inc=[s_gd.inc(16)])
        GP.op("indirect_dma_start", out=regin_h,
              out_offset=IndirectOffsetOnAxis(itmp1[:], 0),
              in_=gtile(gt % 2)[:, :], in_offset=None, inc=[s_gd.inc(16)])
        GP.wait(s_gd)

    for e in range(E):
        SY.wait(s_ve)
        SY.wait(s_gd)
        for i in range(3):
            SY.op("dma_start", out=gtile(i)[:, :],
                  in_=regin_h[e * R0STR + i * 128: e * R0STR + (i + 1) * 128, :],
                  inc=[s_ld.inc(16)])
        SY.op("dma_start", out=gtile(3)[:, :],
              in_=regin_h[K1BASE + e * R1STR: K1BASE + e * R1STR + 128, :],
              inc=[s_ld.inc(16)])
        TE.wait(s_ld)
        TE.wait(s_se)
        TE.wait(s_ve)
        for i in range(4):
            for dc_ in range(DC):
                TE.op("transpose", psB[:, dc_ * 128:(dc_ + 1) * 128],
                      gtile(i)[:, dc_ * 128:(dc_ + 1) * 128], ident[:],
                      inc=([s_te.inc()] if dc_ == DC - 1 else None))
            VE.wait(s_te)
            if i < 3:
                VE.op("tensor_copy",
                      out=d0g[:, 0:3072].rearrange("p (c w) -> p c w", w=384)[:, :, i * 128:(i + 1) * 128],
                      in_=psB[:, 0:1024].rearrange("p (c w) -> p c w", w=128),
                      inc=[s_ve.inc()])
            else:
                VE.op("tensor_copy",
                      out=d0g[:, GFM1B:GFM1B + 1024].rearrange("p (c w) -> p c w", w=128),
                      in_=psB[:, 0:1024].rearrange("p (c w) -> p c w", w=128),
                      inc=[s_ve.inc()])
            TE.wait(s_ve)
        for fc in range(32):
            pp = fc % 2
            if fc >= 2:
                SY.wait(s_te)
            SY.op("dma_start", out=wfbuf(pp).rearrange("p (c w) -> p c w", w=128),
                  in_=wi_in[e * D:(e + 1) * D, fc * 128:(fc + 1) * 128]
                  .rearrange("(c p) f -> p c f", p=128), inc=[s_ld.inc(16)])
            TE.wait(s_ld)
            TE.wait(s_se)
            for dc_ in range(DC):
                TE.op("matmul", psA[:, 0:384], wfbuf(pp)[:, dc_ * 128:(dc_ + 1) * 128],
                      gfm0(dc_, 0, 384), start=(dc_ == 0), stop=(dc_ == DC - 1))
            for dc_ in range(DC):
                TE.op("matmul", psA[:, 512:640], wfbuf(pp)[:, dc_ * 128:(dc_ + 1) * 128],
                      gfm1(dc_, 0, 128), start=(dc_ == 0), stop=(dc_ == DC - 1),
                      inc=([s_te.inc()] if dc_ == DC - 1 else None))
            SE.wait(s_te)
            SE.op("activation", out=hT0(fc, 0, 384), in_=psA[:, 0:384], func=ACT.Relu,
                  bias=0.0, scale=1.0)
            SE.op("activation", out=hT1(fc, 0, 128), in_=psA[:, 512:640], func=ACT.Relu,
                  bias=0.0, scale=1.0, inc=[s_se.inc()])
        if DBG2 and e == 0:
            GP.wait(s_se)
            GP.op("dma_start", out=dbg_ht, in_=arenaA[:, :], inc=[s_gd.inc(16)])
            TE.wait(s_gd)
        accs = (psA[:, 0:1024], psA[:, 1024:2048], psB[:, 0:1024], psB[:, 1024:2048])
        for fc in range(32):
            pp = fc % 2
            if fc >= 2:
                SY.wait(s_te)
            SY.op("dma_start", out=wobuf(pp)[:, :],
                  in_=wo_in[e * FF + fc * 128: e * FF + (fc + 1) * 128, :],
                  inc=[s_ld.inc(16)])
            TE.wait(s_ld)
            if fc == 0:
                TE.wait(s_se)
                TE.wait(s_ve)
            for ts_ in range(4):
                src = hT0(fc, ts_ * 128, 128) if ts_ < 3 else hT1(fc, 0, 128)
                for cc_ in range(2):
                    TE.op("matmul", accs[ts_][:, cc_ * 512:(cc_ + 1) * 512],
                          src, wobuf(pp)[:, cc_ * 512:(cc_ + 1) * 512],
                          start=(fc == 0), stop=(fc == 31),
                          inc=([s_te.inc()] if (ts_ == 3 and cc_ == 1) else None))
        VE.wait(s_te)
        VE.wait(s_gd)
        for ts_ in range(4):
            VE.op("tensor_copy", out=outsb(ts_)[:, :], in_=accs[ts_],
                  inc=([s_ve.inc()] if ts_ == 3 else None))
        GP.wait(s_ve)
        for i in range(3):
            GP.op("dma_start",
                  out=regout_h[e * R0STR + i * 128: e * R0STR + (i + 1) * 128, :],
                  in_=outsb(i)[:, :], inc=[s_gd.inc(16)])
        GP.op("dma_start", out=regout_h[K1BASE + e * R1STR: K1BASE + e * R1STR + 128, :],
              in_=outsb(3)[:, :], inc=[s_gd.inc(16)])

    GP.wait(s_gd)
    for gt in range(NTC):
        SY.wait(s_ve)
        SY.wait(s_gd)
        SY.op("dma_start", out=gtile(2)[:, :], in_=xmid_h[gt * 128:(gt + 1) * 128, :],
              inc=[s_ld.inc(16)])
        VE.op("tensor_copy", out=itmp0[:], in_=gidx0[:, gt:gt + 1])
        VE.op("tensor_copy", out=itmp1[:], in_=gidx1[:, gt:gt + 1], inc=[s_ve.inc()])
        GP.wait(s_ve)
        GP.op("indirect_dma_start", out=gtile(0)[:, :], out_offset=None,
              in_=regout_h, in_offset=IndirectOffsetOnAxis(itmp0[:], 0),
              inc=[s_gd.inc(16)])
        GP.op("indirect_dma_start", out=gtile(1)[:, :], out_offset=None,
              in_=regout_h, in_offset=IndirectOffsetOnAxis(itmp1[:], 0),
              inc=[s_gd.inc(16)])
        VE.wait(s_ld)
        VE.wait(s_gd)
        VE.op("tensor_scalar", out=gtile(0)[:, :], in0=gtile(0)[:, :],
              scalar1=gs0[:, gt:gt + 1], scalar2=None, op0=OP.mult)
        VE.op("tensor_scalar", out=gtile(1)[:, :], in0=gtile(1)[:, :],
              scalar1=gs1[:, gt:gt + 1], scalar2=None, op0=OP.mult)
        VE.op("tensor_tensor", out=gtile(2)[:, :], in0=gtile(2)[:, :], in1=gtile(0)[:, :],
              op=OP.add)
        VE.op("tensor_tensor", out=gtile(2)[:, :], in0=gtile(2)[:, :], in1=gtile(1)[:, :],
              op=OP.add, inc=[s_ve.inc()])
        GP.wait(s_ve)
        GP.op("dma_start", out=y_out[gt * 128:(gt + 1) * 128, :], in_=gtile(2)[:, :],
              inc=[s_gd.inc(16)])
    if DBG:
        GP.op("dma_start", out=dbg_xmid, in_=xmid_h, inc=[s_gd.inc(16)])
        GP.op("dma_start", out=dbg_xln2, in_=xln2_h, inc=[s_gd.inc(16)])
        GP.op("dma_start", out=dbg_reg, in_=regin_h, inc=[s_gd.inc(16)])
        GP.op("dma_start", out=dbg_rout, in_=regout_h, inc=[s_gd.inc(16)])
        for k, tile_ in enumerate((e1c, e2c, g1c, g2c, r0c, r1c, of0c, of1c, gs0, gs1)):
            GP.op("dma_start", out=dbg_cols[:, k * NTC:(k + 1) * NTC], in_=tile_[:],
                  inc=[s_gd.inc(16)])
        GP.op("dma_start", out=dbg_all, in_=allout[:], inc=[s_gd.inc(16)])
    GP.wait(s_gd)

    with nc.Block() as block:
        @block.sync
        def _(eng):
            r.replay("sync", eng)

        @block.tensor
        def _(eng):
            r.replay("tensor", eng)

        @block.vector
        def _(eng):
            r.replay("vector", eng)

        @block.scalar
        def _(eng):
            r.replay("scalar", eng)

        @block.gpsimd
        def _(eng):
            r.replay("gpsimd", eng)

    ctx.close()
    return nc


_NC_CACHE = [None]


def make_in_maps(L0_attn_x, L0_attn_W_qkv, L0_attn_b_qkv, L0_attn_W_out, L0_attn_b_out,
                 L0_ffn_w_router, L0_ffn_wi, L0_ffn_wo):
    x = np.ascontiguousarray(np.asarray(L0_attn_x, dtype=np.float32))
    wqkvT = np.ascontiguousarray(np.asarray(L0_attn_W_qkv, dtype=np.float32).T)
    bqkv = np.ascontiguousarray(np.asarray(L0_attn_b_qkv, dtype=np.float32).reshape(3 * D, 1))
    woutT = np.ascontiguousarray(np.asarray(L0_attn_W_out, dtype=np.float32).T)
    bout_bc = np.ascontiguousarray(
        np.broadcast_to(np.asarray(L0_attn_b_out, dtype=np.float32)[None, :], (128, D)))
    wrouter = np.ascontiguousarray(np.asarray(L0_ffn_w_router, dtype=np.float32))
    wi = np.ascontiguousarray(np.asarray(L0_ffn_wi, dtype=np.float32).reshape(E * D, FF))
    wo = np.ascontiguousarray(np.asarray(L0_ffn_wo, dtype=np.float32).reshape(E * FF, D))
    ident = np.eye(128, dtype=np.float32)
    iota8bc = np.ascontiguousarray(
        np.broadcast_to(np.arange(E, dtype=np.float32)[None, :], (128, E)))
    iota8col = np.arange(E, dtype=np.float32).reshape(E, 1)
    ones8 = np.ones((E, 1), dtype=np.float32)

    in_maps = []
    for c in range(NCORES):
        rank = c % 4
        grp = c // 4
        rowmask = np.zeros((E, 1), np.float32)
        rowmask[c, 0] = 1.0
        maskprev = np.zeros((E, 1), np.float32)
        maskprev[grp * 4: grp * 4 + rank, 0] = 1.0
        maskgrp = np.zeros((E, 1), np.float32)
        maskgrp[grp * 4: grp * 4 + 4, 0] = 1.0
        in_maps.append({
            "x_in": np.ascontiguousarray(
                np.stack([x[8 * grp + rank], x[8 * grp + rank + 4]]).reshape(TPC, D)),
            "wqkvT": wqkvT, "bqkv": bqkv, "woutT": woutT, "bout_bc": bout_bc,
            "wrouter": wrouter, "wi_in": wi, "wo_in": wo,
            "ident_in": ident, "iota8bc_in": iota8bc, "iota8col_in": iota8col,
            "ones8_in": ones8, "rowmask_in": rowmask, "maskprev_in": maskprev,
            "maskgrp_in": maskgrp,
        })
    return in_maps


def kernel(**inputs):
    in_maps = make_in_maps(**inputs)
    if _NC_CACHE[0] is None:
        _NC_CACHE[0] = build_nc()
    res = run_bass_kernel_spmd(_NC_CACHE[0], in_maps, core_ids=list(range(NCORES)))
    out = np.empty((B, T, D), np.float32)
    for c in range(NCORES):
        grp, rank = c // 4, c % 4
        yo = res.results[c]["y_out"].reshape(2, T, D)
        out[8 * grp + rank] = yo[0]
        out[8 * grp + rank + 4] = yo[1]
    return out


# version 02
# speedup vs baseline: 1.0853x; 1.0853x over previous
import os
import numpy as np
import concourse.bass as bass
import concourse.mybir as mybir
from concourse.bass_utils import run_bass_kernel_spmd
from concourse.bass import IndirectOffsetOnAxis

F32 = mybir.dt.float32
BF16 = mybir.dt.bfloat16
I32 = mybir.dt.int32
U8 = mybir.dt.uint8
OP = mybir.AluOpType
ACT = mybir.ActivationFunctionType
AX = mybir.AxisListType.X

B, T, D, H, HD = 16, 2048, 1024, 16, 64
TOPK = 614
E, FF = 8, 4096
CCAP = 640
NCORES = 8
BPC = 2
TPC = BPC * T
NT = T // 128
NTC = TPC // 128
DC = D // 128
CAP0, CAP1 = 384, 128
R0STR, R1STR = CAP0 + 1, CAP1 + 1
K1BASE = E * R0STR
REGIN_ROWS = E * R0STR + E * R1STR
ZROW = REGIN_ROWS
N_NEWTON = 3
N_FP = 6
N_PROBE = 3
G = 2
Z0 = 0.52533
INV_DENS = 1.0 / (2048 * 0.34765)
TOPKF = float(TOPK)


class Sem:
    def __init__(self, handle):
        self.h = handle
        self.n = 0

    def inc(self, by=1):
        return (self, by)


def _ap_tensor_names(v):
    names = set()
    if v is None or isinstance(v, (int, float, str, bool)):
        return names
    t = getattr(v, "tensor", None)
    if t is not None and hasattr(t, "name"):
        names.add(t.name)
    return names


class EngRec:
    def __init__(self, rec, name):
        self.rec = rec
        self.name = name
        self.last_outs = set()
        self.last_wide_safe = False
        self.sep = None

    def wait(self, sem, val=None):
        v = sem.n if val is None else val
        self.rec.ops[self.name].append(("wait", sem.h, v))

    def op(self, method, *args, inc=None, **kwargs):
        if self.name == "vector" and self.last_outs and not self.last_wide_safe:
            ins = set()
            for v in args[1:]:
                ins |= _ap_tensor_names(v)
            for k, v in kwargs.items():
                if k not in ("out", "accum_out"):
                    ins |= _ap_tensor_names(v)
            if method != "memset" and (ins & self.last_outs) and self.sep is not None:
                self.rec.ops[self.name].append(
                    ("op", "tensor_copy", (), {"out": self.sep[0], "in_": self.sep[1]}, []))
        spec = []
        if inc is not None:
            for s, by in (inc if isinstance(inc, list) else [inc]):
                s.n += by
                spec.append((s.h, by))
        self.rec.ops[self.name].append(("op", method, args, kwargs, spec))
        outs = set()
        wide = False
        if args:
            outs |= _ap_tensor_names(args[0])
            try:
                wide = args[0].free_size() >= 512
            except Exception:
                wide = False
        ov = kwargs.get("out")
        if ov is not None:
            outs |= _ap_tensor_names(ov)
            try:
                wide = ov.free_size() >= 512
            except Exception:
                pass
        acc = kwargs.get("accum_out")
        if acc is not None:
            outs |= _ap_tensor_names(acc)
        self.last_wide_safe = (wide and acc is None and method in
                               ("tensor_copy", "tensor_scalar", "tensor_tensor",
                                "scalar_tensor_tensor", "select"))
        self.last_outs = outs


class Rec:
    def __init__(self):
        self.ops = {k: [] for k in ("sync", "tensor", "vector", "scalar", "gpsimd")}
        for k in self.ops:
            setattr(self, k, EngRec(self, k))

    def replay(self, name, eng):
        for item in self.ops[name]:
            if item[0] == "wait":
                _, h, v = item
                eng.wait_ge(h, v)
            else:
                _, method, args, kwargs, incs = item
                inst = getattr(eng, method)(*args, **kwargs)
                for h, by in incs:
                    inst = inst.then_inc(h, by)


def build_nc():
    nc = bass.Bass("TRN2", target_bir_lowering=False)

    x_in = nc.dram_tensor("x_in", [TPC, D], F32, kind="ExternalInput").ap()
    wqkvT = nc.dram_tensor("wqkvT", [D, 3 * D], F32, kind="ExternalInput").ap()
    bqkv = nc.dram_tensor("bqkv", [3 * D, 1], F32, kind="ExternalInput").ap()
    woutT = nc.dram_tensor("woutT", [D, D], F32, kind="ExternalInput").ap()
    bout_bc = nc.dram_tensor("bout_bc", [128, D], F32, kind="ExternalInput").ap()
    wrouter = nc.dram_tensor("wrouter", [D, E], F32, kind="ExternalInput").ap()
    wi_in = nc.dram_tensor("wi_in", [E * D, FF], F32, kind="ExternalInput").ap()
    wo_in = nc.dram_tensor("wo_in", [E * FF, D], F32, kind="ExternalInput").ap()
    ident_in = nc.dram_tensor("ident_in", [128, 128], F32, kind="ExternalInput").ap()
    iota8bc_in = nc.dram_tensor("iota8bc_in", [128, E], F32, kind="ExternalInput").ap()
    iota8col_in = nc.dram_tensor("iota8col_in", [E, 1], F32, kind="ExternalInput").ap()
    ones8_in = nc.dram_tensor("ones8_in", [E, 1], F32, kind="ExternalInput").ap()
    rowmask_in = nc.dram_tensor("rowmask_in", [E, 1], F32, kind="ExternalInput").ap()
    maskprev_in = nc.dram_tensor("maskprev_in", [E, 1], F32, kind="ExternalInput").ap()
    maskgrp_in = nc.dram_tensor("maskgrp_in", [E, 1], F32, kind="ExternalInput").ap()

    y_out = nc.dram_tensor("y_out", [TPC, D], F32, kind="ExternalOutput").ap()
    _dbg_env = os.environ.get("MK_DEBUG", "")
    DBG = _dbg_env in ("1", "2")
    DBG2 = _dbg_env == "2"
    if DBG:
        dbg_xmid = nc.dram_tensor("dbg_xmid", [TPC, D], F32, kind="ExternalOutput").ap()
        dbg_xln2 = nc.dram_tensor("dbg_xln2", [TPC, D], F32, kind="ExternalOutput").ap()
        dbg_cols = nc.dram_tensor("dbg_cols", [128, 10 * NTC], F32, kind="ExternalOutput").ap()
        dbg_all = nc.dram_tensor("dbg_all", [8, 32], F32, kind="ExternalOutput").ap()
        dbg_reg = nc.dram_tensor("dbg_reg", [REGIN_ROWS, D], F32, kind="ExternalOutput").ap()
        dbg_rout = nc.dram_tensor("dbg_rout", [REGIN_ROWS + 1, D], F32, kind="ExternalOutput").ap()
        dbg_ht = nc.dram_tensor("dbg_ht", [128, 16384], F32, kind="ExternalOutput").ap()
        dbg_sc = nc.dram_tensor("dbg_sc", [128, 2 * 2048], F32, kind="ExternalOutput").ap()
        dbg_w = nc.dram_tensor("dbg_w", [128, 2 * 2048], F32, kind="ExternalOutput").ap()
        dbg_st = nc.dram_tensor("dbg_st", [128, 8 * G], F32, kind="ExternalOutput").ap()
        dbg_kq = nc.dram_tensor("dbg_kq", [64, 3 * 2048], F32, kind="ExternalOutput").ap()
        dbg_xfm = nc.dram_tensor("dbg_xfm", [128, 16384], F32, kind="ExternalOutput").ap()
        dbg_ln = nc.dram_tensor("dbg_ln", [128, 1032], F32, kind="ExternalOutput").ap()

    xmid_h = nc.dram_tensor("xmid_h", [TPC, D], F32, kind="Internal").ap()
    xln2_h = nc.dram_tensor("xln2_h", [TPC, D], F32, kind="Internal").ap()
    regin_h = nc.dram_tensor("regin_h", [REGIN_ROWS, D], F32, kind="Internal").ap()
    regout_h = nc.dram_tensor("regout_h", [REGIN_ROWS + 1, D], F32, kind="Internal").ap()
    ccin_h = nc.dram_tensor("ccin_h", [E, 32], F32, kind="Internal")
    ccout_h = nc.dram_tensor("ccout_h", [E, 32], F32, kind="Internal")

    from contextlib import ExitStack
    ctx = ExitStack()
    sb = lambda name, shape, dt=F32: ctx.enter_context(nc.sbuf_tensor(name, shape, dt))

    arenaA = sb("arenaA", [128, 16384])
    arenaB = sb("arenaB", [128, 16384])
    d0g = sb("d0g", [128, 2 * G * 2048])
    junk = sb("junk", [128, 2048])
    msk = sb("msk", [128, 2048], mybir.dt.uint16)
    kfm_t = sb("kfm_t", [64, 2048])
    qfm_t = sb("qfm_t", [64, 2048])
    vfm = sb("vfm", [64, 2048])
    vtm = sb("vtm", [128, 16 * 65])
    wqkvh = sb("wqkvh", [128, DC * 192])
    ident = sb("ident", [128, 128])
    bq_h = sb("bq_h", [64, 4])
    tLO = sb("tLO", [128, G]); tHI = sb("tHI", [128, G])
    tCLO = sb("tCLO", [128, G]); tCHI = sb("tCHI", [128, G])
    tT = sb("tT", [128, G]); tC = sb("tC", [128, G])
    tTHIT = sb("tTHIT", [128, G]); tFND = sb("tFND", [128, G])
    tS1 = sb("tS1", [128, G]); tS2 = sb("tS2", [128, G]); tS3 = sb("tS3", [128, G])
    tSIG = sb("tSIG", [128, G]); tMU = sb("tMU", [128, G])
    tNEG = sb("tNEG", [128, G]); tP = sb("tP", [128, G])
    tSUMn = sb("tSUMn", [128, 2 * G]); tSQn = sb("tSQn", [128, 2 * G])
    u8g = sb("u8g", [128, G], U8)
    lnA = sb("lnA", [128, 1]); lnB = sb("lnB", [128, 1]); lnC = sb("lnC", [128, 1])
    neghuge = sb("neghuge", [128, 1])
    recz = sb("recz", [128, 1])
    c104n = sb("c104n", [128, 1])
    sepA = sb("sepA", [128, 2]); sepB = sb("sepB", [128, 2])

    itmp0 = sb("itmp0", [128, 1], I32); itmp1 = sb("itmp1", [128, 1], I32)
    aflag = vtm[:, 10 * NTC:11 * NTC]
    sidx0 = vtm[:, 12 * NTC:13 * NTC]; sidx1 = vtm[:, 13 * NTC:14 * NTC]
    gidx0 = vtm[:, 14 * NTC:15 * NTC]; gidx1 = vtm[:, 15 * NTC:16 * NTC]
    rt1 = vtm[:, 11 * NTC:11 * NTC + 8]; rt2 = vtm[:, 11 * NTC + 8:11 * NTC + 16]
    rm1 = sb("rm1", [128, 1]); rm2 = sb("rm2", [128, 1]); rz8 = sb("rz8", [128, 1])
    re2 = sb("re2", [128, 1])
    allin = vtm[0:8, 16 * NTC:16 * NTC + 32]; allout = vtm[0:8, 16 * NTC + 32:16 * NTC + 64]
    cnt16 = vtm[0:1, 16 * NTC + 64:16 * NTC + 96]
    offp = vtm[0:1, 16 * NTC + 96:16 * NTC + 128]; offg = vtm[0:1, 16 * NTC + 128:16 * NTC + 160]
    cA0 = vtm[0:8, 16 * NTC + 160 + 0:16 * NTC + 160 + 1]
    cB0 = vtm[0:8, 16 * NTC + 160 + 1:16 * NTC + 160 + 2]
    cA1 = vtm[0:8, 16 * NTC + 160 + 2:16 * NTC + 160 + 3]
    cB1 = vtm[0:8, 16 * NTC + 160 + 3:16 * NTC + 160 + 4]
    offA0 = vtm[0:8, 16 * NTC + 160 + 4:16 * NTC + 160 + 5]
    offB0 = vtm[0:8, 16 * NTC + 160 + 5:16 * NTC + 160 + 6]
    offA1 = vtm[0:8, 16 * NTC + 160 + 6:16 * NTC + 160 + 7]
    offB1 = vtm[0:8, 16 * NTC + 160 + 7:16 * NTC + 160 + 8]
    tot0c = vtm[0:8, 16 * NTC + 160 + 8:16 * NTC + 160 + 9]
    tmp8 = vtm[0:8, 16 * NTC + 160 + 9:16 * NTC + 160 + 10]
    tmp8b = vtm[0:8, 16 * NTC + 160 + 10:16 * NTC + 160 + 11]
    off0col = vtm[0:8, 16 * NTC + 160 + 11:16 * NTC + 160 + 12]
    off1col = vtm[0:8, 16 * NTC + 160 + 12:16 * NTC + 160 + 13]
    iota8bc = sb("iota8bc", [128, E])
    iota8col = sb("iota8col", [E, 1])
    iotaP = sb("iotaP", [128, 1])
    ones8 = sb("ones8", [E, 1])
    rowmask = sb("rowmask", [E, 1])
    maskprev = sb("maskprev", [E, 1])
    maskgrp = sb("maskgrp", [E, 1])

    psA = ctx.enter_context(nc.psum_tensor("psA", [128, 2048], F32))
    psB = ctx.enter_context(nc.psum_tensor("psB", [128, 2048], F32))

    s_ld = Sem(ctx.enter_context(nc.semaphore("s_ld")))
    s_gd = Sem(ctx.enter_context(nc.semaphore("s_gd")))
    s_te = Sem(ctx.enter_context(nc.semaphore("s_te")))
    s_ve = Sem(ctx.enter_context(nc.semaphore("s_ve")))
    s_se = Sem(ctx.enter_context(nc.semaphore("s_se")))
    s_cc = Sem(ctx.enter_context(nc.semaphore("s_cc")))

    r = Rec()
    SY, TE, VE, SE, GP = r.sync, r.tensor, r.vector, r.scalar, r.gpsimd
    VE.sep = (sepA[:, 0:1], sepB[:, 0:1])


    def xlnfm(dc_, lo, n):
        return arenaA[:, dc_ * 2048 + lo: dc_ * 2048 + lo + n]

    def attnout(tt, lo, n):
        return arenaB[:, tt * 1024 + lo: tt * 1024 + lo + n]

    logits = d0g[:, 6144:6144 + NTC * E]
    e1c = vtm[:, 0:NTC]; e2c = vtm[:, NTC:2 * NTC]
    g1c = vtm[:, 2 * NTC:3 * NTC]; g2c = vtm[:, 3 * NTC:4 * NTC]
    gs0 = vtm[:, 4 * NTC:5 * NTC]; gs1 = vtm[:, 5 * NTC:6 * NTC]
    r0c = vtm[:, 6 * NTC:7 * NTC]; r1c = vtm[:, 7 * NTC:8 * NTC]
    of0c = vtm[:, 8 * NTC:9 * NTC]; of1c = vtm[:, 9 * NTC:10 * NTC]
    ind0 = arenaA[0:8, 0:4096]; cum0 = arenaA[0:8, 4096:8192]
    ind1 = arenaA[0:8, 8192:12288]; cum1 = arenaA[0:8, 12288:16384]
    e1b = arenaA[96:104, 0:4096]; e2b = arenaA[96:104, 4096:8192]
    rowsA = arenaA[32:33, 0:4096]; rowsB = arenaA[32:33, 4096:8192]
    rowsC = arenaA[32:33, 8192:12288]; rowsD = arenaA[32:33, 12288:16384]
    zfill = arenaA[64:96, 0:1024]

    def hT0(fc, lo, n):
        return arenaA[:, fc * 384 + lo: fc * 384 + lo + n]

    HT1B = 32 * 384

    def hT1(fc, lo, n):
        return arenaA[:, HT1B + fc * 128 + lo: HT1B + fc * 128 + lo + n]

    def wfbuf(pp):
        return arenaB[:, pp * 1024: (pp + 1) * 1024]

    def wobuf(pp):
        return arenaB[:, 2048 + pp * 1024: 2048 + (pp + 1) * 1024]

    def gtile(i):
        return arenaB[:, 4096 + i * 1024: 4096 + (i + 1) * 1024]

    def outsb(i):
        return arenaB[:, 8192 + i * 1024: 8192 + (i + 1) * 1024]

    xfm2 = arenaB[:, 12288:13312]
    xtile_w = junk[:, 1024:2048]

    def gfm0(dc_, lo, n):
        return d0g[:, dc_ * 384 + lo: dc_ * 384 + lo + n]

    GFM1B = 8 * 384

    def gfm1(dc_, lo, n):
        return d0g[:, GFM1B + dc_ * 128 + lo: GFM1B + dc_ * 128 + lo + n]

    def xtile_ln(par):
        return d0g[:, par * 1024:(par + 1) * 1024]

    def d0(gi):
        return d0g[:, gi * 2048:(gi + 1) * 2048]

    SY.op("dma_start", out=ident[:], in_=ident_in, inc=[s_ld.inc(16)])
    SY.op("dma_start", out=iota8bc[:], in_=iota8bc_in, inc=[s_ld.inc(16)])
    SY.op("dma_start", out=iota8col[:], in_=iota8col_in, inc=[s_ld.inc(16)])
    SY.op("dma_start", out=iotaP[96:104, :], in_=iota8col_in, inc=[s_ld.inc(16)])
    SY.op("dma_start", out=ones8[:], in_=ones8_in, inc=[s_ld.inc(16)])
    SY.op("dma_start", out=rowmask[:], in_=rowmask_in, inc=[s_ld.inc(16)])
    SY.op("dma_start", out=maskprev[:], in_=maskprev_in, inc=[s_ld.inc(16)])
    SY.op("dma_start", out=maskgrp[:], in_=maskgrp_in, inc=[s_ld.inc(16)])
    VE.wait(s_ld)
    VE.op("memset", neghuge[:], -1e30)
    VE.op("memset", c104n[:], -104.0)
    VE.op("memset", sepB[:], 0.0)
    VE.op("memset", sepA[:], 0.0)
    VE.op("memset", vtm[:].rearrange("p (c w) -> p c w", w=65)[:, :, 64:65], 1.0,
          inc=[s_ve.inc()])

    def ln_tile(xt_view, out_view):
        VE.op("reduce_sum", lnA[:], xt_view, axis=AX)
        VE.op("tensor_scalar", out=lnA[:], in0=lnA[:], scalar1=1.0 / D, scalar2=None,
              op0=OP.mult)
        VE.op("tensor_scalar", out=lnB[:], in0=lnA[:], scalar1=-1.0, scalar2=None,
              op0=OP.mult, inc=[s_ve.inc()])
        SE.wait(s_ve)
        SE.op("activation", out=junk[:, 0:1024], in_=xt_view, func=ACT.Square,
              bias=lnB[:], scale=1.0, accum_out=lnC[:], inc=[s_se.inc()])
        VE.wait(s_se)
        VE.op("tensor_scalar", out=lnC[:], in0=lnC[:], scalar1=1.0 / D, scalar2=1e-5,
              op0=OP.mult, op1=OP.add, inc=[s_ve.inc()])
        SE.wait(s_ve)
        SE.op("activation", out=lnB[:], in_=lnC[:], func=ACT.Ln, bias=0.0, scale=1.0)
        SE.op("activation", out=lnC[:], in_=lnB[:], func=ACT.Exp, bias=0.0, scale=-0.5,
              inc=[s_se.inc()])
        VE.wait(s_se)
        VE.op("tensor_scalar", out=out_view, in0=xt_view, scalar1=lnA[:],
              scalar2=lnC[:], op0=OP.subtract, op1=OP.mult, inc=[s_ve.inc()])

    for b in range(BPC):
        tok0 = b * T
        for tt in range(NT):
            xt = xtile_ln(tt % 2)
            SY.wait(s_ve)
            SY.op("dma_start", out=xt, in_=x_in[tok0 + tt * 128: tok0 + (tt + 1) * 128, :],
                  inc=[s_ld.inc(16)])
            VE.wait(s_ld)
            ln_tile(xt, junk[:, 0:1024])
            TE.wait(s_ve)
            TE.wait(s_se)
            for dc_ in range(DC):
                TE.op("transpose", psB[:, dc_ * 128:(dc_ + 1) * 128],
                      junk[:, dc_ * 128:(dc_ + 1) * 128], ident[:],
                      inc=([s_te.inc()] if dc_ == DC - 1 else None))
            VE.wait(s_te)
            VE.op("tensor_copy",
                  out=arenaA[:].rearrange("p (c w) -> p c w", w=2048)[:, :, tt * 128:(tt + 1) * 128],
                  in_=psB[:, 0:1024].rearrange("p (c w) -> p c w", w=128),
                  inc=[s_ve.inc()])
            if DBG2 and b == 0 and tt == 0:
                GP.wait(s_ve)
                GP.op("dma_start", out=dbg_ln[:, 0:1024], in_=junk[:, 0:1024], inc=[s_gd.inc(16)])
                GP.op("dma_start", out=dbg_ln[:, 1024:1025], in_=lnA[:], allow_slow_non_contiguous=True, inc=[s_gd.inc(16)])
                GP.op("dma_start", out=dbg_ln[:, 1025:1026], in_=lnC[:], allow_slow_non_contiguous=True, inc=[s_gd.inc(16)])
                VE.wait(s_gd)
            TE.wait(s_ve)

        VE.wait(s_te)
        VE.wait(s_ld)
        VE.op("memset", vtm[:].rearrange("p (c w) -> p c w", w=65)[:, :, 64:65], 1.0,
              inc=[s_ve.inc()])
        for h in range(H):
            SY.wait(s_se)
            for j, base in enumerate((0, D, 2 * D)):
                SY.op("dma_start",
                      out=wqkvh[:].rearrange("p (c w) -> p c w", w=192)[:, :, j * 64:(j + 1) * 64],
                      in_=wqkvT[:, base + h * 64: base + (h + 1) * 64]
                      .rearrange("(c p) f -> p c f", p=128),
                      inc=[s_ld.inc(16)])
                SY.op("dma_start", out=bq_h[:, j:j + 1],
                      in_=bqkv[base + h * 64: base + (h + 1) * 64, :],
                      inc=[s_ld.inc(16)])
            TE.wait(s_ld)
            SE.wait(s_ld)
            SE.op("activation", out=bq_h[0:64, 3:4], in_=bq_h[0:64, 0:1],
                  func=ACT.Identity, bias=0.0, scale=1.0 / 32, inc=[s_se.inc()])
            for j, dest, bias_col, scl in ((0, qfm_t[0:64, :], 3, 1.0 / 32), (1, kfm_t[0:64, :], 1, 1.0),
                                           (2, vfm[0:64, :], 2, 1.0)):
                TE.wait(s_se)
                TE.wait(s_ve)
                for dc_ in range(DC):
                    for cc_ in range(4):
                        TE.op("matmul", psB[0:64, cc_ * 512:(cc_ + 1) * 512],
                              wqkvh[:, dc_ * 192 + j * 64: dc_ * 192 + (j + 1) * 64],
                              xlnfm(dc_, cc_ * 512, 512),
                              start=(dc_ == 0), stop=(dc_ == DC - 1),
                              inc=([s_te.inc()] if (dc_ == DC - 1 and cc_ == 3) else None))
                SE.wait(s_te)
                SE.op("activation", out=dest, in_=psB[0:64, :], func=ACT.Identity,
                      bias=bq_h[0:64, bias_col:bias_col + 1], scale=scl,
                      inc=[s_se.inc()])
            TE.wait(s_se)
            for jc in range(16):
                TE.op("transpose", psB[:, jc * 64:(jc + 1) * 64],
                      vfm[0:64, jc * 128:(jc + 1) * 128], ident[0:64, 0:64],
                      inc=([s_te.inc()] if jc == 15 else None))
            VE.wait(s_te)
            VE.op("tensor_copy",
                  out=vtm[:].rearrange("p (c w) -> p c w", w=65)[:, :, 0:64],
                  in_=psB[:, 0:1024].rearrange("p (c w) -> p c w", w=64),
                  inc=[s_ve.inc()])
            TE.wait(s_ve)

            for g in range(NT // G):
                for gi in range(G):
                    tt = g * G + gi
                    TE.wait(s_se)
                    TE.wait(s_ve)
                    for cc_ in range(4):
                        TE.op("matmul", psA[:, cc_ * 512:(cc_ + 1) * 512],
                              qfm_t[0:64, tt * 128:(tt + 1) * 128],
                              kfm_t[0:64, cc_ * 512:(cc_ + 1) * 512],
                              start=True, stop=True,
                              inc=([s_te.inc()] if cc_ == 3 else None))
                    SE.wait(s_te)
                    SE.wait(s_ve)
                    SE.op("activation", out=d0(gi), in_=psA[:, :], func=ACT.Identity,
                          bias=0.0, scale=1.0, inc=[s_se.inc()])
                    SE.op("activation", out=junk[:], in_=d0(gi), func=ACT.Square,
                          bias=0.0, scale=1.0, accum_out=tS3[:, gi:gi + 1],
                          inc=[s_se.inc()])
                    VE.wait(s_se)
                    VE.op("reduce_sum", tS1[:, gi:gi + 1], d0(gi), axis=AX,
                          inc=[s_ve.inc()])
                if DBG2 and b == 0 and h == 0 and g == 0:
                    GP.wait(s_ve)
                    GP.op("dma_start", out=dbg_sc, in_=d0g[:, :], inc=[s_gd.inc(16)])
                    GP.op("dma_start", out=dbg_kq[0:64, 0:2048], in_=kfm_t[:, :], inc=[s_gd.inc(16)])
                    GP.op("dma_start", out=dbg_kq[0:64, 2048:4096], in_=qfm_t[:, :], inc=[s_gd.inc(16)])
                    GP.op("dma_start", out=dbg_kq[0:64, 4096:6144], in_=vfm[:, :], inc=[s_gd.inc(16)])
                    GP.op("dma_start", out=dbg_xfm, in_=arenaA[:, :], inc=[s_gd.inc(16)])
                    VE.wait(s_gd)
                VE.op("tensor_scalar", out=tMU[:], in0=tS1[:], scalar1=1.0 / 2048,
                      scalar2=None, op0=OP.mult)
                VE.op("tensor_scalar", out=tS2[:], in0=tS3[:], scalar1=1.0 / 2048,
                      scalar2=None, op0=OP.mult)
                VE.op("tensor_tensor", out=tS1[:], in0=tMU[:], in1=tMU[:], op=OP.mult)
                VE.op("tensor_tensor", out=tS2[:], in0=tS2[:], in1=tS1[:], op=OP.subtract)
                VE.op("tensor_scalar", out=tS2[:], in0=tS2[:], scalar1=1e-12, scalar2=None,
                      op0=OP.max, inc=[s_ve.inc()])
                SE.wait(s_ve)
                SE.op("activation", out=tS1[:], in_=tS2[:], func=ACT.Ln, bias=0.0, scale=1.0)
                SE.op("activation", out=tSIG[:], in_=tS1[:], func=ACT.Exp, bias=0.0,
                      scale=0.5, inc=[s_se.inc()])
                VE.wait(s_se)
                VE.op("scalar_tensor_tensor", out=tLO[:], in0=tSIG[:], scalar=-2.0,
                      in1=tMU[:], op0=OP.mult, op1=OP.add)
                VE.op("scalar_tensor_tensor", out=tHI[:], in0=tSIG[:], scalar=3.0,
                      in1=tMU[:], op0=OP.mult, op1=OP.add)
                VE.op("scalar_tensor_tensor", out=tT[:], in0=tSIG[:], scalar=Z0,
                      in1=tMU[:], op0=OP.mult, op1=OP.add)
                VE.op("memset", tCLO[:], 2048.0)
                VE.op("memset", tCHI[:], 0.0)
                VE.op("memset", tTHIT[:], 0.0)
                VE.op("memset", tFND[:], 0.0)

                def upd_eval():
                    VE.op("tensor_scalar", out=u8g[:], in0=tC[:], scalar1=TOPKF,
                          scalar2=None, op0=OP.is_ge)
                    VE.op("select", out=tLO[:], mask=u8g[:], on_true=tT[:], on_false=tLO[:])
                    VE.op("select", out=tCLO[:], mask=u8g[:], on_true=tC[:], on_false=tCLO[:])
                    VE.op("select", out=tHI[:], mask=u8g[:], on_true=tHI[:], on_false=tT[:])
                    VE.op("select", out=tCHI[:], mask=u8g[:], on_true=tCHI[:], on_false=tC[:])
                    VE.op("tensor_scalar", out=u8g[:], in0=tC[:], scalar1=TOPKF,
                          scalar2=None, op0=OP.is_equal)
                    VE.op("select", out=tTHIT[:], mask=u8g[:], on_true=tT[:], on_false=tTHIT[:])
                    VE.op("tensor_scalar", out=tS1[:], in0=tC[:], scalar1=TOPKF,
                          scalar2=None, op0=OP.is_equal)
                    VE.op("tensor_tensor", out=tFND[:], in0=tFND[:], in1=tS1[:], op=OP.max)

                def clip_t():
                    VE.op("tensor_tensor", out=tS1[:], in0=tHI[:], in1=tLO[:], op=OP.subtract)
                    VE.op("scalar_tensor_tensor", out=tS2[:], in0=tS1[:], scalar=0.002,
                          in1=tLO[:], op0=OP.mult, op1=OP.add)
                    VE.op("scalar_tensor_tensor", out=tS3[:], in0=tS1[:], scalar=-0.002,
                          in1=tHI[:], op0=OP.mult, op1=OP.add)
                    VE.op("tensor_tensor", out=tT[:], in0=tT[:], in1=tS2[:], op=OP.max)
                    VE.op("tensor_tensor", out=tT[:], in0=tT[:], in1=tS3[:], op=OP.min)
                    VE.op("tensor_scalar", out=u8g[:], in0=tFND[:], scalar1=0.5,
                          scalar2=None, op0=OP.is_ge)
                    VE.op("select", out=tT[:], mask=u8g[:], on_true=tLO[:], on_false=tT[:])

                def count_eval(use_scalar):
                    if use_scalar:
                        VE.op("tensor_scalar", out=tNEG[:], in0=tT[:], scalar1=-1.0,
                              scalar2=None, op0=OP.mult, inc=[s_ve.inc()])
                        SE.wait(s_ve)
                        for gi in range(G):
                            SE.op("activation", out=junk[:], in_=d0(gi), func=ACT.Sign,
                                  bias=tNEG[:, gi:gi + 1], scale=1.0,
                                  accum_out=tC[:, gi:gi + 1],
                                  inc=([s_se.inc()] if gi == G - 1 else None))
                        VE.wait(s_se)
                        VE.op("tensor_scalar", out=tC[:], in0=tC[:], scalar1=2048.0,
                              scalar2=0.5, op0=OP.add, op1=OP.mult)
                    else:
                        for gi in range(G):
                            VE.op("tensor_scalar", out=junk[:], in0=d0(gi),
                                  scalar1=tT[:, gi:gi + 1], scalar2=None,
                                  op0=OP.is_ge, op1=OP.add, accum_out=tC[:, gi:gi + 1])

                n_ev = 0
                count_eval(False); n_ev += 1
                upd_eval()
                for _ in range(N_NEWTON - 1):
                    VE.op("tensor_scalar", out=tS1[:], in0=tC[:], scalar1=TOPKF,
                          scalar2=None, op0=OP.subtract)
                    VE.op("tensor_tensor", out=tS1[:], in0=tS1[:], in1=tSIG[:], op=OP.mult)
                    VE.op("scalar_tensor_tensor", out=tT[:], in0=tS1[:], scalar=INV_DENS,
                          in1=tT[:], op0=OP.mult, op1=OP.add)
                    clip_t()
                    count_eval(n_ev % 2 == 1); n_ev += 1
                    upd_eval()
                for _ in range(N_FP):
                    VE.op("tensor_scalar", out=tS1[:], in0=tCLO[:], scalar1=TOPKF,
                          scalar2=None, op0=OP.subtract)
                    VE.op("tensor_tensor", out=tS2[:], in0=tCLO[:], in1=tCHI[:],
                          op=OP.subtract)
                    VE.op("tensor_scalar", out=tS2[:], in0=tS2[:], scalar1=1.0,
                          scalar2=None, op0=OP.max)
                    VE.op("reciprocal", tS2[:], tS2[:])
                    VE.op("tensor_tensor", out=tS1[:], in0=tS1[:], in1=tS2[:], op=OP.mult)
                    VE.op("tensor_tensor", out=tS2[:], in0=tHI[:], in1=tLO[:], op=OP.subtract)
                    VE.op("tensor_tensor", out=tS1[:], in0=tS1[:], in1=tS2[:], op=OP.mult)
                    VE.op("tensor_tensor", out=tT[:], in0=tLO[:], in1=tS1[:], op=OP.add)
                    clip_t()
                    count_eval(n_ev % 2 == 1); n_ev += 1
                    upd_eval()
                for _ in range(N_PROBE):
                    for gi in range(G):
                        VE.op("tensor_scalar", out=msk[:], in0=d0(gi),
                              scalar1=tHI[:, gi:gi + 1], scalar2=None, op0=OP.is_lt)
                        VE.op("select", out=junk[:], mask=msk[:], on_true=d0(gi),
                              on_false=neghuge[:].to_broadcast([128, 2048]))
                        VE.op("reduce_max", tP[:, gi:gi + 1], junk[:], axis=AX)
                    VE.op("tensor_copy", out=tT[:], in_=tP[:])
                    clip_t()
                    count_eval(False); n_ev += 1
                    upd_eval()
                VE.op("tensor_scalar", out=u8g[:], in0=tFND[:], scalar1=0.5,
                      scalar2=None, op0=OP.is_ge)
                VE.op("select", out=tT[:], mask=u8g[:], on_true=tTHIT[:], on_false=tLO[:])
                VE.op("tensor_scalar", out=tS2[:], in0=tT[:], scalar1=104.0,
                      scalar2=None, op0=OP.subtract, inc=[s_ve.inc()])
                if DBG2 and b == 0 and h == 0 and g == 0:
                    GP.wait(s_ve)
                    for kk, tile_ in enumerate((tT, tC, tCLO, tCHI, tFND, tLO, tHI, tSIG)):
                        GP.op("dma_start", out=dbg_st[:, kk * G:(kk + 1) * G], in_=tile_[:],
                              inc=[s_gd.inc(16)])
                    VE.wait(s_gd)

                for gi in range(G):
                    tt = g * G + gi
                    VE.op("tensor_scalar", out=msk[:], in0=d0(gi), scalar1=tT[:, gi:gi + 1],
                          scalar2=None, op0=OP.is_ge)
                    VE.op("scalar_tensor_tensor", out=junk[:], in0=d0(gi),
                          scalar=tS2[:, gi:gi + 1], in1=msk[:],
                          op0=OP.subtract, op1=OP.mult, inc=[s_ve.inc()])
                    TE.wait(s_ve)
                    TE.wait(s_se)
                    for jc in range(16):
                        TE.op("transpose", psB[:, jc * 128:(jc + 1) * 128],
                              junk[:, jc * 128:(jc + 1) * 128], ident[:],
                              inc=([s_te.inc()] if jc == 15 else None))
                    SE.wait(s_te)
                    SE.op("activation", out=d0(gi), in_=psB[:, :], func=ACT.Exp,
                          bias=c104n[:], scale=1.0, inc=[s_se.inc()])
                    TE.wait(s_se)
                    TE.wait(s_ve)
                    for jc in range(16):
                        TE.op("matmul", psA[:, 0:65], d0(gi)[:, jc * 128:(jc + 1) * 128],
                              vtm[:, jc * 65:(jc + 1) * 65],
                              start=(jc == 0), stop=(jc == 15),
                              inc=([s_te.inc()] if jc == 15 else None))
                    if DBG2 and b == 0 and h == 0 and g == 0:
                        GP.wait(s_se)
                        GP.op("dma_start", out=dbg_w[:, gi * 2048:(gi + 1) * 2048],
                              in_=d0(gi), inc=[s_gd.inc(16)])
                        TE.wait(s_gd)
                    VE.wait(s_te)
                    VE.op("reciprocal", recz[:], psA[:, 64:65])
                    VE.op("tensor_scalar", out=attnout(tt, h * 64, 64), in0=psA[:, 0:64],
                          scalar1=recz[:], scalar2=None, op0=OP.mult, inc=[s_ve.inc()])

        SY.wait(s_ve)
        SY.wait(s_se)
        SY.wait(s_te)
        SY.op("dma_start", out=arenaA[:, 0:8192].rearrange("p (c w) -> p c w", w=1024),
              in_=woutT.rearrange("(c p) o -> p c o", p=128), inc=[s_ld.inc(16)])
        SY.op("dma_start", out=vtm[:, 0:1024], in_=bout_bc, inc=[s_ld.inc(16)])
        for tt in range(NT):
            SY.wait(s_ve)
            SY.op("dma_start", out=xtile_w,
                  in_=x_in[tok0 + tt * 128: tok0 + (tt + 1) * 128, :], inc=[s_ld.inc(16)])
            TE.wait(s_ve)
            for dc_ in range(DC):
                TE.op("transpose", psB[:, dc_ * 128:(dc_ + 1) * 128],
                      attnout(tt, dc_ * 128, 128), ident[:],
                      inc=([s_te.inc()] if dc_ == DC - 1 else None))
            VE.wait(s_te)
            VE.op("tensor_copy", out=junk[:, 0:1024], in_=psB[:, 0:1024], inc=[s_ve.inc()])
            TE.wait(s_ve)
            TE.wait(s_ld)
            for dc_ in range(DC):
                for cc_ in range(2):
                    TE.op("matmul", psA[:, 1024 + cc_ * 512: 1024 + (cc_ + 1) * 512],
                          junk[:, dc_ * 128:(dc_ + 1) * 128],
                          arenaA[:, dc_ * 1024 + cc_ * 512: dc_ * 1024 + (cc_ + 1) * 512],
                          start=(dc_ == 0), stop=(dc_ == DC - 1),
                          inc=([s_te.inc()] if (dc_ == DC - 1 and cc_ == 1) else None))
            VE.wait(s_te)
            VE.wait(s_ld)
            VE.op("tensor_tensor", out=xtile_w, in0=xtile_w, in1=vtm[:, 0:1024], op=OP.add)
            VE.op("scalar_tensor_tensor", out=xtile_w, in0=psA[:, 1024:2048], scalar=0.0,
                  in1=xtile_w, op0=OP.add, op1=OP.add, inc=[s_ve.inc()])
            GP.wait(s_ve)
            GP.op("dma_start", out=xmid_h[tok0 + tt * 128: tok0 + (tt + 1) * 128, :],
                  in_=xtile_w, inc=[s_gd.inc(16)])
            TE.wait(s_ve)
            SY.wait(s_gd)

    VE.wait(s_gd)
    VE.op("memset", zfill[:, :], 0.0, inc=[s_ve.inc()])
    GP.wait(s_ve)
    nfill = (REGIN_ROWS + 31) // 32
    for i in range(nfill):
        lo = i * 32
        n = min(32, REGIN_ROWS - lo)
        GP.op("dma_start", out=regin_h[lo:lo + n, :], in_=zfill[0:n, :], inc=[s_gd.inc(16)])
    GP.op("dma_start", out=regout_h[ZROW:ZROW + 1, :], in_=zfill[0:1, :], inc=[s_gd.inc(16)])

    SY.op("dma_start", out=wqkvh[:].rearrange("p (c w) -> p c w", w=192)[:, :, 184:192],
          in_=wrouter.rearrange("(c p) e -> p c e", p=128), inc=[s_ld.inc(16)])
    for gt in range(NTC):
        xt = xtile_ln(gt % 2)
        SY.wait(s_ve)
        SY.wait(s_gd)
        SY.op("dma_start", out=xt, in_=xmid_h[gt * 128:(gt + 1) * 128, :], inc=[s_ld.inc(16)])
        VE.wait(s_ld)
        ln_tile(xt, junk[:, 0:1024])
        GP.wait(s_ve)
        GP.op("dma_start", out=xln2_h[gt * 128:(gt + 1) * 128, :], in_=junk[:, 0:1024],
              inc=[s_gd.inc(16)])
        TE.wait(s_ve)
        TE.wait(s_se)
        for dc_ in range(DC):
            TE.op("transpose", psB[:, dc_ * 128:(dc_ + 1) * 128],
                  junk[:, dc_ * 128:(dc_ + 1) * 128], ident[:],
                  inc=([s_te.inc()] if dc_ == DC - 1 else None))
        VE.wait(s_te)
        VE.wait(s_gd)
        VE.op("tensor_copy", out=xfm2[:], in_=psB[:, 0:1024], inc=[s_ve.inc()])
        TE.wait(s_ve)
        TE.wait(s_ld)
        for dc_ in range(DC):
            TE.op("matmul", psA[:, 0:8], xfm2[:, dc_ * 128:(dc_ + 1) * 128],
                  wqkvh[:, dc_ * 192 + 184: dc_ * 192 + 192],
                  start=(dc_ == 0), stop=(dc_ == DC - 1),
                  inc=([s_te.inc()] if dc_ == DC - 1 else None))
        VE.wait(s_te)
        VE.op("tensor_copy", out=logits[:, gt * 8:(gt + 1) * 8], in_=psA[:, 0:8],
              inc=[s_ve.inc()])
        TE.wait(s_ve)

    for gt in range(NTC):
        lg = logits[:, gt * 8:(gt + 1) * 8]
        VE.op("reduce_max", rm1[:], lg, axis=AX)
        VE.op("tensor_scalar", out=rt1[:], in0=lg, scalar1=rm1[:], scalar2=None,
              op0=OP.is_equal)
        VE.op("scalar_tensor_tensor", out=rt2[:], in0=rt1[:], scalar=-1e30, in1=lg,
              op0=OP.mult, op1=OP.add)
        VE.op("reduce_max", rm2[:], rt2[:], axis=AX)
        VE.op("tensor_tensor", out=rt1[:], in0=rt1[:], in1=iota8bc[:], op=OP.mult)
        VE.op("reduce_sum", e1c[:, gt:gt + 1], rt1[:], axis=AX)
        VE.op("tensor_scalar", out=rt1[:], in0=rt2[:], scalar1=rm2[:], scalar2=None,
              op0=OP.is_equal)
        VE.op("tensor_tensor", out=rt1[:], in0=rt1[:], in1=iota8bc[:], op=OP.mult)
        VE.op("reduce_sum", e2c[:, gt:gt + 1], rt1[:], axis=AX)
        VE.op("tensor_scalar", out=rm1[:], in0=rm1[:], scalar1=-1.0, scalar2=None,
              op0=OP.mult, inc=[s_ve.inc()])
        SE.wait(s_ve)
        SE.op("activation", out=rt2[:], in_=lg, func=ACT.Exp, bias=rm1[:], scale=1.0,
              accum_out=rz8[:])
        SE.op("activation", out=re2[:], in_=rm2[:], func=ACT.Exp, bias=rm1[:], scale=1.0,
              inc=[s_se.inc()])
        VE.wait(s_se)
        VE.op("reciprocal", g1c[:, gt:gt + 1], rz8[:])
        VE.op("tensor_tensor", out=g2c[:, gt:gt + 1], in0=re2[:], in1=g1c[:, gt:gt + 1],
              op=OP.mult, inc=[s_ve.inc()])

    for (col, row) in ((e1c, rowsA), (e2c, rowsB)):
        for half in range(2):
            TE.wait(s_ve)
            for i in range(16):
                gt = half * 16 + i
                TE.op("transpose", psB[0:1, i * 128:(i + 1) * 128], col[:, gt:gt + 1],
                      ident[:], inc=([s_te.inc()] if i == 15 else None))
            VE.wait(s_te)
            VE.op("tensor_copy", out=row[:, half * 2048:(half + 1) * 2048],
                  in_=psB[0:1, 0:2048], inc=[s_ve.inc()])
            TE.wait(s_ve)
    GP.wait(s_ve)
    for p in range(8):
        GP.op("dma_start", out=e1b[p:p + 1, :], in_=rowsA[:, :], inc=[s_gd.inc(16)])
        GP.op("dma_start", out=e2b[p:p + 1, :], in_=rowsB[:, :], inc=[s_gd.inc(16)])
    VE.wait(s_gd)
    VE.op("tensor_scalar", out=ind0[:, :], in0=e1b[:, :], scalar1=iotaP[96:104, :],
          scalar2=None, op0=OP.is_equal)
    VE.op("tensor_scalar", out=ind1[:, :], in0=e2b[:, :], scalar1=iotaP[96:104, :],
          scalar2=None, op0=OP.is_equal)
    VE.op("tensor_tensor_scan", out=cum0[:, :], data0=ind0[:, :], data1=ind0[:, :],
          initial=0.0, op0=OP.add, op1=OP.max)
    VE.op("tensor_tensor_scan", out=cum1[:, :], data0=ind1[:, :], data1=ind1[:, :],
          initial=0.0, op0=OP.add, op1=OP.max)
    VE.op("tensor_copy", out=cA0[:], in_=cum0[:, T - 1: T])
    VE.op("tensor_copy", out=cA1[:], in_=cum1[:, T - 1: T])
    VE.op("tensor_copy", out=cB0[:], in_=cum0[:, TPC - 1: TPC])
    VE.op("tensor_copy", out=cB1[:], in_=cum1[:, TPC - 1: TPC])
    VE.op("tensor_copy", out=sepA[:, 0:1], in_=sepB[:, 0:1])
    VE.op("tensor_tensor", out=cB0[:], in0=cB0[:], in1=cA0[:], op=OP.subtract)
    VE.op("tensor_tensor", out=cB1[:], in0=cB1[:], in1=cA1[:], op=OP.subtract,
          inc=[s_ve.inc()])
    TE.wait(s_ve)
    TE.op("transpose", psB[0:1, 0:8], cA0[:], ident[0:8, 0:8])
    TE.op("transpose", psB[0:1, 8:16], cB0[:], ident[0:8, 0:8])
    TE.op("transpose", psB[0:1, 16:24], cA1[:], ident[0:8, 0:8])
    TE.op("transpose", psB[0:1, 24:32], cB1[:], ident[0:8, 0:8],
          inc=[s_te.inc()])
    VE.wait(s_te)
    VE.op("tensor_copy", out=cnt16[:, :], in_=psB[0:1, 0:32], inc=[s_ve.inc()])
    TE.wait(s_ve)
    GP.wait(s_ve)
    for p in range(8):
        GP.op("dma_start", out=allin[p:p + 1, :], in_=cnt16[:, :], inc=[s_gd.inc(16)])
    VE.wait(s_gd)
    VE.op("tensor_scalar", out=allin[:], in0=allin[:], scalar1=rowmask[:], scalar2=None,
          op0=OP.mult, inc=[s_ve.inc()])
    GP.wait(s_ve)
    GP.op("dma_start", out=ccin_h.ap(), in_=allin[:], inc=[s_gd.inc(16)])
    GP.wait(s_gd)
    GP.op("collective_compute", "AllReduce", OP.add, replica_groups=[list(range(8))],
          ins=[ccin_h.ap().opt()], outs=[ccout_h.ap().opt()], inc=[s_cc.inc()])
    GP.wait(s_cc)
    GP.op("dma_start", out=allout[:], in_=ccout_h.ap(), inc=[s_gd.inc(16)])

    VE.op("tensor_tensor", out=cum0[:, :], in0=cum0[:, :], in1=ind0[:, :], op=OP.mult)
    VE.op("tensor_tensor", out=cum1[:, :], in0=cum1[:, :], in1=ind1[:, :], op=OP.mult,
          inc=[s_ve.inc()])
    TE.wait(s_ve)

    def row_reduce_and_transpose(src8, rowbuf, colbuf):
        for half in range(2):
            for cc_ in range(4):
                TE.op("matmul", psB[0:1, cc_ * 512:(cc_ + 1) * 512], ones8[:],
                      src8[:, half * 2048 + cc_ * 512: half * 2048 + (cc_ + 1) * 512],
                      start=True, stop=True,
                      inc=([s_te.inc()] if cc_ == 3 else None))
            VE.wait(s_te)
            VE.op("tensor_copy", out=rowbuf[:, half * 2048:(half + 1) * 2048],
                  in_=psB[0:1, 0:2048], inc=[s_ve.inc()])
            TE.wait(s_ve)
        for half in range(2):
            for i in range(16):
                TE.op("transpose", psB[:, i:i + 1],
                      rowbuf[:, (half * 16 + i) * 128:(half * 16 + i + 1) * 128],
                      ident[32:33, 32:33], inc=([s_te.inc()] if i == 15 else None))
            VE.wait(s_te)
            VE.op("tensor_copy", out=colbuf[:, half * 16:(half + 1) * 16], in_=psB[:, 0:16],
                  inc=[s_ve.inc()])
            TE.wait(s_ve)

    row_reduce_and_transpose(cum0, rowsC, r0c)
    row_reduce_and_transpose(cum1, rowsD, r1c)

    VE.wait(s_gd)
    VE.op("engine_nop", inc=[s_ve.inc()])
    TE.wait(s_ve)
    TE.op("matmul", psB[0:1, 0:32], maskprev[:], allout[:], start=True, stop=True)
    TE.op("matmul", psB[0:1, 32:64], maskgrp[:], allout[:], start=True, stop=True,
          inc=[s_te.inc()])
    VE.wait(s_te)
    VE.op("tensor_copy", out=offp[:, :], in_=psB[0:1, 0:32])
    VE.op("tensor_copy", out=offg[:, :], in_=psB[0:1, 32:64], inc=[s_ve.inc()])
    TE.wait(s_ve)
    pass
    for i4, rowv in enumerate((offp, offg)):
        for j4 in range(4):
            TE.op("transpose", psB[0:8, i4 * 4 + j4: i4 * 4 + j4 + 1],
                  rowv[0:1, j4 * 8:(j4 + 1) * 8], ident[0:1, 0:1],
                  inc=([s_te.inc()] if (i4 == 1 and j4 == 3) else None))
    VE.wait(s_te)
    VE.op("tensor_copy", out=offA0[:], in_=psB[0:8, 0:1])
    VE.op("tensor_copy", out=offB0[:], in_=psB[0:8, 1:2])
    VE.op("tensor_copy", out=offA1[:], in_=psB[0:8, 2:3])
    VE.op("tensor_copy", out=offB1[:], in_=psB[0:8, 3:4])
    VE.op("tensor_copy", out=tot0c[:], in_=psB[0:8, 4:5])
    VE.op("tensor_copy", out=tmp8[:], in_=psB[0:8, 6:7])
    VE.op("tensor_tensor", out=offB0[:], in0=offB0[:], in1=tot0c[:], op=OP.add)
    VE.op("tensor_tensor", out=offB0[:], in0=offB0[:], in1=cA0[:], op=OP.subtract)
    VE.op("tensor_copy", out=tmp8b[:], in_=psB[0:8, 5:6])
    VE.op("tensor_tensor", out=tot0c[:], in0=tot0c[:], in1=tmp8b[:], op=OP.add)
    VE.op("tensor_tensor", out=offA1[:], in0=offA1[:], in1=tot0c[:], op=OP.add)
    VE.op("tensor_tensor", out=offB1[:], in0=offB1[:], in1=tot0c[:], op=OP.add)
    VE.op("tensor_tensor", out=offB1[:], in0=offB1[:], in1=tmp8[:], op=OP.add)
    VE.op("tensor_tensor", out=offB1[:], in0=offB1[:], in1=cA1[:], op=OP.subtract,
          inc=[s_ve.inc()])
    for (ind, ocA, ocB, rowbuf, colbuf) in ((ind0, offA0, offB0, rowsA, of0c),
                                            (ind1, offA1, offB1, rowsB, of1c)):
        VE.op("tensor_scalar", out=ind[:, 0:T], in0=ind[:, 0:T], scalar1=ocA[:],
              scalar2=None, op0=OP.mult)
        VE.op("tensor_scalar", out=ind[:, T:TPC], in0=ind[:, T:TPC], scalar1=ocB[:],
              scalar2=None, op0=OP.mult, inc=[s_ve.inc()])
        TE.wait(s_ve)
        row_reduce_and_transpose(ind, rowbuf, colbuf)

    for (rcol, ocol, ecol, gcol, gscol, scol, gicol, cap, rstr, base) in (
            (r0c, of0c, e1c, g1c, gs0, sidx0, gidx0, CAP0, R0STR, 0),
            (r1c, of1c, e2c, g2c, gs1, sidx1, gidx1, CAP1, R1STR, K1BASE)):
        VE.op("tensor_tensor", out=ocol[:], in0=ocol[:], in1=rcol[:], op=OP.add)
        VE.op("tensor_scalar", out=ocol[:], in0=ocol[:], scalar1=1.0, scalar2=None,
              op0=OP.subtract)
        VE.op("tensor_scalar", out=gscol[:], in0=ocol[:], scalar1=float(CCAP),
              scalar2=None, op0=OP.is_lt)
        VE.op("tensor_scalar", out=ocol[:], in0=rcol[:], scalar1=float(cap) + 0.5,
              scalar2=None, op0=OP.is_lt)
        VE.op("tensor_tensor", out=gscol[:], in0=gscol[:], in1=ocol[:], op=OP.mult)
        VE.op("tensor_tensor", out=gscol[:], in0=gscol[:], in1=gcol[:], op=OP.mult)
        VE.op("tensor_scalar", out=ocol[:], in0=rcol[:], scalar1=1.0, scalar2=float(cap),
              op0=OP.subtract, op1=OP.min)
        VE.op("scalar_tensor_tensor", out=ocol[:], in0=ecol[:], scalar=float(rstr),
              in1=ocol[:], op0=OP.mult, op1=OP.add)
        VE.op("tensor_scalar", out=ocol[:], in0=ocol[:], scalar1=float(base),
              scalar2=None, op0=OP.add)
        VE.op("tensor_copy", out=scol[:], in_=ocol[:])
        VE.op("tensor_scalar", out=aflag[:], in0=gscol[:], scalar1=0.0, scalar2=None,
              op0=OP.is_gt)
        VE.op("tensor_tensor", out=ocol[:], in0=ocol[:], in1=aflag[:], op=OP.mult)
        VE.op("tensor_scalar", out=aflag[:], in0=aflag[:], scalar1=-float(ZROW),
              scalar2=float(ZROW), op0=OP.mult, op1=OP.add)
        VE.op("tensor_tensor", out=ocol[:], in0=ocol[:], in1=aflag[:], op=OP.add)
        VE.op("tensor_copy", out=gicol[:], in_=ocol[:], inc=[s_ve.inc()])

    GP.wait(s_ve)
    GP.wait(s_gd)
    for gt in range(NTC):
        VE.wait(s_gd)
        VE.op("tensor_copy", out=itmp0[:], in_=sidx0[:, gt:gt + 1])
        VE.op("tensor_copy", out=itmp1[:], in_=sidx1[:, gt:gt + 1], inc=[s_ve.inc()])
        GP.wait(s_ve)
        GP.op("dma_start", out=gtile(gt % 2)[:, :], in_=xln2_h[gt * 128:(gt + 1) * 128, :],
              inc=[s_gd.inc(16)])
        GP.wait(s_gd)
        GP.op("indirect_dma_start", out=regin_h,
              out_offset=IndirectOffsetOnAxis(itmp0[:], 0),
              in_=gtile(gt % 2)[:, :], in_offset=None, inc=[s_gd.inc(16)])
        GP.op("indirect_dma_start", out=regin_h,
              out_offset=IndirectOffsetOnAxis(itmp1[:], 0),
              in_=gtile(gt % 2)[:, :], in_offset=None, inc=[s_gd.inc(16)])
        GP.wait(s_gd)

    for e in range(E):
        SY.wait(s_ve)
        SY.wait(s_gd)
        for i in range(3):
            SY.op("dma_start", out=gtile(i)[:, :],
                  in_=regin_h[e * R0STR + i * 128: e * R0STR + (i + 1) * 128, :],
                  inc=[s_ld.inc(16)])
        SY.op("dma_start", out=gtile(3)[:, :],
              in_=regin_h[K1BASE + e * R1STR: K1BASE + e * R1STR + 128, :],
              inc=[s_ld.inc(16)])
        TE.wait(s_ld)
        TE.wait(s_se)
        TE.wait(s_ve)
        for i in range(4):
            for dc_ in range(DC):
                TE.op("transpose", psB[:, dc_ * 128:(dc_ + 1) * 128],
                      gtile(i)[:, dc_ * 128:(dc_ + 1) * 128], ident[:],
                      inc=([s_te.inc()] if dc_ == DC - 1 else None))
            VE.wait(s_te)
            if i < 3:
                VE.op("tensor_copy",
                      out=d0g[:, 0:3072].rearrange("p (c w) -> p c w", w=384)[:, :, i * 128:(i + 1) * 128],
                      in_=psB[:, 0:1024].rearrange("p (c w) -> p c w", w=128),
                      inc=[s_ve.inc()])
            else:
                VE.op("tensor_copy",
                      out=d0g[:, GFM1B:GFM1B + 1024].rearrange("p (c w) -> p c w", w=128),
                      in_=psB[:, 0:1024].rearrange("p (c w) -> p c w", w=128),
                      inc=[s_ve.inc()])
            TE.wait(s_ve)
        for fc in range(32):
            pp = fc % 2
            if fc >= 2:
                SY.wait(s_te)
            SY.op("dma_start", out=wfbuf(pp).rearrange("p (c w) -> p c w", w=128),
                  in_=wi_in[e * D:(e + 1) * D, fc * 128:(fc + 1) * 128]
                  .rearrange("(c p) f -> p c f", p=128), inc=[s_ld.inc(16)])
            TE.wait(s_ld)
            TE.wait(s_se)
            for dc_ in range(DC):
                TE.op("matmul", psA[:, 0:384], wfbuf(pp)[:, dc_ * 128:(dc_ + 1) * 128],
                      gfm0(dc_, 0, 384), start=(dc_ == 0), stop=(dc_ == DC - 1))
            for dc_ in range(DC):
                TE.op("matmul", psA[:, 512:640], wfbuf(pp)[:, dc_ * 128:(dc_ + 1) * 128],
                      gfm1(dc_, 0, 128), start=(dc_ == 0), stop=(dc_ == DC - 1),
                      inc=([s_te.inc()] if dc_ == DC - 1 else None))
            SE.wait(s_te)
            SE.op("activation", out=hT0(fc, 0, 384), in_=psA[:, 0:384], func=ACT.Relu,
                  bias=0.0, scale=1.0)
            SE.op("activation", out=hT1(fc, 0, 128), in_=psA[:, 512:640], func=ACT.Relu,
                  bias=0.0, scale=1.0, inc=[s_se.inc()])
        if DBG2 and e == 0:
            GP.wait(s_se)
            GP.op("dma_start", out=dbg_ht, in_=arenaA[:, :], inc=[s_gd.inc(16)])
            TE.wait(s_gd)
        accs = (psA[:, 0:1024], psA[:, 1024:2048], psB[:, 0:1024], psB[:, 1024:2048])
        for fc in range(32):
            pp = fc % 2
            if fc >= 2:
                SY.wait(s_te)
            SY.op("dma_start", out=wobuf(pp)[:, :],
                  in_=wo_in[e * FF + fc * 128: e * FF + (fc + 1) * 128, :],
                  inc=[s_ld.inc(16)])
            TE.wait(s_ld)
            if fc == 0:
                TE.wait(s_se)
                TE.wait(s_ve)
            for ts_ in range(4):
                src = hT0(fc, ts_ * 128, 128) if ts_ < 3 else hT1(fc, 0, 128)
                for cc_ in range(2):
                    TE.op("matmul", accs[ts_][:, cc_ * 512:(cc_ + 1) * 512],
                          src, wobuf(pp)[:, cc_ * 512:(cc_ + 1) * 512],
                          start=(fc == 0), stop=(fc == 31),
                          inc=([s_te.inc()] if (ts_ == 3 and cc_ == 1) else None))
        VE.wait(s_te)
        VE.wait(s_gd)
        for ts_ in range(4):
            VE.op("tensor_copy", out=outsb(ts_)[:, :], in_=accs[ts_],
                  inc=([s_ve.inc()] if ts_ == 3 else None))
        GP.wait(s_ve)
        for i in range(3):
            GP.op("dma_start",
                  out=regout_h[e * R0STR + i * 128: e * R0STR + (i + 1) * 128, :],
                  in_=outsb(i)[:, :], inc=[s_gd.inc(16)])
        GP.op("dma_start", out=regout_h[K1BASE + e * R1STR: K1BASE + e * R1STR + 128, :],
              in_=outsb(3)[:, :], inc=[s_gd.inc(16)])

    GP.wait(s_gd)
    for gt in range(NTC):
        SY.wait(s_ve)
        SY.wait(s_gd)
        SY.op("dma_start", out=gtile(2)[:, :], in_=xmid_h[gt * 128:(gt + 1) * 128, :],
              inc=[s_ld.inc(16)])
        VE.op("tensor_copy", out=itmp0[:], in_=gidx0[:, gt:gt + 1])
        VE.op("tensor_copy", out=itmp1[:], in_=gidx1[:, gt:gt + 1], inc=[s_ve.inc()])
        GP.wait(s_ve)
        GP.op("indirect_dma_start", out=gtile(0)[:, :], out_offset=None,
              in_=regout_h, in_offset=IndirectOffsetOnAxis(itmp0[:], 0),
              inc=[s_gd.inc(16)])
        GP.op("indirect_dma_start", out=gtile(1)[:, :], out_offset=None,
              in_=regout_h, in_offset=IndirectOffsetOnAxis(itmp1[:], 0),
              inc=[s_gd.inc(16)])
        VE.wait(s_ld)
        VE.wait(s_gd)
        VE.op("tensor_scalar", out=gtile(0)[:, :], in0=gtile(0)[:, :],
              scalar1=gs0[:, gt:gt + 1], scalar2=None, op0=OP.mult)
        VE.op("tensor_scalar", out=gtile(1)[:, :], in0=gtile(1)[:, :],
              scalar1=gs1[:, gt:gt + 1], scalar2=None, op0=OP.mult)
        VE.op("tensor_tensor", out=gtile(2)[:, :], in0=gtile(2)[:, :], in1=gtile(0)[:, :],
              op=OP.add)
        VE.op("tensor_tensor", out=gtile(2)[:, :], in0=gtile(2)[:, :], in1=gtile(1)[:, :],
              op=OP.add, inc=[s_ve.inc()])
        GP.wait(s_ve)
        GP.op("dma_start", out=y_out[gt * 128:(gt + 1) * 128, :], in_=gtile(2)[:, :],
              inc=[s_gd.inc(16)])
    if DBG:
        GP.op("dma_start", out=dbg_xmid, in_=xmid_h, inc=[s_gd.inc(16)])
        GP.op("dma_start", out=dbg_xln2, in_=xln2_h, inc=[s_gd.inc(16)])
        GP.op("dma_start", out=dbg_reg, in_=regin_h, inc=[s_gd.inc(16)])
        GP.op("dma_start", out=dbg_rout, in_=regout_h, inc=[s_gd.inc(16)])
        for k, tile_ in enumerate((e1c, e2c, g1c, g2c, r0c, r1c, of0c, of1c, gs0, gs1)):
            GP.op("dma_start", out=dbg_cols[:, k * NTC:(k + 1) * NTC], in_=tile_[:],
                  inc=[s_gd.inc(16)])
        GP.op("dma_start", out=dbg_all, in_=allout[:], inc=[s_gd.inc(16)])
    GP.wait(s_gd)

    with nc.Block() as block:
        @block.sync
        def _(eng):
            r.replay("sync", eng)

        @block.tensor
        def _(eng):
            r.replay("tensor", eng)

        @block.vector
        def _(eng):
            r.replay("vector", eng)

        @block.scalar
        def _(eng):
            r.replay("scalar", eng)

        @block.gpsimd
        def _(eng):
            r.replay("gpsimd", eng)

    ctx.close()
    return nc


_NC_CACHE = [None]


def make_in_maps(L0_attn_x, L0_attn_W_qkv, L0_attn_b_qkv, L0_attn_W_out, L0_attn_b_out,
                 L0_ffn_w_router, L0_ffn_wi, L0_ffn_wo):
    x = np.ascontiguousarray(np.asarray(L0_attn_x, dtype=np.float32))
    wqkvT = np.ascontiguousarray(np.asarray(L0_attn_W_qkv, dtype=np.float32).T)
    bqkv = np.ascontiguousarray(np.asarray(L0_attn_b_qkv, dtype=np.float32).reshape(3 * D, 1))
    woutT = np.ascontiguousarray(np.asarray(L0_attn_W_out, dtype=np.float32).T)
    bout_bc = np.ascontiguousarray(
        np.broadcast_to(np.asarray(L0_attn_b_out, dtype=np.float32)[None, :], (128, D)))
    wrouter = np.ascontiguousarray(np.asarray(L0_ffn_w_router, dtype=np.float32))
    wi = np.ascontiguousarray(np.asarray(L0_ffn_wi, dtype=np.float32).reshape(E * D, FF))
    wo = np.ascontiguousarray(np.asarray(L0_ffn_wo, dtype=np.float32).reshape(E * FF, D))
    ident = np.eye(128, dtype=np.float32)
    iota8bc = np.ascontiguousarray(
        np.broadcast_to(np.arange(E, dtype=np.float32)[None, :], (128, E)))
    iota8col = np.arange(E, dtype=np.float32).reshape(E, 1)
    ones8 = np.ones((E, 1), dtype=np.float32)

    in_maps = []
    for c in range(NCORES):
        rank = c % 4
        grp = c // 4
        rowmask = np.zeros((E, 1), np.float32)
        rowmask[c, 0] = 1.0
        maskprev = np.zeros((E, 1), np.float32)
        maskprev[grp * 4: grp * 4 + rank, 0] = 1.0
        maskgrp = np.zeros((E, 1), np.float32)
        maskgrp[grp * 4: grp * 4 + 4, 0] = 1.0
        in_maps.append({
            "x_in": np.ascontiguousarray(
                np.stack([x[8 * grp + rank], x[8 * grp + rank + 4]]).reshape(TPC, D)),
            "wqkvT": wqkvT, "bqkv": bqkv, "woutT": woutT, "bout_bc": bout_bc,
            "wrouter": wrouter, "wi_in": wi, "wo_in": wo,
            "ident_in": ident, "iota8bc_in": iota8bc, "iota8col_in": iota8col,
            "ones8_in": ones8, "rowmask_in": rowmask, "maskprev_in": maskprev,
            "maskgrp_in": maskgrp,
        })
    return in_maps


def kernel(**inputs):
    in_maps = make_in_maps(**inputs)
    if _NC_CACHE[0] is None:
        _NC_CACHE[0] = build_nc()
    res = run_bass_kernel_spmd(_NC_CACHE[0], in_maps, core_ids=list(range(NCORES)))
    out = np.empty((B, T, D), np.float32)
    for c in range(NCORES):
        grp, rank = c // 4, c % 4
        yo = res.results[c]["y_out"].reshape(2, T, D)
        out[8 * grp + rank] = yo[0]
        out[8 * grp + rank + 4] = yo[1]
    return out


# version 03
# speedup vs baseline: 1.0906x; 1.0049x over previous
import os
import numpy as np
import concourse.bass as bass
import concourse.mybir as mybir
from concourse.bass_utils import run_bass_kernel_spmd
from concourse.bass import IndirectOffsetOnAxis

F32 = mybir.dt.float32
BF16 = mybir.dt.bfloat16
I32 = mybir.dt.int32
U8 = mybir.dt.uint8
OP = mybir.AluOpType
ACT = mybir.ActivationFunctionType
AX = mybir.AxisListType.X

B, T, D, H, HD = 16, 2048, 1024, 16, 64
TOPK = 614
E, FF = 8, 4096
CCAP = 640
NCORES = 8
BPC = 2
TPC = BPC * T
NT = T // 128
NTC = TPC // 128
DC = D // 128
CAP0, CAP1 = 384, 128
R0STR, R1STR = CAP0 + 1, CAP1 + 1
K1BASE = E * R0STR
REGIN_ROWS = E * R0STR + E * R1STR
ZROW = REGIN_ROWS
N_NEWTON = 3
N_FP = 6
N_PROBE = 3
G = 2
Z0 = 0.52533
INV_DENS = 1.0 / (2048 * 0.34765)
TOPKF = float(TOPK)


class Sem:
    def __init__(self, handle):
        self.h = handle
        self.n = 0

    def inc(self, by=1):
        return (self, by)


def _ap_tensor_names(v):
    names = set()
    if v is None or isinstance(v, (int, float, str, bool)):
        return names
    t = getattr(v, "tensor", None)
    if t is not None and hasattr(t, "name"):
        names.add(t.name)
    return names


class EngRec:
    def __init__(self, rec, name):
        self.rec = rec
        self.name = name
        self.last_outs = set()
        self.last_wide_safe = False
        self.sep = None

    def wait(self, sem, val=None):
        v = sem.n if val is None else val
        self.rec.ops[self.name].append(("wait", sem.h, v))

    def op(self, method, *args, inc=None, **kwargs):
        if self.name == "vector" and self.last_outs and not self.last_wide_safe:
            ins = set()
            for v in args[1:]:
                ins |= _ap_tensor_names(v)
            for k, v in kwargs.items():
                if k not in ("out", "accum_out"):
                    ins |= _ap_tensor_names(v)
            if method != "memset" and (ins & self.last_outs) and self.sep is not None:
                self.rec.ops[self.name].append(
                    ("op", "tensor_copy", (), {"out": self.sep[0], "in_": self.sep[1]}, []))
        spec = []
        if inc is not None:
            for s, by in (inc if isinstance(inc, list) else [inc]):
                s.n += by
                spec.append((s.h, by))
        self.rec.ops[self.name].append(("op", method, args, kwargs, spec))
        outs = set()
        wide = False
        if args:
            outs |= _ap_tensor_names(args[0])
            try:
                wide = args[0].free_size() >= 512
            except Exception:
                wide = False
        ov = kwargs.get("out")
        if ov is not None:
            outs |= _ap_tensor_names(ov)
            try:
                wide = ov.free_size() >= 512
            except Exception:
                pass
        acc = kwargs.get("accum_out")
        if acc is not None:
            outs |= _ap_tensor_names(acc)
        self.last_wide_safe = (wide and acc is None and method in
                               ("tensor_copy", "tensor_scalar", "tensor_tensor",
                                "scalar_tensor_tensor", "select"))
        self.last_outs = outs


class Rec:
    def __init__(self):
        self.ops = {k: [] for k in ("sync", "tensor", "vector", "scalar", "gpsimd")}
        for k in self.ops:
            setattr(self, k, EngRec(self, k))

    def replay(self, name, eng):
        for item in self.ops[name]:
            if item[0] == "wait":
                _, h, v = item
                eng.wait_ge(h, v)
            else:
                _, method, args, kwargs, incs = item
                inst = getattr(eng, method)(*args, **kwargs)
                for h, by in incs:
                    inst = inst.then_inc(h, by)


def build_nc():
    nc = bass.Bass("TRN2", target_bir_lowering=False)

    x_in = nc.dram_tensor("x_in", [TPC, D], F32, kind="ExternalInput").ap()
    wqkvT = nc.dram_tensor("wqkvT", [D, 3 * D], F32, kind="ExternalInput").ap()
    bqkv = nc.dram_tensor("bqkv", [3 * D, 1], F32, kind="ExternalInput").ap()
    woutT = nc.dram_tensor("woutT", [D, D], F32, kind="ExternalInput").ap()
    bout_bc = nc.dram_tensor("bout_bc", [128, D], F32, kind="ExternalInput").ap()
    wrouter = nc.dram_tensor("wrouter", [D, E], F32, kind="ExternalInput").ap()
    wi_in = nc.dram_tensor("wi_in", [E * D, FF], F32, kind="ExternalInput").ap()
    wo_in = nc.dram_tensor("wo_in", [E * FF, D], F32, kind="ExternalInput").ap()
    ident_in = nc.dram_tensor("ident_in", [128, 128], F32, kind="ExternalInput").ap()
    iota8bc_in = nc.dram_tensor("iota8bc_in", [128, E], F32, kind="ExternalInput").ap()
    iota8col_in = nc.dram_tensor("iota8col_in", [E, 1], F32, kind="ExternalInput").ap()
    ones8_in = nc.dram_tensor("ones8_in", [E, 1], F32, kind="ExternalInput").ap()
    rowmask_in = nc.dram_tensor("rowmask_in", [E, 1], F32, kind="ExternalInput").ap()
    maskprev_in = nc.dram_tensor("maskprev_in", [E, 1], F32, kind="ExternalInput").ap()
    maskgrp_in = nc.dram_tensor("maskgrp_in", [E, 1], F32, kind="ExternalInput").ap()

    y_out = nc.dram_tensor("y_out", [TPC, D], F32, kind="ExternalOutput").ap()
    _dbg_env = os.environ.get("MK_DEBUG", "")
    DBG = _dbg_env in ("1", "2")
    DBG2 = _dbg_env == "2"
    if DBG:
        dbg_xmid = nc.dram_tensor("dbg_xmid", [TPC, D], F32, kind="ExternalOutput").ap()
        dbg_xln2 = nc.dram_tensor("dbg_xln2", [TPC, D], F32, kind="ExternalOutput").ap()
        dbg_cols = nc.dram_tensor("dbg_cols", [128, 10 * NTC], F32, kind="ExternalOutput").ap()
        dbg_all = nc.dram_tensor("dbg_all", [8, 32], F32, kind="ExternalOutput").ap()
        dbg_reg = nc.dram_tensor("dbg_reg", [REGIN_ROWS, D], F32, kind="ExternalOutput").ap()
        dbg_rout = nc.dram_tensor("dbg_rout", [REGIN_ROWS + 1, D], F32, kind="ExternalOutput").ap()
        dbg_ht = nc.dram_tensor("dbg_ht", [128, 16384], F32, kind="ExternalOutput").ap()
        dbg_sc = nc.dram_tensor("dbg_sc", [128, 2 * 2048], F32, kind="ExternalOutput").ap()
        dbg_w = nc.dram_tensor("dbg_w", [128, 2 * 2048], F32, kind="ExternalOutput").ap()
        dbg_st = nc.dram_tensor("dbg_st", [128, 8 * G], F32, kind="ExternalOutput").ap()
        dbg_kq = nc.dram_tensor("dbg_kq", [64, 3 * 2048], F32, kind="ExternalOutput").ap()
        dbg_xfm = nc.dram_tensor("dbg_xfm", [128, 16384], F32, kind="ExternalOutput").ap()
        dbg_ln = nc.dram_tensor("dbg_ln", [128, 1032], F32, kind="ExternalOutput").ap()

    xmid_h = nc.dram_tensor("xmid_h", [TPC, D], F32, kind="Internal").ap()
    xln2_h = nc.dram_tensor("xln2_h", [TPC, D], F32, kind="Internal").ap()
    regin_h = nc.dram_tensor("regin_h", [REGIN_ROWS, D], F32, kind="Internal").ap()
    regout_h = nc.dram_tensor("regout_h", [REGIN_ROWS + 1, D], F32, kind="Internal").ap()
    ccin_h = nc.dram_tensor("ccin_h", [E, 32], F32, kind="Internal")
    ccout_h = nc.dram_tensor("ccout_h", [E, 32], F32, kind="Internal")

    from contextlib import ExitStack
    ctx = ExitStack()
    sb = lambda name, shape, dt=F32: ctx.enter_context(nc.sbuf_tensor(name, shape, dt))

    arenaA = sb("arenaA", [128, 16384])
    arenaB = sb("arenaB", [128, 16384])
    d0g = sb("d0g", [128, 2 * G * 2048])
    junk = sb("junk", [128, 2048])
    msk = sb("msk", [128, 2048], mybir.dt.uint16)
    kfm_t = sb("kfm_t", [64, 2048])
    qfm_t = sb("qfm_t", [64, 2048])
    vfm = sb("vfm", [64, 2048])
    vtm = sb("vtm", [128, 16 * 65])
    wqkvh = sb("wqkvh", [128, DC * 192])
    ident = sb("ident", [128, 128])
    bq_h = sb("bq_h", [64, 4])
    tLO = sb("tLO", [128, G]); tHI = sb("tHI", [128, G])
    tCLO = sb("tCLO", [128, G]); tCHI = sb("tCHI", [128, G])
    tT = sb("tT", [128, G]); tC = sb("tC", [128, G])
    tTHIT = sb("tTHIT", [128, G]); tFND = sb("tFND", [128, G])
    tS1 = sb("tS1", [128, G]); tS2 = sb("tS2", [128, G]); tS3 = sb("tS3", [128, G])
    tSIG = sb("tSIG", [128, G]); tMU = sb("tMU", [128, G])
    tNEG = sb("tNEG", [128, G]); tP = sb("tP", [128, G])
    tSUMn = sb("tSUMn", [128, 2 * G]); tSQn = sb("tSQn", [128, 2 * G])
    u8g = sb("u8g", [128, G], U8)
    lnA = sb("lnA", [128, 1]); lnB = sb("lnB", [128, 1]); lnC = sb("lnC", [128, 1])
    neghuge = sb("neghuge", [128, 1])
    recz = sb("recz", [128, 1])
    c104n = sb("c104n", [128, 1])
    sepA = sb("sepA", [128, 2]); sepB = sb("sepB", [128, 2])

    itmp0 = sb("itmp0", [128, 1], I32); itmp1 = sb("itmp1", [128, 1], I32)
    aflag = vtm[:, 10 * NTC:11 * NTC]
    sidx0 = vtm[:, 12 * NTC:13 * NTC]; sidx1 = vtm[:, 13 * NTC:14 * NTC]
    gidx0 = vtm[:, 14 * NTC:15 * NTC]; gidx1 = vtm[:, 15 * NTC:16 * NTC]
    rt1 = vtm[:, 11 * NTC:11 * NTC + 8]; rt2 = vtm[:, 11 * NTC + 8:11 * NTC + 16]
    rm1 = sb("rm1", [128, 1]); rm2 = sb("rm2", [128, 1]); rz8 = sb("rz8", [128, 1])
    re2 = sb("re2", [128, 1])
    allin = vtm[0:8, 16 * NTC:16 * NTC + 32]; allout = vtm[0:8, 16 * NTC + 32:16 * NTC + 64]
    cnt16 = vtm[0:1, 16 * NTC + 64:16 * NTC + 96]
    offp = vtm[0:1, 16 * NTC + 96:16 * NTC + 128]; offg = vtm[0:1, 16 * NTC + 128:16 * NTC + 160]
    cA0 = vtm[0:8, 16 * NTC + 160 + 0:16 * NTC + 160 + 1]
    cB0 = vtm[0:8, 16 * NTC + 160 + 1:16 * NTC + 160 + 2]
    cA1 = vtm[0:8, 16 * NTC + 160 + 2:16 * NTC + 160 + 3]
    cB1 = vtm[0:8, 16 * NTC + 160 + 3:16 * NTC + 160 + 4]
    offA0 = vtm[0:8, 16 * NTC + 160 + 4:16 * NTC + 160 + 5]
    offB0 = vtm[0:8, 16 * NTC + 160 + 5:16 * NTC + 160 + 6]
    offA1 = vtm[0:8, 16 * NTC + 160 + 6:16 * NTC + 160 + 7]
    offB1 = vtm[0:8, 16 * NTC + 160 + 7:16 * NTC + 160 + 8]
    tot0c = vtm[0:8, 16 * NTC + 160 + 8:16 * NTC + 160 + 9]
    tmp8 = vtm[0:8, 16 * NTC + 160 + 9:16 * NTC + 160 + 10]
    tmp8b = vtm[0:8, 16 * NTC + 160 + 10:16 * NTC + 160 + 11]
    off0col = vtm[0:8, 16 * NTC + 160 + 11:16 * NTC + 160 + 12]
    off1col = vtm[0:8, 16 * NTC + 160 + 12:16 * NTC + 160 + 13]
    iota8bc = sb("iota8bc", [128, E])
    iota8col = sb("iota8col", [E, 1])
    iotaP = sb("iotaP", [128, 1])
    ones8 = sb("ones8", [E, 1])
    rowmask = sb("rowmask", [E, 1])
    maskprev = sb("maskprev", [E, 1])
    maskgrp = sb("maskgrp", [E, 1])

    psA = ctx.enter_context(nc.psum_tensor("psA", [128, 2048], F32))
    psB = ctx.enter_context(nc.psum_tensor("psB", [128, 2048], F32))

    s_ld = Sem(ctx.enter_context(nc.semaphore("s_ld")))
    s_gd = Sem(ctx.enter_context(nc.semaphore("s_gd")))
    s_te = Sem(ctx.enter_context(nc.semaphore("s_te")))
    s_ve = Sem(ctx.enter_context(nc.semaphore("s_ve")))
    s_se = Sem(ctx.enter_context(nc.semaphore("s_se")))
    s_cc = Sem(ctx.enter_context(nc.semaphore("s_cc")))

    r = Rec()
    SY, TE, VE, SE, GP = r.sync, r.tensor, r.vector, r.scalar, r.gpsimd
    VE.sep = (sepA[:, 0:1], sepB[:, 0:1])


    def xlnfm(dc_, lo, n):
        return arenaA[:, dc_ * 2048 + lo: dc_ * 2048 + lo + n]

    def attnout(tt, lo, n):
        return arenaB[:, tt * 1024 + lo: tt * 1024 + lo + n]

    logits = d0g[:, 6144:6144 + NTC * E]
    e1c = vtm[:, 0:NTC]; e2c = vtm[:, NTC:2 * NTC]
    g1c = vtm[:, 2 * NTC:3 * NTC]; g2c = vtm[:, 3 * NTC:4 * NTC]
    gs0 = vtm[:, 4 * NTC:5 * NTC]; gs1 = vtm[:, 5 * NTC:6 * NTC]
    r0c = vtm[:, 6 * NTC:7 * NTC]; r1c = vtm[:, 7 * NTC:8 * NTC]
    of0c = vtm[:, 8 * NTC:9 * NTC]; of1c = vtm[:, 9 * NTC:10 * NTC]
    ind0 = arenaA[0:8, 0:4096]; cum0 = arenaA[0:8, 4096:8192]
    ind1 = arenaA[0:8, 8192:12288]; cum1 = arenaA[0:8, 12288:16384]
    e1b = arenaA[96:104, 0:4096]; e2b = arenaA[96:104, 4096:8192]
    rowsA = arenaA[32:33, 0:4096]; rowsB = arenaA[32:33, 4096:8192]
    rowsC = arenaA[32:33, 8192:12288]; rowsD = arenaA[32:33, 12288:16384]
    zfill = arenaA[64:96, 0:1024]

    def hT0(fc, lo, n):
        return arenaA[:, fc * 384 + lo: fc * 384 + lo + n]

    HT1B = 32 * 384

    def hT1(fc, lo, n):
        return arenaA[:, HT1B + fc * 128 + lo: HT1B + fc * 128 + lo + n]

    def wfbuf(pp):
        return arenaB[:, pp * 1024: (pp + 1) * 1024]

    def wobuf(pp):
        return arenaB[:, 2048 + pp * 1024: 2048 + (pp + 1) * 1024]

    def gtile(i):
        return arenaB[:, 4096 + i * 1024: 4096 + (i + 1) * 1024]

    def outsb(i):
        return arenaB[:, 8192 + i * 1024: 8192 + (i + 1) * 1024]

    xfm2 = arenaB[:, 12288:13312]
    xtile_w = junk[:, 1024:2048]

    def gfm0(dc_, lo, n):
        return d0g[:, dc_ * 384 + lo: dc_ * 384 + lo + n]

    GFM1B = 8 * 384

    def gfm1(dc_, lo, n):
        return d0g[:, GFM1B + dc_ * 128 + lo: GFM1B + dc_ * 128 + lo + n]

    def xtile_ln(par):
        return d0g[:, par * 1024:(par + 1) * 1024]

    def d0(gi):
        return d0g[:, gi * 2048:(gi + 1) * 2048]

    SY.op("dma_start", out=ident[:], in_=ident_in, inc=[s_ld.inc(16)])
    SY.op("dma_start", out=iota8bc[:], in_=iota8bc_in, inc=[s_ld.inc(16)])
    SY.op("dma_start", out=iota8col[:], in_=iota8col_in, inc=[s_ld.inc(16)])
    SY.op("dma_start", out=iotaP[96:104, :], in_=iota8col_in, inc=[s_ld.inc(16)])
    SY.op("dma_start", out=ones8[:], in_=ones8_in, inc=[s_ld.inc(16)])
    SY.op("dma_start", out=rowmask[:], in_=rowmask_in, inc=[s_ld.inc(16)])
    SY.op("dma_start", out=maskprev[:], in_=maskprev_in, inc=[s_ld.inc(16)])
    SY.op("dma_start", out=maskgrp[:], in_=maskgrp_in, inc=[s_ld.inc(16)])
    VE.wait(s_ld)
    VE.op("memset", neghuge[:], -1e30)
    VE.op("memset", c104n[:], -104.0)
    VE.op("memset", sepB[:], 0.0)
    VE.op("memset", sepA[:], 0.0)
    VE.op("memset", vtm[:].rearrange("p (c w) -> p c w", w=65)[:, :, 64:65], 1.0,
          inc=[s_ve.inc()])

    def ln_tile(xt_view, out_view):
        VE.op("reduce_sum", lnA[:], xt_view, axis=AX)
        VE.op("tensor_scalar", out=lnA[:], in0=lnA[:], scalar1=1.0 / D, scalar2=None,
              op0=OP.mult)
        VE.op("tensor_scalar", out=lnB[:], in0=lnA[:], scalar1=-1.0, scalar2=None,
              op0=OP.mult, inc=[s_ve.inc()])
        SE.wait(s_ve)
        SE.op("activation", out=junk[:, 0:1024], in_=xt_view, func=ACT.Square,
              bias=lnB[:], scale=1.0, accum_out=lnC[:], inc=[s_se.inc()])
        VE.wait(s_se)
        VE.op("tensor_scalar", out=lnC[:], in0=lnC[:], scalar1=1.0 / D, scalar2=1e-5,
              op0=OP.mult, op1=OP.add, inc=[s_ve.inc()])
        SE.wait(s_ve)
        SE.op("activation", out=lnB[:], in_=lnC[:], func=ACT.Ln, bias=0.0, scale=1.0)
        SE.op("activation", out=lnC[:], in_=lnB[:], func=ACT.Exp, bias=0.0, scale=-0.5,
              inc=[s_se.inc()])
        VE.wait(s_se)
        VE.op("tensor_scalar", out=out_view, in0=xt_view, scalar1=lnA[:],
              scalar2=lnC[:], op0=OP.subtract, op1=OP.mult, inc=[s_ve.inc()])

    for b in range(BPC):
        tok0 = b * T
        for tt in range(NT):
            xt = xtile_ln(tt % 2)
            SY.wait(s_ve)
            SY.op("dma_start", out=xt, in_=x_in[tok0 + tt * 128: tok0 + (tt + 1) * 128, :],
                  inc=[s_ld.inc(16)])
            VE.wait(s_ld)
            ln_tile(xt, junk[:, 0:1024])
            TE.wait(s_ve)
            TE.wait(s_se)
            for dc_ in range(DC):
                TE.op("transpose", psB[:, dc_ * 128:(dc_ + 1) * 128],
                      junk[:, dc_ * 128:(dc_ + 1) * 128], ident[:],
                      inc=([s_te.inc()] if dc_ == DC - 1 else None))
            VE.wait(s_te)
            VE.op("tensor_copy",
                  out=arenaA[:].rearrange("p (c w) -> p c w", w=2048)[:, :, tt * 128:(tt + 1) * 128],
                  in_=psB[:, 0:1024].rearrange("p (c w) -> p c w", w=128),
                  inc=[s_ve.inc()])
            if DBG2 and b == 0 and tt == 0:
                GP.wait(s_ve)
                GP.op("dma_start", out=dbg_ln[:, 0:1024], in_=junk[:, 0:1024], inc=[s_gd.inc(16)])
                GP.op("dma_start", out=dbg_ln[:, 1024:1025], in_=lnA[:], allow_slow_non_contiguous=True, inc=[s_gd.inc(16)])
                GP.op("dma_start", out=dbg_ln[:, 1025:1026], in_=lnC[:], allow_slow_non_contiguous=True, inc=[s_gd.inc(16)])
                VE.wait(s_gd)
            TE.wait(s_ve)

        VE.wait(s_te)
        VE.wait(s_ld)
        VE.op("memset", vtm[:].rearrange("p (c w) -> p c w", w=65)[:, :, 64:65], 1.0,
              inc=[s_ve.inc()])
        for h in range(H):
            SY.wait(s_se)
            for j, base in enumerate((0, D, 2 * D)):
                SY.op("dma_start",
                      out=wqkvh[:].rearrange("p (c w) -> p c w", w=192)[:, :, j * 64:(j + 1) * 64],
                      in_=wqkvT[:, base + h * 64: base + (h + 1) * 64]
                      .rearrange("(c p) f -> p c f", p=128),
                      inc=[s_ld.inc(16)])
                SY.op("dma_start", out=bq_h[:, j:j + 1],
                      in_=bqkv[base + h * 64: base + (h + 1) * 64, :],
                      inc=[s_ld.inc(16)])
            TE.wait(s_ld)
            SE.wait(s_ld)
            SE.op("activation", out=bq_h[0:64, 3:4], in_=bq_h[0:64, 0:1],
                  func=ACT.Identity, bias=0.0, scale=1.0 / 32, inc=[s_se.inc()])
            for j, dest, bias_col, scl in ((0, qfm_t[0:64, :], 3, 1.0 / 32), (1, kfm_t[0:64, :], 1, 1.0),
                                           (2, vfm[0:64, :], 2, 1.0)):
                TE.wait(s_se)
                TE.wait(s_ve)
                for dc_ in range(DC):
                    for cc_ in range(4):
                        TE.op("matmul", psB[0:64, cc_ * 512:(cc_ + 1) * 512],
                              wqkvh[:, dc_ * 192 + j * 64: dc_ * 192 + (j + 1) * 64],
                              xlnfm(dc_, cc_ * 512, 512),
                              start=(dc_ == 0), stop=(dc_ == DC - 1),
                              inc=([s_te.inc()] if (dc_ == DC - 1 and cc_ == 3) else None))
                SE.wait(s_te)
                SE.op("activation", out=dest, in_=psB[0:64, :], func=ACT.Identity,
                      bias=bq_h[0:64, bias_col:bias_col + 1], scale=scl,
                      inc=[s_se.inc()])
            TE.wait(s_se)
            for jc in range(16):
                TE.op("transpose", psB[:, jc * 64:(jc + 1) * 64],
                      vfm[0:64, jc * 128:(jc + 1) * 128], ident[0:64, 0:64],
                      inc=([s_te.inc()] if jc == 15 else None))
            VE.wait(s_te)
            VE.op("tensor_copy",
                  out=vtm[:].rearrange("p (c w) -> p c w", w=65)[:, :, 0:64],
                  in_=psB[:, 0:1024].rearrange("p (c w) -> p c w", w=64),
                  inc=[s_ve.inc()])
            TE.wait(s_ve)

            NGR = NT // G

            def d0b(gg, gi):
                off = ((gg % 2) * G + gi) * 2048
                return d0g[:, off:off + 2048]

            def emit_scores(gg):
                sp = (gg % 2) * G
                for gi in range(G):
                    tt = gg * G + gi
                    TE.wait(s_se)
                    TE.wait(s_ve)
                    for cc_ in range(4):
                        TE.op("matmul", psA[:, cc_ * 512:(cc_ + 1) * 512],
                              qfm_t[0:64, tt * 128:(tt + 1) * 128],
                              kfm_t[0:64, cc_ * 512:(cc_ + 1) * 512],
                              start=True, stop=True,
                              inc=([s_te.inc()] if cc_ == 3 else None))
                    SE.wait(s_te)
                    SE.wait(s_ve)
                    SE.op("activation", out=d0b(gg, gi), in_=psA[:, :], func=ACT.Identity,
                          bias=0.0, scale=1.0, inc=[s_se.inc()])
                    SE.op("activation", out=junk[:], in_=d0b(gg, gi), func=ACT.Square,
                          bias=0.0, scale=1.0, accum_out=tSQn[:, sp + gi:sp + gi + 1],
                          inc=[s_se.inc()])
                    VE.wait(s_se)
                    VE.op("reduce_sum", tSUMn[:, sp + gi:sp + gi + 1], d0b(gg, gi),
                          axis=AX, inc=[s_ve.inc()])

            def emit_search_epi(gg):
                sp = (gg % 2) * G

                def d0(gi):
                    return d0b(gg, gi)

                VE.op("tensor_copy", out=sepA[:, 0:1], in_=sepB[:, 0:1])
                VE.op("tensor_scalar", out=tMU[:], in0=tSUMn[:, sp:sp + G],
                      scalar1=1.0 / 2048, scalar2=None, op0=OP.mult)
                VE.op("tensor_scalar", out=tS2[:], in0=tSQn[:, sp:sp + G],
                      scalar1=1.0 / 2048, scalar2=None, op0=OP.mult)
                VE.op("tensor_tensor", out=tS1[:], in0=tMU[:], in1=tMU[:], op=OP.mult)
                VE.op("tensor_tensor", out=tS2[:], in0=tS2[:], in1=tS1[:], op=OP.subtract)
                VE.op("tensor_scalar", out=tS2[:], in0=tS2[:], scalar1=1e-12, scalar2=None,
                      op0=OP.max, inc=[s_ve.inc()])
                SE.wait(s_ve)
                SE.op("activation", out=tS1[:], in_=tS2[:], func=ACT.Ln, bias=0.0, scale=1.0)
                SE.op("activation", out=tSIG[:], in_=tS1[:], func=ACT.Exp, bias=0.0,
                      scale=0.5, inc=[s_se.inc()])
                VE.wait(s_se)
                VE.op("scalar_tensor_tensor", out=tLO[:], in0=tSIG[:], scalar=-2.0,
                      in1=tMU[:], op0=OP.mult, op1=OP.add)
                VE.op("scalar_tensor_tensor", out=tHI[:], in0=tSIG[:], scalar=3.0,
                      in1=tMU[:], op0=OP.mult, op1=OP.add)
                VE.op("scalar_tensor_tensor", out=tT[:], in0=tSIG[:], scalar=Z0,
                      in1=tMU[:], op0=OP.mult, op1=OP.add)
                VE.op("memset", tCLO[:], 2048.0)
                VE.op("memset", tCHI[:], 0.0)
                VE.op("memset", tTHIT[:], 0.0)
                VE.op("memset", tFND[:], 0.0)

                def upd_eval():
                    VE.op("tensor_scalar", out=u8g[:], in0=tC[:], scalar1=TOPKF,
                          scalar2=None, op0=OP.is_ge)
                    VE.op("select", out=tLO[:], mask=u8g[:], on_true=tT[:], on_false=tLO[:])
                    VE.op("select", out=tCLO[:], mask=u8g[:], on_true=tC[:], on_false=tCLO[:])
                    VE.op("select", out=tHI[:], mask=u8g[:], on_true=tHI[:], on_false=tT[:])
                    VE.op("select", out=tCHI[:], mask=u8g[:], on_true=tCHI[:], on_false=tC[:])
                    VE.op("tensor_scalar", out=u8g[:], in0=tC[:], scalar1=TOPKF,
                          scalar2=None, op0=OP.is_equal)
                    VE.op("select", out=tTHIT[:], mask=u8g[:], on_true=tT[:], on_false=tTHIT[:])
                    VE.op("tensor_scalar", out=tS1[:], in0=tC[:], scalar1=TOPKF,
                          scalar2=None, op0=OP.is_equal)
                    VE.op("tensor_tensor", out=tFND[:], in0=tFND[:], in1=tS1[:], op=OP.max)

                def clip_t():
                    VE.op("tensor_tensor", out=tS1[:], in0=tHI[:], in1=tLO[:], op=OP.subtract)
                    VE.op("scalar_tensor_tensor", out=tS2[:], in0=tS1[:], scalar=0.002,
                          in1=tLO[:], op0=OP.mult, op1=OP.add)
                    VE.op("scalar_tensor_tensor", out=tS3[:], in0=tS1[:], scalar=-0.002,
                          in1=tHI[:], op0=OP.mult, op1=OP.add)
                    VE.op("tensor_tensor", out=tT[:], in0=tT[:], in1=tS2[:], op=OP.max)
                    VE.op("tensor_tensor", out=tT[:], in0=tT[:], in1=tS3[:], op=OP.min)
                    VE.op("tensor_scalar", out=u8g[:], in0=tFND[:], scalar1=0.5,
                          scalar2=None, op0=OP.is_ge)
                    VE.op("select", out=tT[:], mask=u8g[:], on_true=tLO[:], on_false=tT[:])

                def count_eval(use_scalar):
                    if use_scalar:
                        VE.op("tensor_scalar", out=tNEG[:], in0=tT[:], scalar1=-1.0,
                              scalar2=None, op0=OP.mult, inc=[s_ve.inc()])
                        SE.wait(s_ve)
                        for gi in range(G):
                            SE.op("activation", out=junk[:], in_=d0(gi), func=ACT.Sign,
                                  bias=tNEG[:, gi:gi + 1], scale=1.0,
                                  accum_out=tC[:, gi:gi + 1],
                                  inc=([s_se.inc()] if gi == G - 1 else None))
                        VE.wait(s_se)
                        VE.op("tensor_scalar", out=tC[:], in0=tC[:], scalar1=2048.0,
                              scalar2=0.5, op0=OP.add, op1=OP.mult)
                    else:
                        for gi in range(G):
                            VE.op("tensor_scalar", out=junk[:], in0=d0(gi),
                                  scalar1=tT[:, gi:gi + 1], scalar2=None,
                                  op0=OP.is_ge, op1=OP.add, accum_out=tC[:, gi:gi + 1])

                n_ev = 0
                count_eval(False); n_ev += 1
                upd_eval()
                for _ in range(N_NEWTON - 1):
                    VE.op("tensor_scalar", out=tS1[:], in0=tC[:], scalar1=TOPKF,
                          scalar2=None, op0=OP.subtract)
                    VE.op("tensor_tensor", out=tS1[:], in0=tS1[:], in1=tSIG[:], op=OP.mult)
                    VE.op("scalar_tensor_tensor", out=tT[:], in0=tS1[:], scalar=INV_DENS,
                          in1=tT[:], op0=OP.mult, op1=OP.add)
                    clip_t()
                    count_eval(n_ev % 2 == 1); n_ev += 1
                    upd_eval()
                for _ in range(N_FP):
                    VE.op("tensor_scalar", out=tS1[:], in0=tCLO[:], scalar1=TOPKF,
                          scalar2=None, op0=OP.subtract)
                    VE.op("tensor_tensor", out=tS2[:], in0=tCLO[:], in1=tCHI[:],
                          op=OP.subtract)
                    VE.op("tensor_scalar", out=tS2[:], in0=tS2[:], scalar1=1.0,
                          scalar2=None, op0=OP.max)
                    VE.op("reciprocal", tS2[:], tS2[:])
                    VE.op("tensor_tensor", out=tS1[:], in0=tS1[:], in1=tS2[:], op=OP.mult)
                    VE.op("tensor_tensor", out=tS2[:], in0=tHI[:], in1=tLO[:], op=OP.subtract)
                    VE.op("tensor_tensor", out=tS1[:], in0=tS1[:], in1=tS2[:], op=OP.mult)
                    VE.op("tensor_tensor", out=tT[:], in0=tLO[:], in1=tS1[:], op=OP.add)
                    clip_t()
                    count_eval(n_ev % 2 == 1); n_ev += 1
                    upd_eval()
                for _ in range(N_PROBE):
                    for gi in range(G):
                        VE.op("tensor_scalar", out=msk[:], in0=d0(gi),
                              scalar1=tHI[:, gi:gi + 1], scalar2=None, op0=OP.is_lt)
                        VE.op("select", out=junk[:], mask=msk[:], on_true=d0(gi),
                              on_false=neghuge[:].to_broadcast([128, 2048]))
                        VE.op("reduce_max", tP[:, gi:gi + 1], junk[:], axis=AX)
                    VE.op("tensor_copy", out=tT[:], in_=tP[:])
                    clip_t()
                    count_eval(False); n_ev += 1
                    upd_eval()
                VE.op("tensor_scalar", out=u8g[:], in0=tFND[:], scalar1=0.5,
                      scalar2=None, op0=OP.is_ge)
                VE.op("select", out=tT[:], mask=u8g[:], on_true=tTHIT[:], on_false=tLO[:])
                VE.op("tensor_scalar", out=tS2[:], in0=tT[:], scalar1=104.0,
                      scalar2=None, op0=OP.subtract, inc=[s_ve.inc()])

                for gi in range(G):
                    tt = gg * G + gi
                    VE.op("tensor_scalar", out=msk[:], in0=d0(gi), scalar1=tT[:, gi:gi + 1],
                          scalar2=None, op0=OP.is_ge)
                    VE.op("scalar_tensor_tensor", out=junk[:], in0=d0(gi),
                          scalar=tS2[:, gi:gi + 1], in1=msk[:],
                          op0=OP.subtract, op1=OP.mult, inc=[s_ve.inc()])
                    TE.wait(s_ve)
                    TE.wait(s_se)
                    for jc in range(16):
                        TE.op("transpose", psB[:, jc * 128:(jc + 1) * 128],
                              junk[:, jc * 128:(jc + 1) * 128], ident[:],
                              inc=([s_te.inc()] if jc == 15 else None))
                    SE.wait(s_te)
                    SE.op("activation", out=d0(gi), in_=psB[:, :], func=ACT.Exp,
                          bias=c104n[:], scale=1.0, inc=[s_se.inc()])
                    TE.wait(s_se)
                    avs = psA[:, 0:65] if gi == 0 else psA[:, 512:577]
                    for jc in range(16):
                        TE.op("matmul", avs, d0(gi)[:, jc * 128:(jc + 1) * 128],
                              vtm[:, jc * 65:(jc + 1) * 65],
                              start=(jc == 0), stop=(jc == 15),
                              inc=([s_te.inc()] if jc == 15 else None))
                    VE.wait(s_te)
                    VE.op("reciprocal", recz[:], avs[:, 64:65])
                    VE.op("tensor_copy", out=sepA[:, 0:1], in_=sepB[:, 0:1])
                    VE.op("tensor_scalar", out=attnout(tt, h * 64, 64), in0=avs[:, 0:64],
                          scalar1=recz[:], scalar2=None, op0=OP.mult, inc=[s_ve.inc()])

            for rr in range(NGR + 1):
                if rr < NGR:
                    emit_scores(rr)
                if rr >= 1:
                    emit_search_epi(rr - 1)

        SY.wait(s_ve)
        SY.wait(s_se)
        SY.wait(s_te)
        SY.op("dma_start", out=arenaA[:, 0:8192].rearrange("p (c w) -> p c w", w=1024),
              in_=woutT.rearrange("(c p) o -> p c o", p=128), inc=[s_ld.inc(16)])
        SY.op("dma_start", out=vtm[:, 0:1024], in_=bout_bc, inc=[s_ld.inc(16)])
        for tt in range(NT):
            SY.wait(s_ve)
            SY.op("dma_start", out=xtile_w,
                  in_=x_in[tok0 + tt * 128: tok0 + (tt + 1) * 128, :], inc=[s_ld.inc(16)])
            TE.wait(s_ve)
            for dc_ in range(DC):
                TE.op("transpose", psB[:, dc_ * 128:(dc_ + 1) * 128],
                      attnout(tt, dc_ * 128, 128), ident[:],
                      inc=([s_te.inc()] if dc_ == DC - 1 else None))
            VE.wait(s_te)
            VE.op("tensor_copy", out=junk[:, 0:1024], in_=psB[:, 0:1024], inc=[s_ve.inc()])
            TE.wait(s_ve)
            TE.wait(s_ld)
            for dc_ in range(DC):
                for cc_ in range(2):
                    TE.op("matmul", psA[:, 1024 + cc_ * 512: 1024 + (cc_ + 1) * 512],
                          junk[:, dc_ * 128:(dc_ + 1) * 128],
                          arenaA[:, dc_ * 1024 + cc_ * 512: dc_ * 1024 + (cc_ + 1) * 512],
                          start=(dc_ == 0), stop=(dc_ == DC - 1),
                          inc=([s_te.inc()] if (dc_ == DC - 1 and cc_ == 1) else None))
            VE.wait(s_te)
            VE.wait(s_ld)
            VE.op("tensor_tensor", out=xtile_w, in0=xtile_w, in1=vtm[:, 0:1024], op=OP.add)
            VE.op("scalar_tensor_tensor", out=xtile_w, in0=psA[:, 1024:2048], scalar=0.0,
                  in1=xtile_w, op0=OP.add, op1=OP.add, inc=[s_ve.inc()])
            GP.wait(s_ve)
            GP.op("dma_start", out=xmid_h[tok0 + tt * 128: tok0 + (tt + 1) * 128, :],
                  in_=xtile_w, inc=[s_gd.inc(16)])
            TE.wait(s_ve)
            SY.wait(s_gd)

    VE.wait(s_gd)
    VE.op("memset", zfill[:, :], 0.0, inc=[s_ve.inc()])
    GP.wait(s_ve)
    nfill = (REGIN_ROWS + 31) // 32
    for i in range(nfill):
        lo = i * 32
        n = min(32, REGIN_ROWS - lo)
        GP.op("dma_start", out=regin_h[lo:lo + n, :], in_=zfill[0:n, :], inc=[s_gd.inc(16)])
    GP.op("dma_start", out=regout_h[ZROW:ZROW + 1, :], in_=zfill[0:1, :], inc=[s_gd.inc(16)])

    SY.op("dma_start", out=wqkvh[:].rearrange("p (c w) -> p c w", w=192)[:, :, 184:192],
          in_=wrouter.rearrange("(c p) e -> p c e", p=128), inc=[s_ld.inc(16)])
    for gt in range(NTC):
        xt = xtile_ln(gt % 2)
        SY.wait(s_ve)
        SY.wait(s_gd)
        SY.op("dma_start", out=xt, in_=xmid_h[gt * 128:(gt + 1) * 128, :], inc=[s_ld.inc(16)])
        VE.wait(s_ld)
        ln_tile(xt, junk[:, 0:1024])
        GP.wait(s_ve)
        GP.op("dma_start", out=xln2_h[gt * 128:(gt + 1) * 128, :], in_=junk[:, 0:1024],
              inc=[s_gd.inc(16)])
        TE.wait(s_ve)
        TE.wait(s_se)
        for dc_ in range(DC):
            TE.op("transpose", psB[:, dc_ * 128:(dc_ + 1) * 128],
                  junk[:, dc_ * 128:(dc_ + 1) * 128], ident[:],
                  inc=([s_te.inc()] if dc_ == DC - 1 else None))
        VE.wait(s_te)
        VE.wait(s_gd)
        VE.op("tensor_copy", out=xfm2[:], in_=psB[:, 0:1024], inc=[s_ve.inc()])
        TE.wait(s_ve)
        TE.wait(s_ld)
        for dc_ in range(DC):
            TE.op("matmul", psA[:, 0:8], xfm2[:, dc_ * 128:(dc_ + 1) * 128],
                  wqkvh[:, dc_ * 192 + 184: dc_ * 192 + 192],
                  start=(dc_ == 0), stop=(dc_ == DC - 1),
                  inc=([s_te.inc()] if dc_ == DC - 1 else None))
        VE.wait(s_te)
        VE.op("tensor_copy", out=logits[:, gt * 8:(gt + 1) * 8], in_=psA[:, 0:8],
              inc=[s_ve.inc()])
        TE.wait(s_ve)

    for gt in range(NTC):
        lg = logits[:, gt * 8:(gt + 1) * 8]
        VE.op("reduce_max", rm1[:], lg, axis=AX)
        VE.op("tensor_scalar", out=rt1[:], in0=lg, scalar1=rm1[:], scalar2=None,
              op0=OP.is_equal)
        VE.op("scalar_tensor_tensor", out=rt2[:], in0=rt1[:], scalar=-1e30, in1=lg,
              op0=OP.mult, op1=OP.add)
        VE.op("reduce_max", rm2[:], rt2[:], axis=AX)
        VE.op("tensor_tensor", out=rt1[:], in0=rt1[:], in1=iota8bc[:], op=OP.mult)
        VE.op("reduce_sum", e1c[:, gt:gt + 1], rt1[:], axis=AX)
        VE.op("tensor_scalar", out=rt1[:], in0=rt2[:], scalar1=rm2[:], scalar2=None,
              op0=OP.is_equal)
        VE.op("tensor_tensor", out=rt1[:], in0=rt1[:], in1=iota8bc[:], op=OP.mult)
        VE.op("reduce_sum", e2c[:, gt:gt + 1], rt1[:], axis=AX)
        VE.op("tensor_scalar", out=rm1[:], in0=rm1[:], scalar1=-1.0, scalar2=None,
              op0=OP.mult, inc=[s_ve.inc()])
        SE.wait(s_ve)
        SE.op("activation", out=rt2[:], in_=lg, func=ACT.Exp, bias=rm1[:], scale=1.0,
              accum_out=rz8[:])
        SE.op("activation", out=re2[:], in_=rm2[:], func=ACT.Exp, bias=rm1[:], scale=1.0,
              inc=[s_se.inc()])
        VE.wait(s_se)
        VE.op("reciprocal", g1c[:, gt:gt + 1], rz8[:])
        VE.op("tensor_tensor", out=g2c[:, gt:gt + 1], in0=re2[:], in1=g1c[:, gt:gt + 1],
              op=OP.mult, inc=[s_ve.inc()])

    for (col, row) in ((e1c, rowsA), (e2c, rowsB)):
        for half in range(2):
            TE.wait(s_ve)
            for i in range(16):
                gt = half * 16 + i
                TE.op("transpose", psB[0:1, i * 128:(i + 1) * 128], col[:, gt:gt + 1],
                      ident[:], inc=([s_te.inc()] if i == 15 else None))
            VE.wait(s_te)
            VE.op("tensor_copy", out=row[:, half * 2048:(half + 1) * 2048],
                  in_=psB[0:1, 0:2048], inc=[s_ve.inc()])
            TE.wait(s_ve)
    GP.wait(s_ve)
    for p in range(8):
        GP.op("dma_start", out=e1b[p:p + 1, :], in_=rowsA[:, :], inc=[s_gd.inc(16)])
        GP.op("dma_start", out=e2b[p:p + 1, :], in_=rowsB[:, :], inc=[s_gd.inc(16)])
    VE.wait(s_gd)
    VE.op("tensor_scalar", out=ind0[:, :], in0=e1b[:, :], scalar1=iotaP[96:104, :],
          scalar2=None, op0=OP.is_equal)
    VE.op("tensor_scalar", out=ind1[:, :], in0=e2b[:, :], scalar1=iotaP[96:104, :],
          scalar2=None, op0=OP.is_equal)
    VE.op("tensor_tensor_scan", out=cum0[:, :], data0=ind0[:, :], data1=ind0[:, :],
          initial=0.0, op0=OP.add, op1=OP.max)
    VE.op("tensor_tensor_scan", out=cum1[:, :], data0=ind1[:, :], data1=ind1[:, :],
          initial=0.0, op0=OP.add, op1=OP.max)
    VE.op("tensor_copy", out=cA0[:], in_=cum0[:, T - 1: T])
    VE.op("tensor_copy", out=cA1[:], in_=cum1[:, T - 1: T])
    VE.op("tensor_copy", out=cB0[:], in_=cum0[:, TPC - 1: TPC])
    VE.op("tensor_copy", out=cB1[:], in_=cum1[:, TPC - 1: TPC])
    VE.op("tensor_copy", out=sepA[:, 0:1], in_=sepB[:, 0:1])
    VE.op("tensor_tensor", out=cB0[:], in0=cB0[:], in1=cA0[:], op=OP.subtract)
    VE.op("tensor_tensor", out=cB1[:], in0=cB1[:], in1=cA1[:], op=OP.subtract,
          inc=[s_ve.inc()])
    TE.wait(s_ve)
    TE.op("transpose", psB[0:1, 0:8], cA0[:], ident[0:8, 0:8])
    TE.op("transpose", psB[0:1, 8:16], cB0[:], ident[0:8, 0:8])
    TE.op("transpose", psB[0:1, 16:24], cA1[:], ident[0:8, 0:8])
    TE.op("transpose", psB[0:1, 24:32], cB1[:], ident[0:8, 0:8],
          inc=[s_te.inc()])
    VE.wait(s_te)
    VE.op("tensor_copy", out=cnt16[:, :], in_=psB[0:1, 0:32], inc=[s_ve.inc()])
    TE.wait(s_ve)
    GP.wait(s_ve)
    for p in range(8):
        GP.op("dma_start", out=allin[p:p + 1, :], in_=cnt16[:, :], inc=[s_gd.inc(16)])
    VE.wait(s_gd)
    VE.op("tensor_scalar", out=allin[:], in0=allin[:], scalar1=rowmask[:], scalar2=None,
          op0=OP.mult, inc=[s_ve.inc()])
    GP.wait(s_ve)
    GP.op("dma_start", out=ccin_h.ap(), in_=allin[:], inc=[s_gd.inc(16)])
    GP.wait(s_gd)
    GP.op("collective_compute", "AllReduce", OP.add, replica_groups=[list(range(8))],
          ins=[ccin_h.ap().opt()], outs=[ccout_h.ap().opt()], inc=[s_cc.inc()])
    GP.wait(s_cc)
    GP.op("dma_start", out=allout[:], in_=ccout_h.ap(), inc=[s_gd.inc(16)])

    VE.op("tensor_tensor", out=cum0[:, :], in0=cum0[:, :], in1=ind0[:, :], op=OP.mult)
    VE.op("tensor_tensor", out=cum1[:, :], in0=cum1[:, :], in1=ind1[:, :], op=OP.mult,
          inc=[s_ve.inc()])
    TE.wait(s_ve)

    def row_reduce_and_transpose(src8, rowbuf, colbuf):
        for half in range(2):
            for cc_ in range(4):
                TE.op("matmul", psB[0:1, cc_ * 512:(cc_ + 1) * 512], ones8[:],
                      src8[:, half * 2048 + cc_ * 512: half * 2048 + (cc_ + 1) * 512],
                      start=True, stop=True,
                      inc=([s_te.inc()] if cc_ == 3 else None))
            VE.wait(s_te)
            VE.op("tensor_copy", out=rowbuf[:, half * 2048:(half + 1) * 2048],
                  in_=psB[0:1, 0:2048], inc=[s_ve.inc()])
            TE.wait(s_ve)
        for half in range(2):
            for i in range(16):
                TE.op("transpose", psB[:, i:i + 1],
                      rowbuf[:, (half * 16 + i) * 128:(half * 16 + i + 1) * 128],
                      ident[32:33, 32:33], inc=([s_te.inc()] if i == 15 else None))
            VE.wait(s_te)
            VE.op("tensor_copy", out=colbuf[:, half * 16:(half + 1) * 16], in_=psB[:, 0:16],
                  inc=[s_ve.inc()])
            TE.wait(s_ve)

    row_reduce_and_transpose(cum0, rowsC, r0c)
    row_reduce_and_transpose(cum1, rowsD, r1c)

    VE.wait(s_gd)
    VE.op("engine_nop", inc=[s_ve.inc()])
    TE.wait(s_ve)
    TE.op("matmul", psB[0:1, 0:32], maskprev[:], allout[:], start=True, stop=True)
    TE.op("matmul", psB[0:1, 32:64], maskgrp[:], allout[:], start=True, stop=True,
          inc=[s_te.inc()])
    VE.wait(s_te)
    VE.op("tensor_copy", out=offp[:, :], in_=psB[0:1, 0:32])
    VE.op("tensor_copy", out=offg[:, :], in_=psB[0:1, 32:64], inc=[s_ve.inc()])
    TE.wait(s_ve)
    pass
    for i4, rowv in enumerate((offp, offg)):
        for j4 in range(4):
            TE.op("transpose", psB[0:8, i4 * 4 + j4: i4 * 4 + j4 + 1],
                  rowv[0:1, j4 * 8:(j4 + 1) * 8], ident[0:1, 0:1],
                  inc=([s_te.inc()] if (i4 == 1 and j4 == 3) else None))
    VE.wait(s_te)
    VE.op("tensor_copy", out=offA0[:], in_=psB[0:8, 0:1])
    VE.op("tensor_copy", out=offB0[:], in_=psB[0:8, 1:2])
    VE.op("tensor_copy", out=offA1[:], in_=psB[0:8, 2:3])
    VE.op("tensor_copy", out=offB1[:], in_=psB[0:8, 3:4])
    VE.op("tensor_copy", out=tot0c[:], in_=psB[0:8, 4:5])
    VE.op("tensor_copy", out=tmp8[:], in_=psB[0:8, 6:7])
    VE.op("tensor_tensor", out=offB0[:], in0=offB0[:], in1=tot0c[:], op=OP.add)
    VE.op("tensor_tensor", out=offB0[:], in0=offB0[:], in1=cA0[:], op=OP.subtract)
    VE.op("tensor_copy", out=tmp8b[:], in_=psB[0:8, 5:6])
    VE.op("tensor_tensor", out=tot0c[:], in0=tot0c[:], in1=tmp8b[:], op=OP.add)
    VE.op("tensor_tensor", out=offA1[:], in0=offA1[:], in1=tot0c[:], op=OP.add)
    VE.op("tensor_tensor", out=offB1[:], in0=offB1[:], in1=tot0c[:], op=OP.add)
    VE.op("tensor_tensor", out=offB1[:], in0=offB1[:], in1=tmp8[:], op=OP.add)
    VE.op("tensor_tensor", out=offB1[:], in0=offB1[:], in1=cA1[:], op=OP.subtract,
          inc=[s_ve.inc()])
    for (ind, ocA, ocB, rowbuf, colbuf) in ((ind0, offA0, offB0, rowsA, of0c),
                                            (ind1, offA1, offB1, rowsB, of1c)):
        VE.op("tensor_scalar", out=ind[:, 0:T], in0=ind[:, 0:T], scalar1=ocA[:],
              scalar2=None, op0=OP.mult)
        VE.op("tensor_scalar", out=ind[:, T:TPC], in0=ind[:, T:TPC], scalar1=ocB[:],
              scalar2=None, op0=OP.mult, inc=[s_ve.inc()])
        TE.wait(s_ve)
        row_reduce_and_transpose(ind, rowbuf, colbuf)

    for (rcol, ocol, ecol, gcol, gscol, scol, gicol, cap, rstr, base) in (
            (r0c, of0c, e1c, g1c, gs0, sidx0, gidx0, CAP0, R0STR, 0),
            (r1c, of1c, e2c, g2c, gs1, sidx1, gidx1, CAP1, R1STR, K1BASE)):
        VE.op("tensor_tensor", out=ocol[:], in0=ocol[:], in1=rcol[:], op=OP.add)
        VE.op("tensor_scalar", out=ocol[:], in0=ocol[:], scalar1=1.0, scalar2=None,
              op0=OP.subtract)
        VE.op("tensor_scalar", out=gscol[:], in0=ocol[:], scalar1=float(CCAP),
              scalar2=None, op0=OP.is_lt)
        VE.op("tensor_scalar", out=ocol[:], in0=rcol[:], scalar1=float(cap) + 0.5,
              scalar2=None, op0=OP.is_lt)
        VE.op("tensor_tensor", out=gscol[:], in0=gscol[:], in1=ocol[:], op=OP.mult)
        VE.op("tensor_tensor", out=gscol[:], in0=gscol[:], in1=gcol[:], op=OP.mult)
        VE.op("tensor_scalar", out=ocol[:], in0=rcol[:], scalar1=1.0, scalar2=float(cap),
              op0=OP.subtract, op1=OP.min)
        VE.op("scalar_tensor_tensor", out=ocol[:], in0=ecol[:], scalar=float(rstr),
              in1=ocol[:], op0=OP.mult, op1=OP.add)
        VE.op("tensor_scalar", out=ocol[:], in0=ocol[:], scalar1=float(base),
              scalar2=None, op0=OP.add)
        VE.op("tensor_copy", out=scol[:], in_=ocol[:])
        VE.op("tensor_scalar", out=aflag[:], in0=gscol[:], scalar1=0.0, scalar2=None,
              op0=OP.is_gt)
        VE.op("tensor_tensor", out=ocol[:], in0=ocol[:], in1=aflag[:], op=OP.mult)
        VE.op("tensor_scalar", out=aflag[:], in0=aflag[:], scalar1=-float(ZROW),
              scalar2=float(ZROW), op0=OP.mult, op1=OP.add)
        VE.op("tensor_tensor", out=ocol[:], in0=ocol[:], in1=aflag[:], op=OP.add)
        VE.op("tensor_copy", out=gicol[:], in_=ocol[:], inc=[s_ve.inc()])

    GP.wait(s_ve)
    GP.wait(s_gd)
    for gt in range(NTC):
        VE.wait(s_gd)
        VE.op("tensor_copy", out=itmp0[:], in_=sidx0[:, gt:gt + 1])
        VE.op("tensor_copy", out=itmp1[:], in_=sidx1[:, gt:gt + 1], inc=[s_ve.inc()])
        GP.wait(s_ve)
        GP.op("dma_start", out=gtile(gt % 2)[:, :], in_=xln2_h[gt * 128:(gt + 1) * 128, :],
              inc=[s_gd.inc(16)])
        GP.wait(s_gd)
        GP.op("indirect_dma_start", out=regin_h,
              out_offset=IndirectOffsetOnAxis(itmp0[:], 0),
              in_=gtile(gt % 2)[:, :], in_offset=None, inc=[s_gd.inc(16)])
        GP.op("indirect_dma_start", out=regin_h,
              out_offset=IndirectOffsetOnAxis(itmp1[:], 0),
              in_=gtile(gt % 2)[:, :], in_offset=None, inc=[s_gd.inc(16)])
        GP.wait(s_gd)

    for e in range(E):
        SY.wait(s_ve)
        SY.wait(s_gd)
        for i in range(3):
            SY.op("dma_start", out=gtile(i)[:, :],
                  in_=regin_h[e * R0STR + i * 128: e * R0STR + (i + 1) * 128, :],
                  inc=[s_ld.inc(16)])
        SY.op("dma_start", out=gtile(3)[:, :],
              in_=regin_h[K1BASE + e * R1STR: K1BASE + e * R1STR + 128, :],
              inc=[s_ld.inc(16)])
        TE.wait(s_ld)
        TE.wait(s_se)
        TE.wait(s_ve)
        for i in range(4):
            for dc_ in range(DC):
                TE.op("transpose", psB[:, dc_ * 128:(dc_ + 1) * 128],
                      gtile(i)[:, dc_ * 128:(dc_ + 1) * 128], ident[:],
                      inc=([s_te.inc()] if dc_ == DC - 1 else None))
            VE.wait(s_te)
            if i < 3:
                VE.op("tensor_copy",
                      out=d0g[:, 0:3072].rearrange("p (c w) -> p c w", w=384)[:, :, i * 128:(i + 1) * 128],
                      in_=psB[:, 0:1024].rearrange("p (c w) -> p c w", w=128),
                      inc=[s_ve.inc()])
            else:
                VE.op("tensor_copy",
                      out=d0g[:, GFM1B:GFM1B + 1024].rearrange("p (c w) -> p c w", w=128),
                      in_=psB[:, 0:1024].rearrange("p (c w) -> p c w", w=128),
                      inc=[s_ve.inc()])
            TE.wait(s_ve)
        for fc in range(32):
            pp = fc % 2
            if fc >= 2:
                SY.wait(s_te)
            SY.op("dma_start", out=wfbuf(pp).rearrange("p (c w) -> p c w", w=128),
                  in_=wi_in[e * D:(e + 1) * D, fc * 128:(fc + 1) * 128]
                  .rearrange("(c p) f -> p c f", p=128), inc=[s_ld.inc(16)])
            TE.wait(s_ld)
            TE.wait(s_se)
            for dc_ in range(DC):
                TE.op("matmul", psA[:, 0:384], wfbuf(pp)[:, dc_ * 128:(dc_ + 1) * 128],
                      gfm0(dc_, 0, 384), start=(dc_ == 0), stop=(dc_ == DC - 1))
            for dc_ in range(DC):
                TE.op("matmul", psA[:, 512:640], wfbuf(pp)[:, dc_ * 128:(dc_ + 1) * 128],
                      gfm1(dc_, 0, 128), start=(dc_ == 0), stop=(dc_ == DC - 1),
                      inc=([s_te.inc()] if dc_ == DC - 1 else None))
            SE.wait(s_te)
            SE.op("activation", out=hT0(fc, 0, 384), in_=psA[:, 0:384], func=ACT.Relu,
                  bias=0.0, scale=1.0)
            SE.op("activation", out=hT1(fc, 0, 128), in_=psA[:, 512:640], func=ACT.Relu,
                  bias=0.0, scale=1.0, inc=[s_se.inc()])
        if DBG2 and e == 0:
            GP.wait(s_se)
            GP.op("dma_start", out=dbg_ht, in_=arenaA[:, :], inc=[s_gd.inc(16)])
            TE.wait(s_gd)
        accs = (psA[:, 0:1024], psA[:, 1024:2048], psB[:, 0:1024], psB[:, 1024:2048])
        for fc in range(32):
            pp = fc % 2
            if fc >= 2:
                SY.wait(s_te)
            SY.op("dma_start", out=wobuf(pp)[:, :],
                  in_=wo_in[e * FF + fc * 128: e * FF + (fc + 1) * 128, :],
                  inc=[s_ld.inc(16)])
            TE.wait(s_ld)
            if fc == 0:
                TE.wait(s_se)
                TE.wait(s_ve)
            for ts_ in range(4):
                src = hT0(fc, ts_ * 128, 128) if ts_ < 3 else hT1(fc, 0, 128)
                for cc_ in range(2):
                    TE.op("matmul", accs[ts_][:, cc_ * 512:(cc_ + 1) * 512],
                          src, wobuf(pp)[:, cc_ * 512:(cc_ + 1) * 512],
                          start=(fc == 0), stop=(fc == 31),
                          inc=([s_te.inc()] if (ts_ == 3 and cc_ == 1) else None))
        VE.wait(s_te)
        VE.wait(s_gd)
        for ts_ in range(4):
            VE.op("tensor_copy", out=outsb(ts_)[:, :], in_=accs[ts_],
                  inc=([s_ve.inc()] if ts_ == 3 else None))
        GP.wait(s_ve)
        for i in range(3):
            GP.op("dma_start",
                  out=regout_h[e * R0STR + i * 128: e * R0STR + (i + 1) * 128, :],
                  in_=outsb(i)[:, :], inc=[s_gd.inc(16)])
        GP.op("dma_start", out=regout_h[K1BASE + e * R1STR: K1BASE + e * R1STR + 128, :],
              in_=outsb(3)[:, :], inc=[s_gd.inc(16)])

    GP.wait(s_gd)
    for gt in range(NTC):
        SY.wait(s_ve)
        SY.wait(s_gd)
        SY.op("dma_start", out=gtile(2)[:, :], in_=xmid_h[gt * 128:(gt + 1) * 128, :],
              inc=[s_ld.inc(16)])
        VE.op("tensor_copy", out=itmp0[:], in_=gidx0[:, gt:gt + 1])
        VE.op("tensor_copy", out=itmp1[:], in_=gidx1[:, gt:gt + 1], inc=[s_ve.inc()])
        GP.wait(s_ve)
        GP.op("indirect_dma_start", out=gtile(0)[:, :], out_offset=None,
              in_=regout_h, in_offset=IndirectOffsetOnAxis(itmp0[:], 0),
              inc=[s_gd.inc(16)])
        GP.op("indirect_dma_start", out=gtile(1)[:, :], out_offset=None,
              in_=regout_h, in_offset=IndirectOffsetOnAxis(itmp1[:], 0),
              inc=[s_gd.inc(16)])
        VE.wait(s_ld)
        VE.wait(s_gd)
        VE.op("tensor_scalar", out=gtile(0)[:, :], in0=gtile(0)[:, :],
              scalar1=gs0[:, gt:gt + 1], scalar2=None, op0=OP.mult)
        VE.op("tensor_scalar", out=gtile(1)[:, :], in0=gtile(1)[:, :],
              scalar1=gs1[:, gt:gt + 1], scalar2=None, op0=OP.mult)
        VE.op("tensor_tensor", out=gtile(2)[:, :], in0=gtile(2)[:, :], in1=gtile(0)[:, :],
              op=OP.add)
        VE.op("tensor_tensor", out=gtile(2)[:, :], in0=gtile(2)[:, :], in1=gtile(1)[:, :],
              op=OP.add, inc=[s_ve.inc()])
        GP.wait(s_ve)
        GP.op("dma_start", out=y_out[gt * 128:(gt + 1) * 128, :], in_=gtile(2)[:, :],
              inc=[s_gd.inc(16)])
    if DBG:
        GP.op("dma_start", out=dbg_xmid, in_=xmid_h, inc=[s_gd.inc(16)])
        GP.op("dma_start", out=dbg_xln2, in_=xln2_h, inc=[s_gd.inc(16)])
        GP.op("dma_start", out=dbg_reg, in_=regin_h, inc=[s_gd.inc(16)])
        GP.op("dma_start", out=dbg_rout, in_=regout_h, inc=[s_gd.inc(16)])
        for k, tile_ in enumerate((e1c, e2c, g1c, g2c, r0c, r1c, of0c, of1c, gs0, gs1)):
            GP.op("dma_start", out=dbg_cols[:, k * NTC:(k + 1) * NTC], in_=tile_[:],
                  inc=[s_gd.inc(16)])
        GP.op("dma_start", out=dbg_all, in_=allout[:], inc=[s_gd.inc(16)])
    GP.wait(s_gd)

    with nc.Block() as block:
        @block.sync
        def _(eng):
            r.replay("sync", eng)

        @block.tensor
        def _(eng):
            r.replay("tensor", eng)

        @block.vector
        def _(eng):
            r.replay("vector", eng)

        @block.scalar
        def _(eng):
            r.replay("scalar", eng)

        @block.gpsimd
        def _(eng):
            r.replay("gpsimd", eng)

    ctx.close()
    return nc


_NC_CACHE = [None]


def make_in_maps(L0_attn_x, L0_attn_W_qkv, L0_attn_b_qkv, L0_attn_W_out, L0_attn_b_out,
                 L0_ffn_w_router, L0_ffn_wi, L0_ffn_wo):
    x = np.ascontiguousarray(np.asarray(L0_attn_x, dtype=np.float32))
    wqkvT = np.ascontiguousarray(np.asarray(L0_attn_W_qkv, dtype=np.float32).T)
    bqkv = np.ascontiguousarray(np.asarray(L0_attn_b_qkv, dtype=np.float32).reshape(3 * D, 1))
    woutT = np.ascontiguousarray(np.asarray(L0_attn_W_out, dtype=np.float32).T)
    bout_bc = np.ascontiguousarray(
        np.broadcast_to(np.asarray(L0_attn_b_out, dtype=np.float32)[None, :], (128, D)))
    wrouter = np.ascontiguousarray(np.asarray(L0_ffn_w_router, dtype=np.float32))
    wi = np.ascontiguousarray(np.asarray(L0_ffn_wi, dtype=np.float32).reshape(E * D, FF))
    wo = np.ascontiguousarray(np.asarray(L0_ffn_wo, dtype=np.float32).reshape(E * FF, D))
    ident = np.eye(128, dtype=np.float32)
    iota8bc = np.ascontiguousarray(
        np.broadcast_to(np.arange(E, dtype=np.float32)[None, :], (128, E)))
    iota8col = np.arange(E, dtype=np.float32).reshape(E, 1)
    ones8 = np.ones((E, 1), dtype=np.float32)

    in_maps = []
    for c in range(NCORES):
        rank = c % 4
        grp = c // 4
        rowmask = np.zeros((E, 1), np.float32)
        rowmask[c, 0] = 1.0
        maskprev = np.zeros((E, 1), np.float32)
        maskprev[grp * 4: grp * 4 + rank, 0] = 1.0
        maskgrp = np.zeros((E, 1), np.float32)
        maskgrp[grp * 4: grp * 4 + 4, 0] = 1.0
        in_maps.append({
            "x_in": np.ascontiguousarray(
                np.stack([x[8 * grp + rank], x[8 * grp + rank + 4]]).reshape(TPC, D)),
            "wqkvT": wqkvT, "bqkv": bqkv, "woutT": woutT, "bout_bc": bout_bc,
            "wrouter": wrouter, "wi_in": wi, "wo_in": wo,
            "ident_in": ident, "iota8bc_in": iota8bc, "iota8col_in": iota8col,
            "ones8_in": ones8, "rowmask_in": rowmask, "maskprev_in": maskprev,
            "maskgrp_in": maskgrp,
        })
    return in_maps


def kernel(**inputs):
    in_maps = make_in_maps(**inputs)
    if _NC_CACHE[0] is None:
        _NC_CACHE[0] = build_nc()
    res = run_bass_kernel_spmd(_NC_CACHE[0], in_maps, core_ids=list(range(NCORES)))
    out = np.empty((B, T, D), np.float32)
    for c in range(NCORES):
        grp, rank = c // 4, c % 4
        yo = res.results[c]["y_out"].reshape(2, T, D)
        out[8 * grp + rank] = yo[0]
        out[8 * grp + rank + 4] = yo[1]
    return out


# version 04
# speedup vs baseline: 1.0972x; 1.0060x over previous
import os
import numpy as np
import concourse.bass as bass
import concourse.mybir as mybir
from concourse.bass_utils import run_bass_kernel_spmd
from concourse.bass import IndirectOffsetOnAxis

F32 = mybir.dt.float32
BF16 = mybir.dt.bfloat16
I32 = mybir.dt.int32
U8 = mybir.dt.uint8
OP = mybir.AluOpType
ACT = mybir.ActivationFunctionType
AX = mybir.AxisListType.X

B, T, D, H, HD = 16, 2048, 1024, 16, 64
TOPK = 614
E, FF = 8, 4096
CCAP = 640
NCORES = 8
BPC = 2
TPC = BPC * T
NT = T // 128
NTC = TPC // 128
DC = D // 128
CAP0, CAP1 = 384, 128
R0STR, R1STR = CAP0 + 1, CAP1 + 1
K1BASE = E * R0STR
REGIN_ROWS = E * R0STR + E * R1STR
ZROW = REGIN_ROWS
N_NEWTON = 3
N_FP = 6
N_PROBE = 3
G = 2
Z0 = 0.52533
INV_DENS = 1.0 / (2048 * 0.34765)
TOPKF = float(TOPK)


class Sem:
    def __init__(self, handle):
        self.h = handle
        self.n = 0

    def inc(self, by=1):
        return (self, by)


def _ap_tensor_names(v):
    names = set()
    if v is None or isinstance(v, (int, float, str, bool)):
        return names
    t = getattr(v, "tensor", None)
    if t is not None and hasattr(t, "name"):
        names.add(t.name)
    return names


class EngRec:
    def __init__(self, rec, name):
        self.rec = rec
        self.name = name
        self.last_outs = set()
        self.last_wide_safe = False
        self.sep = None

    def wait(self, sem, val=None):
        v = sem.n if val is None else val
        self.rec.ops[self.name].append(("wait", sem.h, v))

    def op(self, method, *args, inc=None, **kwargs):
        if self.name == "vector" and self.last_outs and not self.last_wide_safe:
            ins = set()
            for v in args[1:]:
                ins |= _ap_tensor_names(v)
            for k, v in kwargs.items():
                if k not in ("out", "accum_out"):
                    ins |= _ap_tensor_names(v)
            if method != "memset" and (ins & self.last_outs) and self.sep is not None:
                self.rec.ops[self.name].append(
                    ("op", "tensor_copy", (), {"out": self.sep[0], "in_": self.sep[1]}, []))
        spec = []
        if inc is not None:
            for s, by in (inc if isinstance(inc, list) else [inc]):
                s.n += by
                spec.append((s.h, by))
        self.rec.ops[self.name].append(("op", method, args, kwargs, spec))
        outs = set()
        wide = False
        if args:
            outs |= _ap_tensor_names(args[0])
            try:
                wide = args[0].free_size() >= 512
            except Exception:
                wide = False
        ov = kwargs.get("out")
        if ov is not None:
            outs |= _ap_tensor_names(ov)
            try:
                wide = ov.free_size() >= 512
            except Exception:
                pass
        acc = kwargs.get("accum_out")
        if acc is not None:
            outs |= _ap_tensor_names(acc)
        self.last_wide_safe = (wide and acc is None and method in
                               ("tensor_copy", "tensor_scalar", "tensor_tensor",
                                "scalar_tensor_tensor", "select"))
        self.last_outs = outs


class Rec:
    def __init__(self):
        self.ops = {k: [] for k in ("sync", "tensor", "vector", "scalar", "gpsimd")}
        for k in self.ops:
            setattr(self, k, EngRec(self, k))

    def replay(self, name, eng):
        for item in self.ops[name]:
            if item[0] == "wait":
                _, h, v = item
                eng.wait_ge(h, v)
            else:
                _, method, args, kwargs, incs = item
                inst = getattr(eng, method)(*args, **kwargs)
                for h, by in incs:
                    inst = inst.then_inc(h, by)


def build_nc():
    nc = bass.Bass("TRN2", target_bir_lowering=False)

    x_in = nc.dram_tensor("x_in", [TPC, D], F32, kind="ExternalInput").ap()
    wqkvT = nc.dram_tensor("wqkvT", [D, 3 * D], F32, kind="ExternalInput").ap()
    bqkv = nc.dram_tensor("bqkv", [3 * D, 1], F32, kind="ExternalInput").ap()
    woutT = nc.dram_tensor("woutT", [D, D], F32, kind="ExternalInput").ap()
    bout_bc = nc.dram_tensor("bout_bc", [128, D], F32, kind="ExternalInput").ap()
    wrouter = nc.dram_tensor("wrouter", [D, E], F32, kind="ExternalInput").ap()
    wi_in = nc.dram_tensor("wi_in", [E * D, FF], F32, kind="ExternalInput").ap()
    wo_in = nc.dram_tensor("wo_in", [E * FF, D], F32, kind="ExternalInput").ap()
    ident_in = nc.dram_tensor("ident_in", [128, 128], F32, kind="ExternalInput").ap()
    iota8bc_in = nc.dram_tensor("iota8bc_in", [128, E], F32, kind="ExternalInput").ap()
    iota8col_in = nc.dram_tensor("iota8col_in", [E, 1], F32, kind="ExternalInput").ap()
    ones8_in = nc.dram_tensor("ones8_in", [E, 1], F32, kind="ExternalInput").ap()
    rowmask_in = nc.dram_tensor("rowmask_in", [E, 1], F32, kind="ExternalInput").ap()
    maskprev_in = nc.dram_tensor("maskprev_in", [E, 1], F32, kind="ExternalInput").ap()
    maskgrp_in = nc.dram_tensor("maskgrp_in", [E, 1], F32, kind="ExternalInput").ap()

    y_out = nc.dram_tensor("y_out", [TPC, D], F32, kind="ExternalOutput").ap()
    _dbg_env = os.environ.get("MK_DEBUG", "")
    DBG = _dbg_env in ("1", "2")
    DBG2 = _dbg_env == "2"
    if DBG:
        dbg_xmid = nc.dram_tensor("dbg_xmid", [TPC, D], F32, kind="ExternalOutput").ap()
        dbg_xln2 = nc.dram_tensor("dbg_xln2", [TPC, D], F32, kind="ExternalOutput").ap()
        dbg_cols = nc.dram_tensor("dbg_cols", [128, 10 * NTC], F32, kind="ExternalOutput").ap()
        dbg_all = nc.dram_tensor("dbg_all", [8, 32], F32, kind="ExternalOutput").ap()
        dbg_reg = nc.dram_tensor("dbg_reg", [REGIN_ROWS, D], F32, kind="ExternalOutput").ap()
        dbg_rout = nc.dram_tensor("dbg_rout", [REGIN_ROWS + 1, D], F32, kind="ExternalOutput").ap()
        dbg_ht = nc.dram_tensor("dbg_ht", [128, 16384], F32, kind="ExternalOutput").ap()
        dbg_sc = nc.dram_tensor("dbg_sc", [128, 2 * 2048], F32, kind="ExternalOutput").ap()
        dbg_w = nc.dram_tensor("dbg_w", [128, 2 * 2048], F32, kind="ExternalOutput").ap()
        dbg_st = nc.dram_tensor("dbg_st", [128, 8 * G], F32, kind="ExternalOutput").ap()
        dbg_kq = nc.dram_tensor("dbg_kq", [64, 3 * 2048], F32, kind="ExternalOutput").ap()
        dbg_xfm = nc.dram_tensor("dbg_xfm", [128, 16384], F32, kind="ExternalOutput").ap()
        dbg_ln = nc.dram_tensor("dbg_ln", [128, 1032], F32, kind="ExternalOutput").ap()

    xmid_h = nc.dram_tensor("xmid_h", [TPC, D], F32, kind="Internal").ap()
    xln2_h = nc.dram_tensor("xln2_h", [TPC, D], F32, kind="Internal").ap()
    regin_h = nc.dram_tensor("regin_h", [REGIN_ROWS, D], F32, kind="Internal").ap()
    regout_h = nc.dram_tensor("regout_h", [REGIN_ROWS + 1, D], F32, kind="Internal").ap()
    ccin_h = nc.dram_tensor("ccin_h", [E, 32], F32, kind="Internal")
    ccout_h = nc.dram_tensor("ccout_h", [E, 32], F32, kind="Internal")

    from contextlib import ExitStack
    ctx = ExitStack()
    sb = lambda name, shape, dt=F32: ctx.enter_context(nc.sbuf_tensor(name, shape, dt))

    arenaA = sb("arenaA", [128, 16384])
    arenaB = sb("arenaB", [128, 16384])
    d0g = sb("d0g", [128, 2 * G * 2048])
    junk = sb("junk", [128, 2048])
    msk = sb("msk", [128, 2048], mybir.dt.uint16)
    kfm_t = sb("kfm_t", [64, 2048])
    qfm_t = sb("qfm_t", [64, 2048])
    vfm = sb("vfm", [64, 2048])
    vtm = sb("vtm", [128, 16 * 65])
    wqkvh = sb("wqkvh", [128, DC * 192])
    ident = sb("ident", [128, 128])
    bq_h = sb("bq_h", [64, 4])
    tLO = sb("tLO", [128, G]); tHI = sb("tHI", [128, G])
    tCLO = sb("tCLO", [128, G]); tCHI = sb("tCHI", [128, G])
    tT = sb("tT", [128, G]); tC = sb("tC", [128, G])
    tTHIT = sb("tTHIT", [128, G]); tFND = sb("tFND", [128, G])
    tS1 = sb("tS1", [128, G]); tS2 = sb("tS2", [128, G]); tS3 = sb("tS3", [128, G])
    tSIG = sb("tSIG", [128, G]); tMU = sb("tMU", [128, G])
    tNEG = sb("tNEG", [128, G]); tP = sb("tP", [128, G])
    tSUMn = sb("tSUMn", [128, 2 * G]); tSQn = sb("tSQn", [128, 2 * G])
    u8g = sb("u8g", [128, G], U8)
    lnA = sb("lnA", [128, 1]); lnB = sb("lnB", [128, 1]); lnC = sb("lnC", [128, 1])
    neghuge = sb("neghuge", [128, 1])
    recz = sb("recz", [128, 1])
    c104n = sb("c104n", [128, 1])
    sepA = sb("sepA", [128, 2]); sepB = sb("sepB", [128, 2])

    itmp0 = sb("itmp0", [128, 1], I32); itmp1 = sb("itmp1", [128, 1], I32)
    aflag = vtm[:, 10 * NTC:11 * NTC]
    sidx0 = vtm[:, 12 * NTC:13 * NTC]; sidx1 = vtm[:, 13 * NTC:14 * NTC]
    gidx0 = vtm[:, 14 * NTC:15 * NTC]; gidx1 = vtm[:, 15 * NTC:16 * NTC]
    rt1 = vtm[:, 11 * NTC:11 * NTC + 8]; rt2 = vtm[:, 11 * NTC + 8:11 * NTC + 16]
    rm1 = sb("rm1", [128, 1]); rm2 = sb("rm2", [128, 1]); rz8 = sb("rz8", [128, 1])
    re2 = sb("re2", [128, 1])
    allin = vtm[0:8, 16 * NTC:16 * NTC + 32]; allout = vtm[0:8, 16 * NTC + 32:16 * NTC + 64]
    cnt16 = vtm[0:1, 16 * NTC + 64:16 * NTC + 96]
    offp = vtm[0:1, 16 * NTC + 96:16 * NTC + 128]; offg = vtm[0:1, 16 * NTC + 128:16 * NTC + 160]
    cA0 = vtm[0:8, 16 * NTC + 160 + 0:16 * NTC + 160 + 1]
    cB0 = vtm[0:8, 16 * NTC + 160 + 1:16 * NTC + 160 + 2]
    cA1 = vtm[0:8, 16 * NTC + 160 + 2:16 * NTC + 160 + 3]
    cB1 = vtm[0:8, 16 * NTC + 160 + 3:16 * NTC + 160 + 4]
    offA0 = vtm[0:8, 16 * NTC + 160 + 4:16 * NTC + 160 + 5]
    offB0 = vtm[0:8, 16 * NTC + 160 + 5:16 * NTC + 160 + 6]
    offA1 = vtm[0:8, 16 * NTC + 160 + 6:16 * NTC + 160 + 7]
    offB1 = vtm[0:8, 16 * NTC + 160 + 7:16 * NTC + 160 + 8]
    tot0c = vtm[0:8, 16 * NTC + 160 + 8:16 * NTC + 160 + 9]
    tmp8 = vtm[0:8, 16 * NTC + 160 + 9:16 * NTC + 160 + 10]
    tmp8b = vtm[0:8, 16 * NTC + 160 + 10:16 * NTC + 160 + 11]
    off0col = vtm[0:8, 16 * NTC + 160 + 11:16 * NTC + 160 + 12]
    off1col = vtm[0:8, 16 * NTC + 160 + 12:16 * NTC + 160 + 13]
    iota8bc = sb("iota8bc", [128, E])
    iota8col = sb("iota8col", [E, 1])
    iotaP = sb("iotaP", [128, 1])
    ones8 = sb("ones8", [E, 1])
    rowmask = sb("rowmask", [E, 1])
    maskprev = sb("maskprev", [E, 1])
    maskgrp = sb("maskgrp", [E, 1])

    psA = ctx.enter_context(nc.psum_tensor("psA", [128, 2048], F32))
    psB = ctx.enter_context(nc.psum_tensor("psB", [128, 2048], F32))

    s_ld = Sem(ctx.enter_context(nc.semaphore("s_ld")))
    s_gd = Sem(ctx.enter_context(nc.semaphore("s_gd")))
    s_te = Sem(ctx.enter_context(nc.semaphore("s_te")))
    s_ve = Sem(ctx.enter_context(nc.semaphore("s_ve")))
    s_se = Sem(ctx.enter_context(nc.semaphore("s_se")))
    s_cc = Sem(ctx.enter_context(nc.semaphore("s_cc")))

    r = Rec()
    SY, TE, VE, SE, GP = r.sync, r.tensor, r.vector, r.scalar, r.gpsimd
    VE.sep = (sepA[:, 0:1], sepB[:, 0:1])


    def xlnfm(dc_, lo, n):
        return arenaA[:, dc_ * 2048 + lo: dc_ * 2048 + lo + n]

    def attnout(tt, lo, n):
        return arenaB[:, tt * 1024 + lo: tt * 1024 + lo + n]

    logits = d0g[:, 6144:6144 + NTC * E]
    e1c = vtm[:, 0:NTC]; e2c = vtm[:, NTC:2 * NTC]
    g1c = vtm[:, 2 * NTC:3 * NTC]; g2c = vtm[:, 3 * NTC:4 * NTC]
    gs0 = vtm[:, 4 * NTC:5 * NTC]; gs1 = vtm[:, 5 * NTC:6 * NTC]
    r0c = vtm[:, 6 * NTC:7 * NTC]; r1c = vtm[:, 7 * NTC:8 * NTC]
    of0c = vtm[:, 8 * NTC:9 * NTC]; of1c = vtm[:, 9 * NTC:10 * NTC]
    ind0 = arenaA[0:8, 0:4096]; cum0 = arenaA[0:8, 4096:8192]
    ind1 = arenaA[0:8, 8192:12288]; cum1 = arenaA[0:8, 12288:16384]
    e1b = arenaA[96:104, 0:4096]; e2b = arenaA[96:104, 4096:8192]
    rowsA = arenaA[32:33, 0:4096]; rowsB = arenaA[32:33, 4096:8192]
    rowsC = arenaA[32:33, 8192:12288]; rowsD = arenaA[32:33, 12288:16384]
    zfill = arenaA[64:96, 0:1024]

    def hT0(fc, lo, n):
        return arenaA[:, fc * 384 + lo: fc * 384 + lo + n]

    HT1B = 32 * 384

    def hT1(fc, lo, n):
        return arenaA[:, HT1B + fc * 128 + lo: HT1B + fc * 128 + lo + n]

    def wfbuf(pp):
        return arenaB[:, pp * 1024: (pp + 1) * 1024]

    def wobuf(pp):
        return arenaB[:, 2048 + pp * 1024: 2048 + (pp + 1) * 1024]

    def gtile(i):
        return arenaB[:, 4096 + i * 1024: 4096 + (i + 1) * 1024]

    def outsb(i):
        return arenaB[:, 8192 + i * 1024: 8192 + (i + 1) * 1024]

    xfm2 = arenaB[:, 12288:13312]
    xtile_w = junk[:, 1024:2048]

    def gfm0(dc_, lo, n):
        return d0g[:, dc_ * 384 + lo: dc_ * 384 + lo + n]

    GFM1B = 8 * 384

    def gfm1(dc_, lo, n):
        return d0g[:, GFM1B + dc_ * 128 + lo: GFM1B + dc_ * 128 + lo + n]

    def xtile_ln(par):
        return d0g[:, par * 1024:(par + 1) * 1024]

    def d0(gi):
        return d0g[:, gi * 2048:(gi + 1) * 2048]

    SY.op("dma_start", out=ident[:], in_=ident_in, inc=[s_ld.inc(16)])
    SY.op("dma_start", out=iota8bc[:], in_=iota8bc_in, inc=[s_ld.inc(16)])
    SY.op("dma_start", out=iota8col[:], in_=iota8col_in, inc=[s_ld.inc(16)])
    SY.op("dma_start", out=iotaP[96:104, :], in_=iota8col_in, inc=[s_ld.inc(16)])
    SY.op("dma_start", out=ones8[:], in_=ones8_in, inc=[s_ld.inc(16)])
    SY.op("dma_start", out=rowmask[:], in_=rowmask_in, inc=[s_ld.inc(16)])
    SY.op("dma_start", out=maskprev[:], in_=maskprev_in, inc=[s_ld.inc(16)])
    SY.op("dma_start", out=maskgrp[:], in_=maskgrp_in, inc=[s_ld.inc(16)])
    VE.wait(s_ld)
    VE.op("memset", neghuge[:], -1e30)
    VE.op("memset", c104n[:], -104.0)
    VE.op("memset", sepB[:], 0.0)
    VE.op("memset", sepA[:], 0.0)
    VE.op("memset", vtm[:].rearrange("p (c w) -> p c w", w=65)[:, :, 64:65], 1.0,
          inc=[s_ve.inc()])

    def ln_tile(xt_view, out_view):
        VE.op("reduce_sum", lnA[:], xt_view, axis=AX)
        VE.op("tensor_scalar", out=lnA[:], in0=lnA[:], scalar1=1.0 / D, scalar2=None,
              op0=OP.mult)
        VE.op("tensor_scalar", out=lnB[:], in0=lnA[:], scalar1=-1.0, scalar2=None,
              op0=OP.mult, inc=[s_ve.inc()])
        SE.wait(s_ve)
        SE.op("activation", out=junk[:, 0:1024], in_=xt_view, func=ACT.Square,
              bias=lnB[:], scale=1.0, accum_out=lnC[:], inc=[s_se.inc()])
        VE.wait(s_se)
        VE.op("tensor_scalar", out=lnC[:], in0=lnC[:], scalar1=1.0 / D, scalar2=1e-5,
              op0=OP.mult, op1=OP.add, inc=[s_ve.inc()])
        SE.wait(s_ve)
        SE.op("activation", out=lnB[:], in_=lnC[:], func=ACT.Ln, bias=0.0, scale=1.0)
        SE.op("activation", out=lnC[:], in_=lnB[:], func=ACT.Exp, bias=0.0, scale=-0.5,
              inc=[s_se.inc()])
        VE.wait(s_se)
        VE.op("tensor_scalar", out=out_view, in0=xt_view, scalar1=lnA[:],
              scalar2=lnC[:], op0=OP.subtract, op1=OP.mult, inc=[s_ve.inc()])

    for b in range(BPC):
        tok0 = b * T
        for tt in range(NT):
            xt = xtile_ln(tt % 2)
            SY.wait(s_ve)
            SY.op("dma_start", out=xt, in_=x_in[tok0 + tt * 128: tok0 + (tt + 1) * 128, :],
                  inc=[s_ld.inc(16)])
            VE.wait(s_ld)
            ln_tile(xt, junk[:, 0:1024])
            TE.wait(s_ve)
            TE.wait(s_se)
            for dc_ in range(DC):
                TE.op("transpose", psB[:, dc_ * 128:(dc_ + 1) * 128],
                      junk[:, dc_ * 128:(dc_ + 1) * 128], ident[:],
                      inc=([s_te.inc()] if dc_ == DC - 1 else None))
            VE.wait(s_te)
            VE.op("tensor_copy",
                  out=arenaA[:].rearrange("p (c w) -> p c w", w=2048)[:, :, tt * 128:(tt + 1) * 128],
                  in_=psB[:, 0:1024].rearrange("p (c w) -> p c w", w=128),
                  inc=[s_ve.inc()])
            if DBG2 and b == 0 and tt == 0:
                GP.wait(s_ve)
                GP.op("dma_start", out=dbg_ln[:, 0:1024], in_=junk[:, 0:1024], inc=[s_gd.inc(16)])
                GP.op("dma_start", out=dbg_ln[:, 1024:1025], in_=lnA[:], allow_slow_non_contiguous=True, inc=[s_gd.inc(16)])
                GP.op("dma_start", out=dbg_ln[:, 1025:1026], in_=lnC[:], allow_slow_non_contiguous=True, inc=[s_gd.inc(16)])
                VE.wait(s_gd)
            TE.wait(s_ve)

        VE.wait(s_te)
        VE.wait(s_ld)
        VE.op("memset", vtm[:].rearrange("p (c w) -> p c w", w=65)[:, :, 64:65], 1.0,
              inc=[s_ve.inc()])
        for h in range(H):
            SY.wait(s_se)
            for j, base in enumerate((0, D, 2 * D)):
                SY.op("dma_start",
                      out=wqkvh[:].rearrange("p (c w) -> p c w", w=192)[:, :, j * 64:(j + 1) * 64],
                      in_=wqkvT[:, base + h * 64: base + (h + 1) * 64]
                      .rearrange("(c p) f -> p c f", p=128),
                      inc=[s_ld.inc(16)])
                SY.op("dma_start", out=bq_h[:, j:j + 1],
                      in_=bqkv[base + h * 64: base + (h + 1) * 64, :],
                      inc=[s_ld.inc(16)])
            TE.wait(s_ld)
            SE.wait(s_ld)
            SE.op("activation", out=bq_h[0:64, 3:4], in_=bq_h[0:64, 0:1],
                  func=ACT.Identity, bias=0.0, scale=1.0 / 32, inc=[s_se.inc()])
            for j, dest, bias_col, scl in ((0, qfm_t[0:64, :], 3, 1.0 / 32), (1, kfm_t[0:64, :], 1, 1.0),
                                           (2, vfm[0:64, :], 2, 1.0)):
                TE.wait(s_se)
                TE.wait(s_ve)
                for dc_ in range(DC):
                    for cc_ in range(4):
                        TE.op("matmul", psB[0:64, cc_ * 512:(cc_ + 1) * 512],
                              wqkvh[:, dc_ * 192 + j * 64: dc_ * 192 + (j + 1) * 64],
                              xlnfm(dc_, cc_ * 512, 512),
                              start=(dc_ == 0), stop=(dc_ == DC - 1),
                              inc=([s_te.inc()] if (dc_ == DC - 1 and cc_ == 3) else None))
                SE.wait(s_te)
                SE.op("activation", out=dest, in_=psB[0:64, :], func=ACT.Identity,
                      bias=bq_h[0:64, bias_col:bias_col + 1], scale=scl,
                      inc=[s_se.inc()])
            TE.wait(s_se)
            for jc in range(16):
                TE.op("transpose", psB[:, jc * 64:(jc + 1) * 64],
                      vfm[0:64, jc * 128:(jc + 1) * 128], ident[0:64, 0:64],
                      inc=([s_te.inc()] if jc == 15 else None))
            VE.wait(s_te)
            VE.op("tensor_copy",
                  out=vtm[:].rearrange("p (c w) -> p c w", w=65)[:, :, 0:64],
                  in_=psB[:, 0:1024].rearrange("p (c w) -> p c w", w=64),
                  inc=[s_ve.inc()])
            TE.wait(s_ve)

            NGR = NT // G

            def d0b(gg, gi):
                off = ((gg % 2) * G + gi) * 2048
                return d0g[:, off:off + 2048]

            def emit_scores(gg):
                sp = (gg % 2) * G
                flush_av()
                for gi in range(G):
                    tt = gg * G + gi
                    TE.wait(s_se)
                    TE.wait(s_ve)
                    for cc_ in range(4):
                        TE.op("matmul", psA[:, cc_ * 512:(cc_ + 1) * 512],
                              qfm_t[0:64, tt * 128:(tt + 1) * 128],
                              kfm_t[0:64, cc_ * 512:(cc_ + 1) * 512],
                              start=True, stop=True,
                              inc=([s_te.inc()] if cc_ == 3 else None))
                    SE.wait(s_te)
                    SE.wait(s_ve)
                    SE.op("activation", out=d0b(gg, gi), in_=psA[:, :], func=ACT.Identity,
                          bias=0.0, scale=1.0, inc=[s_se.inc()])
                    SE.op("activation", out=junk[:], in_=d0b(gg, gi), func=ACT.Square,
                          bias=0.0, scale=1.0, accum_out=tSQn[:, sp + gi:sp + gi + 1],
                          inc=[s_se.inc()])
                    VE.wait(s_se)
                    VE.op("reduce_sum", tSUMn[:, sp + gi:sp + gi + 1], d0b(gg, gi),
                          axis=AX, inc=[s_ve.inc()])

            def flush_av():
                if pending_av:
                    VE.wait(s_te)
                    for (avs_, tt_) in pending_av:
                        VE.op("reciprocal", recz[:], avs_[:, 64:65])
                        VE.op("tensor_copy", out=sepA[:, 0:1], in_=sepB[:, 0:1])
                        VE.op("tensor_scalar", out=attnout(tt_, h * 64, 64),
                              in0=avs_[:, 0:64], scalar1=recz[:], scalar2=None,
                              op0=OP.mult, inc=[s_ve.inc()])
                    del pending_av[:]

            def emit_search_epi(gg):
                sp = (gg % 2) * G

                def d0(gi):
                    return d0b(gg, gi)

                VE.op("tensor_copy", out=sepA[:, 0:1], in_=sepB[:, 0:1])
                VE.op("tensor_scalar", out=tMU[:], in0=tSUMn[:, sp:sp + G],
                      scalar1=1.0 / 2048, scalar2=None, op0=OP.mult)
                VE.op("tensor_scalar", out=tS2[:], in0=tSQn[:, sp:sp + G],
                      scalar1=1.0 / 2048, scalar2=None, op0=OP.mult)
                VE.op("tensor_tensor", out=tS1[:], in0=tMU[:], in1=tMU[:], op=OP.mult)
                VE.op("tensor_tensor", out=tS2[:], in0=tS2[:], in1=tS1[:], op=OP.subtract)
                VE.op("tensor_scalar", out=tS2[:], in0=tS2[:], scalar1=1e-12, scalar2=None,
                      op0=OP.max, inc=[s_ve.inc()])
                SE.wait(s_ve)
                SE.op("activation", out=tS1[:], in_=tS2[:], func=ACT.Ln, bias=0.0, scale=1.0)
                SE.op("activation", out=tSIG[:], in_=tS1[:], func=ACT.Exp, bias=0.0,
                      scale=0.5, inc=[s_se.inc()])
                VE.wait(s_se)
                VE.op("scalar_tensor_tensor", out=tLO[:], in0=tSIG[:], scalar=-2.0,
                      in1=tMU[:], op0=OP.mult, op1=OP.add)
                VE.op("scalar_tensor_tensor", out=tHI[:], in0=tSIG[:], scalar=3.0,
                      in1=tMU[:], op0=OP.mult, op1=OP.add)
                VE.op("scalar_tensor_tensor", out=tT[:], in0=tSIG[:], scalar=Z0,
                      in1=tMU[:], op0=OP.mult, op1=OP.add)
                VE.op("memset", tCLO[:], 2048.0)
                VE.op("memset", tCHI[:], 0.0)
                VE.op("memset", tTHIT[:], 0.0)
                VE.op("memset", tFND[:], 0.0)
                flush_av()

                def upd_eval():
                    VE.op("tensor_scalar", out=u8g[:], in0=tC[:], scalar1=TOPKF,
                          scalar2=None, op0=OP.is_ge)
                    VE.op("select", out=tLO[:], mask=u8g[:], on_true=tT[:], on_false=tLO[:])
                    VE.op("select", out=tCLO[:], mask=u8g[:], on_true=tC[:], on_false=tCLO[:])
                    VE.op("select", out=tHI[:], mask=u8g[:], on_true=tHI[:], on_false=tT[:])
                    VE.op("select", out=tCHI[:], mask=u8g[:], on_true=tCHI[:], on_false=tC[:])
                    VE.op("tensor_scalar", out=u8g[:], in0=tC[:], scalar1=TOPKF,
                          scalar2=None, op0=OP.is_equal)
                    VE.op("select", out=tTHIT[:], mask=u8g[:], on_true=tT[:], on_false=tTHIT[:])
                    VE.op("tensor_scalar", out=tS1[:], in0=tC[:], scalar1=TOPKF,
                          scalar2=None, op0=OP.is_equal)
                    VE.op("tensor_tensor", out=tFND[:], in0=tFND[:], in1=tS1[:], op=OP.max)

                def clip_t():
                    VE.op("tensor_tensor", out=tS1[:], in0=tHI[:], in1=tLO[:], op=OP.subtract)
                    VE.op("scalar_tensor_tensor", out=tS2[:], in0=tS1[:], scalar=0.002,
                          in1=tLO[:], op0=OP.mult, op1=OP.add)
                    VE.op("scalar_tensor_tensor", out=tS3[:], in0=tS1[:], scalar=-0.002,
                          in1=tHI[:], op0=OP.mult, op1=OP.add)
                    VE.op("tensor_tensor", out=tT[:], in0=tT[:], in1=tS2[:], op=OP.max)
                    VE.op("tensor_tensor", out=tT[:], in0=tT[:], in1=tS3[:], op=OP.min)
                    VE.op("tensor_scalar", out=u8g[:], in0=tFND[:], scalar1=0.5,
                          scalar2=None, op0=OP.is_ge)
                    VE.op("select", out=tT[:], mask=u8g[:], on_true=tLO[:], on_false=tT[:])

                def count_eval(use_scalar):
                    if use_scalar:
                        VE.op("tensor_scalar", out=tNEG[:], in0=tT[:], scalar1=-1.0,
                              scalar2=None, op0=OP.mult, inc=[s_ve.inc()])
                        SE.wait(s_ve)
                        for gi in range(G):
                            SE.op("activation", out=junk[:], in_=d0(gi), func=ACT.Sign,
                                  bias=tNEG[:, gi:gi + 1], scale=1.0,
                                  accum_out=tC[:, gi:gi + 1],
                                  inc=([s_se.inc()] if gi == G - 1 else None))
                        VE.wait(s_se)
                        VE.op("tensor_scalar", out=tC[:], in0=tC[:], scalar1=2048.0,
                              scalar2=0.5, op0=OP.add, op1=OP.mult)
                    else:
                        for gi in range(G):
                            VE.op("tensor_scalar", out=junk[:], in0=d0(gi),
                                  scalar1=tT[:, gi:gi + 1], scalar2=None,
                                  op0=OP.is_ge, op1=OP.add, accum_out=tC[:, gi:gi + 1])

                n_ev = 0
                count_eval(False); n_ev += 1
                upd_eval()
                for _ in range(N_NEWTON - 1):
                    VE.op("tensor_scalar", out=tS1[:], in0=tC[:], scalar1=TOPKF,
                          scalar2=None, op0=OP.subtract)
                    VE.op("tensor_tensor", out=tS1[:], in0=tS1[:], in1=tSIG[:], op=OP.mult)
                    VE.op("scalar_tensor_tensor", out=tT[:], in0=tS1[:], scalar=INV_DENS,
                          in1=tT[:], op0=OP.mult, op1=OP.add)
                    clip_t()
                    count_eval(False); n_ev += 1
                    upd_eval()
                for _ in range(N_FP):
                    VE.op("tensor_scalar", out=tS1[:], in0=tCLO[:], scalar1=TOPKF,
                          scalar2=None, op0=OP.subtract)
                    VE.op("tensor_tensor", out=tS2[:], in0=tCLO[:], in1=tCHI[:],
                          op=OP.subtract)
                    VE.op("tensor_scalar", out=tS2[:], in0=tS2[:], scalar1=1.0,
                          scalar2=None, op0=OP.max)
                    VE.op("reciprocal", tS2[:], tS2[:])
                    VE.op("tensor_tensor", out=tS1[:], in0=tS1[:], in1=tS2[:], op=OP.mult)
                    VE.op("tensor_tensor", out=tS2[:], in0=tHI[:], in1=tLO[:], op=OP.subtract)
                    VE.op("tensor_tensor", out=tS1[:], in0=tS1[:], in1=tS2[:], op=OP.mult)
                    VE.op("tensor_tensor", out=tT[:], in0=tLO[:], in1=tS1[:], op=OP.add)
                    clip_t()
                    count_eval(False); n_ev += 1
                    upd_eval()
                for _ in range(N_PROBE):
                    for gi in range(G):
                        VE.op("tensor_scalar", out=msk[:], in0=d0(gi),
                              scalar1=tHI[:, gi:gi + 1], scalar2=None, op0=OP.is_lt)
                        VE.op("select", out=junk[:], mask=msk[:], on_true=d0(gi),
                              on_false=neghuge[:].to_broadcast([128, 2048]))
                        VE.op("reduce_max", tP[:, gi:gi + 1], junk[:], axis=AX)
                    VE.op("tensor_copy", out=tT[:], in_=tP[:])
                    clip_t()
                    count_eval(False); n_ev += 1
                    upd_eval()
                VE.op("tensor_scalar", out=u8g[:], in0=tFND[:], scalar1=0.5,
                      scalar2=None, op0=OP.is_ge)
                VE.op("select", out=tT[:], mask=u8g[:], on_true=tTHIT[:], on_false=tLO[:])
                VE.op("tensor_scalar", out=tS2[:], in0=tT[:], scalar1=104.0,
                      scalar2=None, op0=OP.subtract, inc=[s_ve.inc()])

                for gi in range(G):
                    tt = gg * G + gi
                    VE.op("tensor_scalar", out=msk[:], in0=d0(gi), scalar1=tT[:, gi:gi + 1],
                          scalar2=None, op0=OP.is_ge)
                    VE.op("scalar_tensor_tensor", out=junk[:], in0=d0(gi),
                          scalar=tS2[:, gi:gi + 1], in1=msk[:],
                          op0=OP.subtract, op1=OP.mult, inc=[s_ve.inc()])
                    TE.wait(s_ve)
                    TE.wait(s_se)
                    for jc in range(16):
                        TE.op("transpose", psB[:, jc * 128:(jc + 1) * 128],
                              junk[:, jc * 128:(jc + 1) * 128], ident[:],
                              inc=([s_te.inc()] if jc == 15 else None))
                    SE.wait(s_te)
                    SE.op("activation", out=d0(gi), in_=psB[:, :], func=ACT.Exp,
                          bias=c104n[:], scale=1.0, inc=[s_se.inc()])
                    TE.wait(s_se)
                    avs = psA[:, 0:65] if gi == 0 else psA[:, 512:577]
                    for jc in range(16):
                        TE.op("matmul", avs, d0(gi)[:, jc * 128:(jc + 1) * 128],
                              vtm[:, jc * 65:(jc + 1) * 65],
                              start=(jc == 0), stop=(jc == 15),
                              inc=([s_te.inc()] if jc == 15 else None))
                    VE.wait(s_te)
                    VE.op("reciprocal", recz[:], avs[:, 64:65])
                    VE.op("tensor_copy", out=sepA[:, 0:1], in_=sepB[:, 0:1])
                    VE.op("tensor_scalar", out=attnout(tt, h * 64, 64), in0=avs[:, 0:64],
                          scalar1=recz[:], scalar2=None, op0=OP.mult, inc=[s_ve.inc()])

            pending_av = []
            for rr in range(NGR + 1):
                if rr < NGR:
                    emit_scores(rr)
                if rr >= 1:
                    emit_search_epi(rr - 1)
            flush_av()

        SY.wait(s_ve)
        SY.wait(s_se)
        SY.wait(s_te)
        SY.op("dma_start", out=arenaA[:, 0:8192].rearrange("p (c w) -> p c w", w=1024),
              in_=woutT.rearrange("(c p) o -> p c o", p=128), inc=[s_ld.inc(16)])
        SY.op("dma_start", out=vtm[:, 0:1024], in_=bout_bc, inc=[s_ld.inc(16)])
        for tt in range(NT):
            SY.wait(s_ve)
            SY.op("dma_start", out=xtile_w,
                  in_=x_in[tok0 + tt * 128: tok0 + (tt + 1) * 128, :], inc=[s_ld.inc(16)])
            TE.wait(s_ve)
            for dc_ in range(DC):
                TE.op("transpose", psB[:, dc_ * 128:(dc_ + 1) * 128],
                      attnout(tt, dc_ * 128, 128), ident[:],
                      inc=([s_te.inc()] if dc_ == DC - 1 else None))
            VE.wait(s_te)
            VE.op("tensor_copy", out=junk[:, 0:1024], in_=psB[:, 0:1024], inc=[s_ve.inc()])
            TE.wait(s_ve)
            TE.wait(s_ld)
            for dc_ in range(DC):
                for cc_ in range(2):
                    TE.op("matmul", psA[:, 1024 + cc_ * 512: 1024 + (cc_ + 1) * 512],
                          junk[:, dc_ * 128:(dc_ + 1) * 128],
                          arenaA[:, dc_ * 1024 + cc_ * 512: dc_ * 1024 + (cc_ + 1) * 512],
                          start=(dc_ == 0), stop=(dc_ == DC - 1),
                          inc=([s_te.inc()] if (dc_ == DC - 1 and cc_ == 1) else None))
            VE.wait(s_te)
            VE.wait(s_ld)
            VE.op("tensor_tensor", out=xtile_w, in0=xtile_w, in1=vtm[:, 0:1024], op=OP.add)
            VE.op("scalar_tensor_tensor", out=xtile_w, in0=psA[:, 1024:2048], scalar=0.0,
                  in1=xtile_w, op0=OP.add, op1=OP.add, inc=[s_ve.inc()])
            GP.wait(s_ve)
            GP.op("dma_start", out=xmid_h[tok0 + tt * 128: tok0 + (tt + 1) * 128, :],
                  in_=xtile_w, inc=[s_gd.inc(16)])
            TE.wait(s_ve)
            SY.wait(s_gd)

    VE.wait(s_gd)
    VE.op("memset", zfill[:, :], 0.0, inc=[s_ve.inc()])
    GP.wait(s_ve)
    nfill = (REGIN_ROWS + 31) // 32
    for i in range(nfill):
        lo = i * 32
        n = min(32, REGIN_ROWS - lo)
        GP.op("dma_start", out=regin_h[lo:lo + n, :], in_=zfill[0:n, :], inc=[s_gd.inc(16)])
    GP.op("dma_start", out=regout_h[ZROW:ZROW + 1, :], in_=zfill[0:1, :], inc=[s_gd.inc(16)])

    SY.op("dma_start", out=wqkvh[:].rearrange("p (c w) -> p c w", w=192)[:, :, 184:192],
          in_=wrouter.rearrange("(c p) e -> p c e", p=128), inc=[s_ld.inc(16)])
    for gt in range(NTC):
        xt = xtile_ln(gt % 2)
        SY.wait(s_ve)
        SY.wait(s_gd)
        SY.op("dma_start", out=xt, in_=xmid_h[gt * 128:(gt + 1) * 128, :], inc=[s_ld.inc(16)])
        VE.wait(s_ld)
        ln_tile(xt, junk[:, 0:1024])
        GP.wait(s_ve)
        GP.op("dma_start", out=xln2_h[gt * 128:(gt + 1) * 128, :], in_=junk[:, 0:1024],
              inc=[s_gd.inc(16)])
        TE.wait(s_ve)
        TE.wait(s_se)
        for dc_ in range(DC):
            TE.op("transpose", psB[:, dc_ * 128:(dc_ + 1) * 128],
                  junk[:, dc_ * 128:(dc_ + 1) * 128], ident[:],
                  inc=([s_te.inc()] if dc_ == DC - 1 else None))
        VE.wait(s_te)
        VE.wait(s_gd)
        VE.op("tensor_copy", out=xfm2[:], in_=psB[:, 0:1024], inc=[s_ve.inc()])
        TE.wait(s_ve)
        TE.wait(s_ld)
        for dc_ in range(DC):
            TE.op("matmul", psA[:, 0:8], xfm2[:, dc_ * 128:(dc_ + 1) * 128],
                  wqkvh[:, dc_ * 192 + 184: dc_ * 192 + 192],
                  start=(dc_ == 0), stop=(dc_ == DC - 1),
                  inc=([s_te.inc()] if dc_ == DC - 1 else None))
        VE.wait(s_te)
        VE.op("tensor_copy", out=logits[:, gt * 8:(gt + 1) * 8], in_=psA[:, 0:8],
              inc=[s_ve.inc()])
        TE.wait(s_ve)

    for gt in range(NTC):
        lg = logits[:, gt * 8:(gt + 1) * 8]
        VE.op("reduce_max", rm1[:], lg, axis=AX)
        VE.op("tensor_scalar", out=rt1[:], in0=lg, scalar1=rm1[:], scalar2=None,
              op0=OP.is_equal)
        VE.op("scalar_tensor_tensor", out=rt2[:], in0=rt1[:], scalar=-1e30, in1=lg,
              op0=OP.mult, op1=OP.add)
        VE.op("reduce_max", rm2[:], rt2[:], axis=AX)
        VE.op("tensor_tensor", out=rt1[:], in0=rt1[:], in1=iota8bc[:], op=OP.mult)
        VE.op("reduce_sum", e1c[:, gt:gt + 1], rt1[:], axis=AX)
        VE.op("tensor_scalar", out=rt1[:], in0=rt2[:], scalar1=rm2[:], scalar2=None,
              op0=OP.is_equal)
        VE.op("tensor_tensor", out=rt1[:], in0=rt1[:], in1=iota8bc[:], op=OP.mult)
        VE.op("reduce_sum", e2c[:, gt:gt + 1], rt1[:], axis=AX)
        VE.op("tensor_scalar", out=rm1[:], in0=rm1[:], scalar1=-1.0, scalar2=None,
              op0=OP.mult, inc=[s_ve.inc()])
        SE.wait(s_ve)
        SE.op("activation", out=rt2[:], in_=lg, func=ACT.Exp, bias=rm1[:], scale=1.0,
              accum_out=rz8[:])
        SE.op("activation", out=re2[:], in_=rm2[:], func=ACT.Exp, bias=rm1[:], scale=1.0,
              inc=[s_se.inc()])
        VE.wait(s_se)
        VE.op("reciprocal", g1c[:, gt:gt + 1], rz8[:])
        VE.op("tensor_tensor", out=g2c[:, gt:gt + 1], in0=re2[:], in1=g1c[:, gt:gt + 1],
              op=OP.mult, inc=[s_ve.inc()])

    for (col, row) in ((e1c, rowsA), (e2c, rowsB)):
        for half in range(2):
            TE.wait(s_ve)
            for i in range(16):
                gt = half * 16 + i
                TE.op("transpose", psB[0:1, i * 128:(i + 1) * 128], col[:, gt:gt + 1],
                      ident[:], inc=([s_te.inc()] if i == 15 else None))
            VE.wait(s_te)
            VE.op("tensor_copy", out=row[:, half * 2048:(half + 1) * 2048],
                  in_=psB[0:1, 0:2048], inc=[s_ve.inc()])
            TE.wait(s_ve)
    GP.wait(s_ve)
    for p in range(8):
        GP.op("dma_start", out=e1b[p:p + 1, :], in_=rowsA[:, :], inc=[s_gd.inc(16)])
        GP.op("dma_start", out=e2b[p:p + 1, :], in_=rowsB[:, :], inc=[s_gd.inc(16)])
    VE.wait(s_gd)
    VE.op("tensor_scalar", out=ind0[:, :], in0=e1b[:, :], scalar1=iotaP[96:104, :],
          scalar2=None, op0=OP.is_equal)
    VE.op("tensor_scalar", out=ind1[:, :], in0=e2b[:, :], scalar1=iotaP[96:104, :],
          scalar2=None, op0=OP.is_equal)
    VE.op("tensor_tensor_scan", out=cum0[:, :], data0=ind0[:, :], data1=ind0[:, :],
          initial=0.0, op0=OP.add, op1=OP.max)
    VE.op("tensor_tensor_scan", out=cum1[:, :], data0=ind1[:, :], data1=ind1[:, :],
          initial=0.0, op0=OP.add, op1=OP.max)
    VE.op("tensor_copy", out=cA0[:], in_=cum0[:, T - 1: T])
    VE.op("tensor_copy", out=cA1[:], in_=cum1[:, T - 1: T])
    VE.op("tensor_copy", out=cB0[:], in_=cum0[:, TPC - 1: TPC])
    VE.op("tensor_copy", out=cB1[:], in_=cum1[:, TPC - 1: TPC])
    VE.op("tensor_copy", out=sepA[:, 0:1], in_=sepB[:, 0:1])
    VE.op("tensor_tensor", out=cB0[:], in0=cB0[:], in1=cA0[:], op=OP.subtract)
    VE.op("tensor_tensor", out=cB1[:], in0=cB1[:], in1=cA1[:], op=OP.subtract,
          inc=[s_ve.inc()])
    TE.wait(s_ve)
    TE.op("transpose", psB[0:1, 0:8], cA0[:], ident[0:8, 0:8])
    TE.op("transpose", psB[0:1, 8:16], cB0[:], ident[0:8, 0:8])
    TE.op("transpose", psB[0:1, 16:24], cA1[:], ident[0:8, 0:8])
    TE.op("transpose", psB[0:1, 24:32], cB1[:], ident[0:8, 0:8],
          inc=[s_te.inc()])
    VE.wait(s_te)
    VE.op("tensor_copy", out=cnt16[:, :], in_=psB[0:1, 0:32], inc=[s_ve.inc()])
    TE.wait(s_ve)
    GP.wait(s_ve)
    for p in range(8):
        GP.op("dma_start", out=allin[p:p + 1, :], in_=cnt16[:, :], inc=[s_gd.inc(16)])
    VE.wait(s_gd)
    VE.op("tensor_scalar", out=allin[:], in0=allin[:], scalar1=rowmask[:], scalar2=None,
          op0=OP.mult, inc=[s_ve.inc()])
    GP.wait(s_ve)
    GP.op("dma_start", out=ccin_h.ap(), in_=allin[:], inc=[s_gd.inc(16)])
    GP.wait(s_gd)
    GP.op("collective_compute", "AllReduce", OP.add, replica_groups=[list(range(8))],
          ins=[ccin_h.ap().opt()], outs=[ccout_h.ap().opt()], inc=[s_cc.inc()])
    GP.wait(s_cc)
    GP.op("dma_start", out=allout[:], in_=ccout_h.ap(), inc=[s_gd.inc(16)])

    VE.op("tensor_tensor", out=cum0[:, :], in0=cum0[:, :], in1=ind0[:, :], op=OP.mult)
    VE.op("tensor_tensor", out=cum1[:, :], in0=cum1[:, :], in1=ind1[:, :], op=OP.mult,
          inc=[s_ve.inc()])
    TE.wait(s_ve)

    def row_reduce_and_transpose(src8, rowbuf, colbuf):
        for half in range(2):
            for cc_ in range(4):
                TE.op("matmul", psB[0:1, cc_ * 512:(cc_ + 1) * 512], ones8[:],
                      src8[:, half * 2048 + cc_ * 512: half * 2048 + (cc_ + 1) * 512],
                      start=True, stop=True,
                      inc=([s_te.inc()] if cc_ == 3 else None))
            VE.wait(s_te)
            VE.op("tensor_copy", out=rowbuf[:, half * 2048:(half + 1) * 2048],
                  in_=psB[0:1, 0:2048], inc=[s_ve.inc()])
            TE.wait(s_ve)
        for half in range(2):
            for i in range(16):
                TE.op("transpose", psB[:, i:i + 1],
                      rowbuf[:, (half * 16 + i) * 128:(half * 16 + i + 1) * 128],
                      ident[32:33, 32:33], inc=([s_te.inc()] if i == 15 else None))
            VE.wait(s_te)
            VE.op("tensor_copy", out=colbuf[:, half * 16:(half + 1) * 16], in_=psB[:, 0:16],
                  inc=[s_ve.inc()])
            TE.wait(s_ve)

    row_reduce_and_transpose(cum0, rowsC, r0c)
    row_reduce_and_transpose(cum1, rowsD, r1c)

    VE.wait(s_gd)
    VE.op("engine_nop", inc=[s_ve.inc()])
    TE.wait(s_ve)
    TE.op("matmul", psB[0:1, 0:32], maskprev[:], allout[:], start=True, stop=True)
    TE.op("matmul", psB[0:1, 32:64], maskgrp[:], allout[:], start=True, stop=True,
          inc=[s_te.inc()])
    VE.wait(s_te)
    VE.op("tensor_copy", out=offp[:, :], in_=psB[0:1, 0:32])
    VE.op("tensor_copy", out=offg[:, :], in_=psB[0:1, 32:64], inc=[s_ve.inc()])
    TE.wait(s_ve)
    pass
    for i4, rowv in enumerate((offp, offg)):
        for j4 in range(4):
            TE.op("transpose", psB[0:8, i4 * 4 + j4: i4 * 4 + j4 + 1],
                  rowv[0:1, j4 * 8:(j4 + 1) * 8], ident[0:1, 0:1],
                  inc=([s_te.inc()] if (i4 == 1 and j4 == 3) else None))
    VE.wait(s_te)
    VE.op("tensor_copy", out=offA0[:], in_=psB[0:8, 0:1])
    VE.op("tensor_copy", out=offB0[:], in_=psB[0:8, 1:2])
    VE.op("tensor_copy", out=offA1[:], in_=psB[0:8, 2:3])
    VE.op("tensor_copy", out=offB1[:], in_=psB[0:8, 3:4])
    VE.op("tensor_copy", out=tot0c[:], in_=psB[0:8, 4:5])
    VE.op("tensor_copy", out=tmp8[:], in_=psB[0:8, 6:7])
    VE.op("tensor_tensor", out=offB0[:], in0=offB0[:], in1=tot0c[:], op=OP.add)
    VE.op("tensor_tensor", out=offB0[:], in0=offB0[:], in1=cA0[:], op=OP.subtract)
    VE.op("tensor_copy", out=tmp8b[:], in_=psB[0:8, 5:6])
    VE.op("tensor_tensor", out=tot0c[:], in0=tot0c[:], in1=tmp8b[:], op=OP.add)
    VE.op("tensor_tensor", out=offA1[:], in0=offA1[:], in1=tot0c[:], op=OP.add)
    VE.op("tensor_tensor", out=offB1[:], in0=offB1[:], in1=tot0c[:], op=OP.add)
    VE.op("tensor_tensor", out=offB1[:], in0=offB1[:], in1=tmp8[:], op=OP.add)
    VE.op("tensor_tensor", out=offB1[:], in0=offB1[:], in1=cA1[:], op=OP.subtract,
          inc=[s_ve.inc()])
    for (ind, ocA, ocB, rowbuf, colbuf) in ((ind0, offA0, offB0, rowsA, of0c),
                                            (ind1, offA1, offB1, rowsB, of1c)):
        VE.op("tensor_scalar", out=ind[:, 0:T], in0=ind[:, 0:T], scalar1=ocA[:],
              scalar2=None, op0=OP.mult)
        VE.op("tensor_scalar", out=ind[:, T:TPC], in0=ind[:, T:TPC], scalar1=ocB[:],
              scalar2=None, op0=OP.mult, inc=[s_ve.inc()])
        TE.wait(s_ve)
        row_reduce_and_transpose(ind, rowbuf, colbuf)

    for (rcol, ocol, ecol, gcol, gscol, scol, gicol, cap, rstr, base) in (
            (r0c, of0c, e1c, g1c, gs0, sidx0, gidx0, CAP0, R0STR, 0),
            (r1c, of1c, e2c, g2c, gs1, sidx1, gidx1, CAP1, R1STR, K1BASE)):
        VE.op("tensor_tensor", out=ocol[:], in0=ocol[:], in1=rcol[:], op=OP.add)
        VE.op("tensor_scalar", out=ocol[:], in0=ocol[:], scalar1=1.0, scalar2=None,
              op0=OP.subtract)
        VE.op("tensor_scalar", out=gscol[:], in0=ocol[:], scalar1=float(CCAP),
              scalar2=None, op0=OP.is_lt)
        VE.op("tensor_scalar", out=ocol[:], in0=rcol[:], scalar1=float(cap) + 0.5,
              scalar2=None, op0=OP.is_lt)
        VE.op("tensor_tensor", out=gscol[:], in0=gscol[:], in1=ocol[:], op=OP.mult)
        VE.op("tensor_tensor", out=gscol[:], in0=gscol[:], in1=gcol[:], op=OP.mult)
        VE.op("tensor_scalar", out=ocol[:], in0=rcol[:], scalar1=1.0, scalar2=float(cap),
              op0=OP.subtract, op1=OP.min)
        VE.op("scalar_tensor_tensor", out=ocol[:], in0=ecol[:], scalar=float(rstr),
              in1=ocol[:], op0=OP.mult, op1=OP.add)
        VE.op("tensor_scalar", out=ocol[:], in0=ocol[:], scalar1=float(base),
              scalar2=None, op0=OP.add)
        VE.op("tensor_copy", out=scol[:], in_=ocol[:])
        VE.op("tensor_scalar", out=aflag[:], in0=gscol[:], scalar1=0.0, scalar2=None,
              op0=OP.is_gt)
        VE.op("tensor_tensor", out=ocol[:], in0=ocol[:], in1=aflag[:], op=OP.mult)
        VE.op("tensor_scalar", out=aflag[:], in0=aflag[:], scalar1=-float(ZROW),
              scalar2=float(ZROW), op0=OP.mult, op1=OP.add)
        VE.op("tensor_tensor", out=ocol[:], in0=ocol[:], in1=aflag[:], op=OP.add)
        VE.op("tensor_copy", out=gicol[:], in_=ocol[:], inc=[s_ve.inc()])

    GP.wait(s_ve)
    GP.wait(s_gd)
    for gt in range(NTC):
        VE.wait(s_gd)
        VE.op("tensor_copy", out=itmp0[:], in_=sidx0[:, gt:gt + 1])
        VE.op("tensor_copy", out=itmp1[:], in_=sidx1[:, gt:gt + 1], inc=[s_ve.inc()])
        GP.wait(s_ve)
        GP.op("dma_start", out=gtile(gt % 2)[:, :], in_=xln2_h[gt * 128:(gt + 1) * 128, :],
              inc=[s_gd.inc(16)])
        GP.wait(s_gd)
        GP.op("indirect_dma_start", out=regin_h,
              out_offset=IndirectOffsetOnAxis(itmp0[:], 0),
              in_=gtile(gt % 2)[:, :], in_offset=None, inc=[s_gd.inc(16)])
        GP.op("indirect_dma_start", out=regin_h,
              out_offset=IndirectOffsetOnAxis(itmp1[:], 0),
              in_=gtile(gt % 2)[:, :], in_offset=None, inc=[s_gd.inc(16)])
        GP.wait(s_gd)

    for e in range(E):
        SY.wait(s_ve)
        SY.wait(s_gd)
        for i in range(3):
            SY.op("dma_start", out=gtile(i)[:, :],
                  in_=regin_h[e * R0STR + i * 128: e * R0STR + (i + 1) * 128, :],
                  inc=[s_ld.inc(16)])
        SY.op("dma_start", out=gtile(3)[:, :],
              in_=regin_h[K1BASE + e * R1STR: K1BASE + e * R1STR + 128, :],
              inc=[s_ld.inc(16)])
        TE.wait(s_ld)
        TE.wait(s_se)
        TE.wait(s_ve)
        for i in range(4):
            for dc_ in range(DC):
                TE.op("transpose", psB[:, dc_ * 128:(dc_ + 1) * 128],
                      gtile(i)[:, dc_ * 128:(dc_ + 1) * 128], ident[:],
                      inc=([s_te.inc()] if dc_ == DC - 1 else None))
            VE.wait(s_te)
            if i < 3:
                VE.op("tensor_copy",
                      out=d0g[:, 0:3072].rearrange("p (c w) -> p c w", w=384)[:, :, i * 128:(i + 1) * 128],
                      in_=psB[:, 0:1024].rearrange("p (c w) -> p c w", w=128),
                      inc=[s_ve.inc()])
            else:
                VE.op("tensor_copy",
                      out=d0g[:, GFM1B:GFM1B + 1024].rearrange("p (c w) -> p c w", w=128),
                      in_=psB[:, 0:1024].rearrange("p (c w) -> p c w", w=128),
                      inc=[s_ve.inc()])
            TE.wait(s_ve)
        for fc in range(32):
            pp = fc % 2
            if fc >= 2:
                SY.wait(s_te)
            SY.op("dma_start", out=wfbuf(pp).rearrange("p (c w) -> p c w", w=128),
                  in_=wi_in[e * D:(e + 1) * D, fc * 128:(fc + 1) * 128]
                  .rearrange("(c p) f -> p c f", p=128), inc=[s_ld.inc(16)])
            TE.wait(s_ld)
            TE.wait(s_se)
            for dc_ in range(DC):
                TE.op("matmul", psA[:, 0:384], wfbuf(pp)[:, dc_ * 128:(dc_ + 1) * 128],
                      gfm0(dc_, 0, 384), start=(dc_ == 0), stop=(dc_ == DC - 1))
            for dc_ in range(DC):
                TE.op("matmul", psA[:, 512:640], wfbuf(pp)[:, dc_ * 128:(dc_ + 1) * 128],
                      gfm1(dc_, 0, 128), start=(dc_ == 0), stop=(dc_ == DC - 1),
                      inc=([s_te.inc()] if dc_ == DC - 1 else None))
            SE.wait(s_te)
            SE.op("activation", out=hT0(fc, 0, 384), in_=psA[:, 0:384], func=ACT.Relu,
                  bias=0.0, scale=1.0)
            SE.op("activation", out=hT1(fc, 0, 128), in_=psA[:, 512:640], func=ACT.Relu,
                  bias=0.0, scale=1.0, inc=[s_se.inc()])
        if DBG2 and e == 0:
            GP.wait(s_se)
            GP.op("dma_start", out=dbg_ht, in_=arenaA[:, :], inc=[s_gd.inc(16)])
            TE.wait(s_gd)
        accs = (psA[:, 0:1024], psA[:, 1024:2048], psB[:, 0:1024], psB[:, 1024:2048])
        for fc in range(32):
            pp = fc % 2
            if fc >= 2:
                SY.wait(s_te)
            SY.op("dma_start", out=wobuf(pp)[:, :],
                  in_=wo_in[e * FF + fc * 128: e * FF + (fc + 1) * 128, :],
                  inc=[s_ld.inc(16)])
            TE.wait(s_ld)
            if fc == 0:
                TE.wait(s_se)
                TE.wait(s_ve)
            for ts_ in range(4):
                src = hT0(fc, ts_ * 128, 128) if ts_ < 3 else hT1(fc, 0, 128)
                for cc_ in range(2):
                    TE.op("matmul", accs[ts_][:, cc_ * 512:(cc_ + 1) * 512],
                          src, wobuf(pp)[:, cc_ * 512:(cc_ + 1) * 512],
                          start=(fc == 0), stop=(fc == 31),
                          inc=([s_te.inc()] if (ts_ == 3 and cc_ == 1) else None))
        VE.wait(s_te)
        VE.wait(s_gd)
        for ts_ in range(4):
            VE.op("tensor_copy", out=outsb(ts_)[:, :], in_=accs[ts_],
                  inc=([s_ve.inc()] if ts_ == 3 else None))
        GP.wait(s_ve)
        for i in range(3):
            GP.op("dma_start",
                  out=regout_h[e * R0STR + i * 128: e * R0STR + (i + 1) * 128, :],
                  in_=outsb(i)[:, :], inc=[s_gd.inc(16)])
        GP.op("dma_start", out=regout_h[K1BASE + e * R1STR: K1BASE + e * R1STR + 128, :],
              in_=outsb(3)[:, :], inc=[s_gd.inc(16)])

    GP.wait(s_gd)
    for gt in range(NTC):
        SY.wait(s_ve)
        SY.wait(s_gd)
        SY.op("dma_start", out=gtile(2)[:, :], in_=xmid_h[gt * 128:(gt + 1) * 128, :],
              inc=[s_ld.inc(16)])
        VE.op("tensor_copy", out=itmp0[:], in_=gidx0[:, gt:gt + 1])
        VE.op("tensor_copy", out=itmp1[:], in_=gidx1[:, gt:gt + 1], inc=[s_ve.inc()])
        GP.wait(s_ve)
        GP.op("indirect_dma_start", out=gtile(0)[:, :], out_offset=None,
              in_=regout_h, in_offset=IndirectOffsetOnAxis(itmp0[:], 0),
              inc=[s_gd.inc(16)])
        GP.op("indirect_dma_start", out=gtile(1)[:, :], out_offset=None,
              in_=regout_h, in_offset=IndirectOffsetOnAxis(itmp1[:], 0),
              inc=[s_gd.inc(16)])
        VE.wait(s_ld)
        VE.wait(s_gd)
        VE.op("tensor_scalar", out=gtile(0)[:, :], in0=gtile(0)[:, :],
              scalar1=gs0[:, gt:gt + 1], scalar2=None, op0=OP.mult)
        VE.op("tensor_scalar", out=gtile(1)[:, :], in0=gtile(1)[:, :],
              scalar1=gs1[:, gt:gt + 1], scalar2=None, op0=OP.mult)
        VE.op("tensor_tensor", out=gtile(2)[:, :], in0=gtile(2)[:, :], in1=gtile(0)[:, :],
              op=OP.add)
        VE.op("tensor_tensor", out=gtile(2)[:, :], in0=gtile(2)[:, :], in1=gtile(1)[:, :],
              op=OP.add, inc=[s_ve.inc()])
        GP.wait(s_ve)
        GP.op("dma_start", out=y_out[gt * 128:(gt + 1) * 128, :], in_=gtile(2)[:, :],
              inc=[s_gd.inc(16)])
    if DBG:
        GP.op("dma_start", out=dbg_xmid, in_=xmid_h, inc=[s_gd.inc(16)])
        GP.op("dma_start", out=dbg_xln2, in_=xln2_h, inc=[s_gd.inc(16)])
        GP.op("dma_start", out=dbg_reg, in_=regin_h, inc=[s_gd.inc(16)])
        GP.op("dma_start", out=dbg_rout, in_=regout_h, inc=[s_gd.inc(16)])
        for k, tile_ in enumerate((e1c, e2c, g1c, g2c, r0c, r1c, of0c, of1c, gs0, gs1)):
            GP.op("dma_start", out=dbg_cols[:, k * NTC:(k + 1) * NTC], in_=tile_[:],
                  inc=[s_gd.inc(16)])
        GP.op("dma_start", out=dbg_all, in_=allout[:], inc=[s_gd.inc(16)])
    GP.wait(s_gd)

    with nc.Block() as block:
        @block.sync
        def _(eng):
            r.replay("sync", eng)

        @block.tensor
        def _(eng):
            r.replay("tensor", eng)

        @block.vector
        def _(eng):
            r.replay("vector", eng)

        @block.scalar
        def _(eng):
            r.replay("scalar", eng)

        @block.gpsimd
        def _(eng):
            r.replay("gpsimd", eng)

    ctx.close()
    return nc


_NC_CACHE = [None]


def make_in_maps(L0_attn_x, L0_attn_W_qkv, L0_attn_b_qkv, L0_attn_W_out, L0_attn_b_out,
                 L0_ffn_w_router, L0_ffn_wi, L0_ffn_wo):
    x = np.ascontiguousarray(np.asarray(L0_attn_x, dtype=np.float32))
    wqkvT = np.ascontiguousarray(np.asarray(L0_attn_W_qkv, dtype=np.float32).T)
    bqkv = np.ascontiguousarray(np.asarray(L0_attn_b_qkv, dtype=np.float32).reshape(3 * D, 1))
    woutT = np.ascontiguousarray(np.asarray(L0_attn_W_out, dtype=np.float32).T)
    bout_bc = np.ascontiguousarray(
        np.broadcast_to(np.asarray(L0_attn_b_out, dtype=np.float32)[None, :], (128, D)))
    wrouter = np.ascontiguousarray(np.asarray(L0_ffn_w_router, dtype=np.float32))
    wi = np.ascontiguousarray(np.asarray(L0_ffn_wi, dtype=np.float32).reshape(E * D, FF))
    wo = np.ascontiguousarray(np.asarray(L0_ffn_wo, dtype=np.float32).reshape(E * FF, D))
    ident = np.eye(128, dtype=np.float32)
    iota8bc = np.ascontiguousarray(
        np.broadcast_to(np.arange(E, dtype=np.float32)[None, :], (128, E)))
    iota8col = np.arange(E, dtype=np.float32).reshape(E, 1)
    ones8 = np.ones((E, 1), dtype=np.float32)

    in_maps = []
    for c in range(NCORES):
        rank = c % 4
        grp = c // 4
        rowmask = np.zeros((E, 1), np.float32)
        rowmask[c, 0] = 1.0
        maskprev = np.zeros((E, 1), np.float32)
        maskprev[grp * 4: grp * 4 + rank, 0] = 1.0
        maskgrp = np.zeros((E, 1), np.float32)
        maskgrp[grp * 4: grp * 4 + 4, 0] = 1.0
        in_maps.append({
            "x_in": np.ascontiguousarray(
                np.stack([x[8 * grp + rank], x[8 * grp + rank + 4]]).reshape(TPC, D)),
            "wqkvT": wqkvT, "bqkv": bqkv, "woutT": woutT, "bout_bc": bout_bc,
            "wrouter": wrouter, "wi_in": wi, "wo_in": wo,
            "ident_in": ident, "iota8bc_in": iota8bc, "iota8col_in": iota8col,
            "ones8_in": ones8, "rowmask_in": rowmask, "maskprev_in": maskprev,
            "maskgrp_in": maskgrp,
        })
    return in_maps


def kernel(**inputs):
    in_maps = make_in_maps(**inputs)
    if _NC_CACHE[0] is None:
        _NC_CACHE[0] = build_nc()
    res = run_bass_kernel_spmd(_NC_CACHE[0], in_maps, core_ids=list(range(NCORES)))
    out = np.empty((B, T, D), np.float32)
    for c in range(NCORES):
        grp, rank = c // 4, c % 4
        yo = res.results[c]["y_out"].reshape(2, T, D)
        out[8 * grp + rank] = yo[0]
        out[8 * grp + rank + 4] = yo[1]
    return out
